# version 06
# baseline (speedup 1.0000x reference)
import numpy as np
import ml_dtypes
import concourse.bass as bass
import concourse.mybir as mybir
from concourse.bass_utils import run_bass_kernel_spmd

F32 = mybir.dt.float32
BF16 = mybir.dt.bfloat16
I8 = mybir.dt.int8
AF = mybir.ActivationFunctionType
ALU = mybir.AluOpType
AX = mybir.AxisListType

HID = 2048
HPC = 4
INTER = 8192
CCH = 128
MID = 64
SCALE = 128.0 ** -0.5


ALL_SEMS = []


class Sem:
    def __init__(self, nc, name):
        self.h = nc.alloc_semaphore(name=name)
        self.n = 0
        ALL_SEMS.append(self)

    def inc(self, inst, k=1):
        inst.then_inc(self.h, k)
        self.n += k
        return inst

    def wait(self, eng, val=None):
        eng.wait_ge(self.h, self.n if val is None else val)


class Ring:
    def __init__(self, nc, name, slots):
        self.s = [Sem(nc, f"{name}{i}") for i in range(slots)]
        self.k = slots

    def sem(self, i):
        return self.s[i % self.k]


def build_program(T=16384, taps=False):
    ALL_SEMS.clear()
    S = T // 4
    NT1 = S // 128
    NT5Q = S // 512
    NT5B = T // 512
    NCH = T // CCH
    NIB = INTER // 128

    nc = bass.Bass("TRN2", target_bir_lowering=False)
    ap = lambda t: t.ap()
    x_q = ap(nc.dram_tensor("x_q", [S, HID], F32, kind="ExternalInput"))
    wproj = [ap(nc.dram_tensor(n, [128, 16, 512], F32, kind="ExternalInput")) for n in ("wq", "wk", "wv", "ww")]
    cw = ap(nc.dram_tensor("cw", [128, 4, 4, 4], F32, kind="ExternalInput"))
    wf1 = ap(nc.dram_tensor("wf1", [128, 16, 128], F32, kind="ExternalInput"))
    wf2 = ap(nc.dram_tensor("wf2", [128, 512], F32, kind="ExternalInput"))
    wbm = ap(nc.dram_tensor("wbm", [128, 16, 4], F32, kind="ExternalInput"))
    wg1 = ap(nc.dram_tensor("wg1", [128, 16, 128], F32, kind="ExternalInput"))
    wg2 = ap(nc.dram_tensor("wg2", [128, 512], F32, kind="ExternalInput"))
    bg2 = ap(nc.dram_tensor("bg2", [128, 4], F32, kind="ExternalInput"))
    normw = ap(nc.dram_tensor("normw", [128, 1], F32, kind="ExternalInput"))
    wo = ap(nc.dram_tensor("wo", [128, 16, HID], F32, kind="ExternalInput"))
    wgt = ap(nc.dram_tensor("wgt", [NIB, 128, 16, 128], I8, kind="ExternalInput"))
    wup = ap(nc.dram_tensor("wup", [NIB, 128, 16, 128], I8, kind="ExternalInput"))
    wdn = ap(nc.dram_tensor("wdn", [4, 128, 64, 512], I8, kind="ExternalInput"))
    sgt = ap(nc.dram_tensor("sgt", [128, NIB], F32, kind="ExternalInput"))
    sup = ap(nc.dram_tensor("sup", [128, NIB], F32, kind="ExternalInput"))
    sdn = ap(nc.dram_tensor("sdn", [1, HID], F32, kind="ExternalInput"))
    qoff = ap(nc.dram_tensor("qoff", [1, 1], mybir.dt.uint32, kind="ExternalInput"))
    out_q = ap(nc.dram_tensor("out_q", [S, HID], F32, kind="ExternalOutput"))

    AGR = HID + 256
    agin = ap(nc.dram_tensor("agin", [AGR, S], BF16, kind="Internal"))
    agout = ap(nc.dram_tensor("agout", [4 * AGR, S], BF16, kind="Internal"))
    fuse_scr = ap(nc.dram_tensor("fuse_scr", [INTER, S], BF16, kind="Internal"))
    pr_scr = [ap(nc.dram_tensor(f"pr{i}_scr", [HPC, 128, T], BF16, kind="Internal")) for i in range(4)]
    zg_scr = ap(nc.dram_tensor("zg_scr", [T, 512], F32, kind="Internal"))
    sg_scr = ap(nc.dram_tensor("sg_scr", [HPC, 128, T], BF16, kind="Internal"))
    agin2 = ap(nc.dram_tensor("agin2", [HID, S], BF16, kind="Internal"))
    agout2 = ap(nc.dram_tensor("agout2", [4 * HID, S], BF16, kind="Internal"))
    osel = ap(nc.dram_tensor("osel", [HID, S], BF16, kind="Internal"))
    ffn_scr = ap(nc.dram_tensor("ffn_scr", [S, HID], F32, kind="Internal"))
    if taps:
        tp_xnT = ap(nc.dram_tensor("tp_xnT", [4 * AGR, S], BF16, kind="ExternalOutput"))
        tp_fuse = ap(nc.dram_tensor("tp_fuse", [INTER, S], BF16, kind="ExternalOutput"))
        tp_pr = [ap(nc.dram_tensor(f"tp_pr{i}", [HPC, 128, T], BF16, kind="ExternalOutput")) for i in range(4)]
        tp_zg = ap(nc.dram_tensor("tp_zg", [T, 512], F32, kind="ExternalOutput"))
        tp_sg = ap(nc.dram_tensor("tp_sg", [HPC, 128, T], BF16, kind="ExternalOutput"))
        tp_o = ap(nc.dram_tensor("tp_o", [HID, S], BF16, kind="ExternalOutput"))
        tp_ffn = ap(nc.dram_tensor("tp_ffn", [S, HID], F32, kind="ExternalOutput"))

    rg = [[0, 1, 2, 3], [4, 5, 6, 7]]

    from contextlib import ExitStack
    ctx = ExitStack()
    cur_stack = [ctx]
    sb = lambda name, shape, dt: cur_stack[0].enter_context(nc.sbuf_tensor(name, shape, dt))
    ps = lambda name, shape, dt=F32: cur_stack[0].enter_context(nc.psum_tensor(name, shape, dt))

    s_identb = sb("s_identb", [128, 128], BF16)
    s_identf = sb("s_identf", [128, 128], F32)
    s_mcx = sb("s_mcx", [128, 132], F32)
    s_ones1f = sb("s_ones1f", [1, 128], F32)
    s_ones1b = sb("s_ones1b", [1, 128], BF16)
    s_onescf = sb("s_onescf", [128, 1], F32)
    s_onescb = sb("s_onescb", [128, 1], BF16)
    s_sel4 = sb("s_sel4", [4, 512], F32)
    s_cw = sb("s_cw", [128, 64], F32)
    s_sgt = sb("s_sgt", [128, NIB], F32)
    s_sup = sb("s_sup", [128, NIB], F32)
    s_sdnr = sb("s_sdnr", [1, HID], F32)
    s_normw = sb("s_normw", [128, 1], F32)
    s_bg2h = sb("s_bg2h", [128, 4], F32)
    CVALS = {"ln_eps": 1e-5, "l2_eps": 1e-6, "rms_eps": 1e-5, "one": 1.0,
             "neg_shift": -15.0, "clamp_l": 80.0, "neg_clamp_l": -80.0}
    s_const = sb("s_const", [128, len(CVALS)], F32)
    CIDX = {k: i for i, k in enumerate(CVALS)}
    ccol = lambda k: s_const[:, CIDX[k]:CIDX[k] + 1]

    tE = Sem(nc, "tE")
    vE = Sem(nc, "vE")
    sE = Sem(nc, "sE")
    gE = Sem(nc, "gE")
    cS = Sem(nc, "cS")
    rI = Ring(nc, "rI", 1)
    rWf = Ring(nc, "rWf", 1)
    rX = Ring(nc, "rX", 2)
    rAg = Ring(nc, "rAg", 1)
    rA2w = Ring(nc, "rA2w", 2)
    rFu = Ring(nc, "rFu", 2)
    rB1w = Ring(nc, "rB1w", 1)
    rBx = Ring(nc, "rBx", 2)
    rBst = Ring(nc, "rBst", 2)
    rOut = Ring(nc, "rOut", 1)

    def vop(vector, fn, *a, **k):
        vector.wait_ge(vE.h, vE.n)
        return vE.inc(fn(*a, **k))

    def sop(scalar, fn, *a, **k):
        scalar.wait_ge(sE.h, sE.n)
        return sE.inc(fn(*a, **k))

    def gop(gpsimd, fn, *a, **k):
        gpsimd.wait_ge(gE.h, gE.n)
        return gE.inc(fn(*a, **k))

    blk = ctx.enter_context(nc.Block())

    def full_barrier():
        for mk in ("tensor", "vector", "scalar", "gpsimd", "sync"):
            @getattr(blk, mk)
            def _(eng):
                for s in ALL_SEMS:
                    if s.n:
                        eng.wait_ge(s.h, s.n)

    def open_phase():
        st = ExitStack()
        cur_stack[0] = st
        return st

    scope_stack = []
    def scope(name):
        if scope_stack:
            nc.leave_named_scope(scope_stack[-1][0], scope_stack[-1][1], False)
            scope_stack.pop()
        sid, _ = nc.enter_named_scope(name, False)
        scope_stack.append((name, sid))

    scope("pA")
    @blk.sync
    def _(sync):
        for d, s_ in [(cw.rearrange("c p h t -> c (p h t)"), s_cw), (sgt, s_sgt), (sup, s_sup),
                      (sdn, s_sdnr), (normw, s_normw), (bg2, s_bg2h)]:
            rI.sem(0).inc(sync.dma_start(out=s_[:], in_=d), 16)

    @blk.gpsimd
    def _(gpsimd):
        gop(gpsimd, gpsimd.memset, s_identb[:], 1.0)
        gop(gpsimd, gpsimd.affine_select, out=s_identb[:], in_=s_identb[:], pattern=[[-1, 128]],
            compare_op=ALU.is_ge, fill=0.0, base=0, channel_multiplier=1)
        gop(gpsimd, gpsimd.affine_select, out=s_identb[:], in_=s_identb[:], pattern=[[1, 128]],
            compare_op=ALU.is_ge, fill=0.0, base=0, channel_multiplier=-1)
        gop(gpsimd, gpsimd.memset, s_identf[:], 1.0)
        gop(gpsimd, gpsimd.affine_select, out=s_identf[:], in_=s_identf[:], pattern=[[-1, 128]],
            compare_op=ALU.is_ge, fill=0.0, base=0, channel_multiplier=1)
        gop(gpsimd, gpsimd.affine_select, out=s_identf[:], in_=s_identf[:], pattern=[[1, 128]],
            compare_op=ALU.is_ge, fill=0.0, base=0, channel_multiplier=-1)
        gop(gpsimd, gpsimd.memset, s_mcx[:, 0:128], 1.0)
        gop(gpsimd, gpsimd.affine_select, out=s_mcx[:, 0:128], in_=s_mcx[:, 0:128], pattern=[[1, 128]],
            compare_op=ALU.is_ge, fill=0.0, base=0, channel_multiplier=-1)
        gop(gpsimd, gpsimd.memset, s_mcx[0:MID, 128:129], 1.0)
        gop(gpsimd, gpsimd.memset, s_mcx[MID:128, 128:129], 0.0)
        gop(gpsimd, gpsimd.memset, s_mcx[0:128, 129:130], 0.0)
        gop(gpsimd, gpsimd.memset, s_mcx[64:128, 129:130], 0.0)
        gop(gpsimd, gpsimd.memset, s_ones1f[:], 1.0)
        gop(gpsimd, gpsimd.memset, s_ones1b[:], 1.0)
        gop(gpsimd, gpsimd.memset, s_onescf[:], 1.0)
        gop(gpsimd, gpsimd.memset, s_onescb[:], 1.0)
        for k, v in CVALS.items():
            gop(gpsimd, gpsimd.memset, ccol(k), v)
        gop(gpsimd, gpsimd.memset, s_sel4[:], 1.0)
        sel3 = s_sel4.rearrange("k (h j) -> k h j", j=128)
        gop(gpsimd, gpsimd.affine_select, out=sel3, in_=sel3, pattern=[[1, 4], [0, 128]],
            compare_op=ALU.is_ge, fill=0.0, base=0, channel_multiplier=-1)
        gop(gpsimd, gpsimd.affine_select, out=sel3, in_=sel3, pattern=[[-1, 4], [0, 128]],
            compare_op=ALU.is_ge, fill=0.0, base=0, channel_multiplier=1)

    @blk.vector
    def _(vector):
        rI.sem(0).wait(vector)
        gE.wait(vector)
        vop(vector, vector.tensor_scalar, out=s_mcx[:, 0:128], in0=s_mcx[:, 0:128],
            scalar1=s_mcx[:, 128:129], scalar2=None, op0=ALU.subtract)
        vop(vector, vector.tensor_scalar_mul, s_bg2h[:], s_bg2h[:], 0.5)
    init_v_mark = vE.n

    stA = open_phase()
    s_x = [sb(f"s_x{i}", [128, HID], F32) for i in range(2)]
    s_xnt_all = sb("s_xnt_all", [128, 16, S], BF16)
    s_f1t = sb("s_f1t", [128, S], BF16)
    s_g1t = sb("s_g1t", [128, S], BF16)
    s_xn = sb("s_xn", [128, HID], BF16)
    s_mcol = sb("s_mcol", [128, 1], F32)
    s_vcol = sb("s_vcol", [128, 1], F32)
    s_rstd = sb("s_rstd", [128, 1], F32)
    s_wf1 = sb("s_wf1", [128, 16, 128], BF16)
    s_wg1 = sb("s_wg1", [128, 16, 128], BF16)
    p_tr = ps("p_tr", [128, 4, 128], BF16)
    p_f1 = ps("p_f1", [128, 128])
    p_g1 = ps("p_g1", [128, 128])

    @blk.gpsimd
    def _(gpsimd):
        rWf.sem(0).inc(gpsimd.dma_start(out=s_wf1[:, :, :], in_=wf1), 16)
        rWf.sem(0).inc(gpsimd.dma_start(out=s_wg1[:, :, :], in_=wg1), 16)

    apply_marks = []
    tr_marks = []
    for it in range(NT1):
        buf = s_x[it % 2]
        xnb = s_xn
        @blk.sync
        def _(sync):
            if it >= 2:
                sync.wait_ge(vE.h, apply_marks[it - 2])
            rX.sem(it).inc(sync.dma_start(out=buf[:], in_=x_q[it * 128:(it + 1) * 128, :]), 16)

        @blk.scalar
        def _(scalar):
            rX.sem(it).wait(scalar)
            if it >= 1:
                scalar.wait_ge(tE.h, tr_marks[it - 1])
            sop(scalar, scalar.activation, out=s_xn[:], in_=buf[:], func=AF.Square,
                accum_out=s_vcol[:])

        @blk.vector
        def _(vector):
            rX.sem(it).wait(vector)
            vop(vector, vector.reduce_sum, s_mcol[:], buf[:], axis=AX.X)
            vop(vector, vector.tensor_scalar_mul, s_mcol[:], s_mcol[:], 1.0 / HID)
            sE.wait(vector)
            vop(vector, vector.tensor_scalar_mul, s_vcol[:], s_vcol[:], 1.0 / HID)
            vop(vector, vector.scalar_tensor_tensor, out=s_vcol[:], in0=s_mcol[:],
                scalar=s_mcol[:, 0:1], in1=s_vcol[:], op0=ALU.mult, op1=ALU.subtract)

        @blk.scalar
        def _(scalar):
            vE.wait(scalar)
            sop(scalar, scalar.activation, out=s_rstd[:], in_=s_vcol[:], func=AF.Ln,
                scale=-1.0, bias=ccol("ln_eps"))
            sop(scalar, scalar.activation, out=s_rstd[:], in_=s_rstd[:], func=AF.Exp, scale=-0.5)

        @blk.vector
        def _(vector):
            sE.wait(vector)
            if it >= 1:
                vector.wait_ge(tE.h, tr_marks[it - 1])
            vop(vector, vector.tensor_scalar, out=xnb[:], in0=buf[:], scalar1=s_mcol[:, 0:1],
                scalar2=s_rstd[:, 0:1], op0=ALU.subtract, op1=ALU.mult)
        apply_marks.append(vE.n)

        for grp in range(4):
            @blk.tensor
            def _(tensor, grp=grp):
                tensor.wait_ge(vE.h, apply_marks[-1])
                if tE.n > 0:
                    tensor.wait_ge(sE.h, sE.n)
                for j in range(4):
                    dc = grp * 4 + j
                    tE.inc(tensor.transpose(p_tr[:, j, :], xnb[:, dc * 128:(dc + 1) * 128], s_identb[:]))
            tE.n += 0
            @blk.scalar
            def _(scalar, grp=grp):
                tE.wait(scalar)
                sop(scalar, scalar.copy,
                    out=s_xnt_all[:, grp * 4:(grp + 1) * 4, it * 128:(it + 1) * 128], in_=p_tr[:])
        tr_marks.append(tE.n)

        @blk.tensor
        def _(tensor):
            tensor.wait_ge(sE.h, sE.n)
            rWf.sem(0).wait(tensor)
            for dc in range(16):
                xt = s_xnt_all[:, dc, it * 128:(it + 1) * 128]
                mm = tensor.matmul(p_f1[:], s_wf1[:, dc, :], xt, start=(dc == 0), stop=(dc == 15))
            tE.inc(mm)
            for dc in range(16):
                xt = s_xnt_all[:, dc, it * 128:(it + 1) * 128]
                mm = tensor.matmul(p_g1[:], s_wg1[:, dc, :], xt, start=(dc == 0), stop=(dc == 15))
            tE.inc(mm)
        @blk.scalar
        def _(scalar):
            tE.wait(scalar)
            sop(scalar, scalar.copy, out=s_f1t[:, it * 128:(it + 1) * 128], in_=p_f1[:])
            sop(scalar, scalar.copy, out=s_g1t[:, it * 128:(it + 1) * 128], in_=p_g1[:])

    @blk.sync
    def _(sync):
        sync.wait_ge(sE.h, sE.n)
        for dc in range(16):
            rAg.sem(0).inc(sync.dma_start(out=agin[dc * 128:(dc + 1) * 128, :], in_=s_xnt_all[:, dc, :]), 16)
        rAg.sem(0).inc(sync.dma_start(out=agin[HID:HID + 128, :], in_=s_f1t[:]), 16)
        rAg.sem(0).inc(sync.dma_start(out=agin[HID + 128:HID + 256, :], in_=s_g1t[:]), 16)

    @blk.gpsimd
    def _(gpsimd):
        rAg.sem(0).wait(gpsimd)
        gpsimd.collective_compute("AllGather", ALU.bypass, replica_groups=rg,
                                  ins=[agin.opt()], outs=[agout.opt()]).then_inc(cS.h, 1)
    cS.n += 1

    scope("pA2")
    s_wgi = [sb(f"s_wgi{i}", [128, 16, 128], BF16) for i in range(2)]
    s_wui = [sb(f"s_wui{i}", [128, 16, 128], BF16) for i in range(2)]
    s_gtb = [sb(f"s_gtb{i}", [128, 512], BF16) for i in range(2)]
    s_upb = [sb(f"s_upb{i}", [128, 512], BF16) for i in range(2)]
    s_fu = [sb(f"s_fu{i}", [128, 512], BF16) for i in range(2)]
    p_gt = [ps(f"p_gt{i}", [128, 512]) for i in range(2)]
    p_up = [ps(f"p_up{i}", [128, 512]) for i in range(2)]

    fuse_dma_marks = []
    a2_mm_marks = []
    a2_evac_marks = []
    a2_tt_marks = []
    for ib in range(NIB):
        wgb, wub = s_wgi[ib % 2], s_wui[ib % 2]
        @blk.gpsimd
        def _(gpsimd):
            if ib >= 2:
                gpsimd.wait_ge(tE.h, a2_mm_marks[ib - 2])
            rA2w.sem(ib).inc(gpsimd.dma_start(out=wgb[:, :, :], in_=wgt[ib]), 16)
            rA2w.sem(ib).inc(gpsimd.dma_start(out=wub[:, :, :], in_=wup[ib]), 16)
        fub = s_fu[ib % 2]
        for t5 in range(NT5Q):
            u = (ib * NT5Q + t5)
            pg, pu = p_gt[u % 2], p_up[u % 2]
            @blk.tensor
            def _(tensor, t5=t5, pg=pg, pu=pu):
                rA2w.sem(ib).wait(tensor)
                if u >= 2:
                    tensor.wait_ge(sE.h, a2_evac_marks[u - 2])
                for dc in range(16):
                    xt = s_xnt_all[:, dc, t5 * 512:(t5 + 1) * 512]
                    mm = tensor.matmul(pg[:], wgb[:, dc, :], xt, start=(dc == 0), stop=(dc == 15))
                tE.inc(mm)
                for dc in range(16):
                    xt = s_xnt_all[:, dc, t5 * 512:(t5 + 1) * 512]
                    mm = tensor.matmul(pu[:], wub[:, dc, :], xt, start=(dc == 0), stop=(dc == 15))
                tE.inc(mm)
            @blk.scalar
            def _(scalar, t5=t5, pg=pg, pu=pu):
                tE.wait(scalar)
                if u >= 2:
                    scalar.wait_ge(vE.h, a2_tt_marks[u - 2])
                sop(scalar, scalar.activation, out=s_gtb[u % 2][:], in_=pg[:], func=AF.Silu,
                    scale=s_sgt[:, ib:ib + 1])
                sop(scalar, scalar.activation, out=s_upb[u % 2][:], in_=pu[:], func=AF.Copy,
                    scale=s_sup[:, ib:ib + 1])
                a2_evac_marks.append(sE.n)
            fub = s_fu[t5 % 2]
            @blk.vector
            def _(vector, t5=t5, fub=fub):
                sE.wait(vector)
                if ib > 0 or t5 >= 2:
                    rFu.sem(t5).wait(vector)
                vop(vector, vector.tensor_tensor, out=fub[:],
                    in0=s_gtb[u % 2][:], in1=s_upb[u % 2][:], op=ALU.mult)
                a2_tt_marks.append(vE.n)
            @blk.sync
            def _(sync, t5=t5, fub=fub):
                sync.wait_ge(vE.h, vE.n)
                rFu.sem(t5).inc(sync.dma_start(
                    out=fuse_scr[ib * 128:(ib + 1) * 128, t5 * 512:(t5 + 1) * 512], in_=fub[:]), 16)
        a2_mm_marks.append(tE.n)

    full_barrier()
    stA.close()

    scope("pB1")
    stB1 = open_phase()
    s_wp = [sb(f"s_wp{p}", [128, 16, 512], BF16) for p in range(4)]
    s_wf2 = sb("s_wf2", [128, 512], BF16)
    s_wg2 = sb("s_wg2", [128, 512], BF16)
    s_wbs = sb("s_wbs", [128, 16, 4], BF16)
    s_xt = [sb(f"s_xt{i}", [128, 16, 512], BF16) for i in range(2)]
    s_ft = [sb(f"s_ft{i}", [128, 512], BF16) for i in range(2)]
    s_gt1 = [sb(f"s_gt1{i}", [128, 512], BF16) for i in range(2)]
    s_xh = [sb(f"s_xh{p}", [128, 4, 515], BF16) for p in range(4)]
    s_yc = [sb(f"s_yc{i}", [128, 4, 512], BF16) for i in range(2)]
    s_out4 = [[sb(f"s_out4{p}_{i}", [128, 4, 512], BF16) for i in range(2)] for p in range(4)]
    s_wsq = sb("s_wsq", [128, 4, 512], BF16)
    s_beta = [sb(f"s_beta{i}", [4, 512], F32) for i in range(2)]
    s_rsr = sb("s_rsr", [1, 2048], BF16)
    s_rsb = sb("s_rsb", [128, 512], F32)
    s_wbt = sb("s_wbt", [128, 4, 512], BF16)
    s_zgt = sb("s_zgt", [128, 4, 512], F32)
    s_sgt2 = sb("s_sgt2", [128, 4, 512], BF16)
    p_pr = [ps(f"p_pr{i}", [128, 512]) for i in range(4)]
    p_msq = ps("p_msq", [1, 512])
    p_bc1 = ps("p_bc1", [128, 512])
    p_bc2 = ps("p_bc2", [128, 512])
    p_zg = ps("p_zg", [128, 512])

    @blk.gpsimd
    def _(gpsimd):
        for p in range(4):
            rB1w.sem(0).inc(gpsimd.dma_start(out=s_wp[p][:, :, :], in_=wproj[p]), 16)
        rB1w.sem(0).inc(gpsimd.dma_start(out=s_wf2[:], in_=wf2), 16)
        rB1w.sem(0).inc(gpsimd.dma_start(out=s_wg2[:], in_=wg2), 16)
        rB1w.sem(0).inc(gpsimd.dma_start(out=s_wbs[:, :, :], in_=wbm), 16)

    @blk.sync
    def _(sync):
        cS.wait(sync)

    prev_evac_marks = {}
    b1_evac_marks = []
    b1_silu_marks = []
    beta_tanh_marks = {}

    def b1_projs(t5):
        rk = (t5 * 512) // S
        toff = (t5 * 512) % S
        xt = s_xt[t5 % 2]
        ft = s_ft[t5 % 2]
        gt1 = s_gt1[t5 % 2]
        @blk.sync
        def _(sync):
            if t5 >= 2:
                sync.wait_ge(tE.h, prev_evac_marks.get(("mmdone", t5 - 2), 0))
            base = rk * AGR
            for dcg in range(4):
                rBx.sem(t5).inc(sync.dma_start(out=xt[:, dcg * 4:(dcg + 1) * 4, :],
                               in_=agout[base + dcg * 512:base + (dcg + 1) * 512,
                                         toff:toff + 512].rearrange("(dc p) t -> p dc t", p=128)), 16)
            rBx.sem(t5).inc(sync.dma_start(out=ft[:], in_=agout[base + HID:base + HID + 128, toff:toff + 512]), 16)
            rBx.sem(t5).inc(sync.dma_start(out=gt1[:], in_=agout[base + HID + 128:base + HID + 256, toff:toff + 512]), 16)

        for p in range(4):
            @blk.tensor
            def _(tensor, p=p):
                rBx.sem(t5).wait(tensor)
                if p == 0:
                    rB1w.sem(0).wait(tensor)
                if b1_evac_marks:
                    tensor.wait_ge(vE.h, b1_evac_marks[-1])
                for cb in range(4):
                    for dc in range(16):
                        mm = tensor.matmul(p_pr[cb][:], s_wp[p][:, dc, cb * 128:(cb + 1) * 128],
                                      xt[:, dc, :], start=(dc == 0), stop=(dc == 15))
                    tE.inc(mm)
                if p == 3:
                    tensor.wait_ge(vE.h, vE.n)
                    for dc in range(16):
                        mm = tensor.matmul(p_bc2[0:4, :], s_wbs[:, dc, :], xt[:, dc, :],
                                      start=(dc == 0), stop=(dc == 15))
                    tE.inc(mm)
            mm_mark = tE.n

            xh = s_xh[p]
            @blk.gpsimd
            def _(gpsimd, p=p):
                if t5 == 0:
                    gop(gpsimd, gpsimd.memset, xh[:, :, 0:3], 0.0)
            @blk.vector
            def _(vector, p=p):
                if t5 == 0:
                    gE.wait(vector)
                vector.wait_ge(tE.h, mm_mark)
                for cb in range(4):
                    vop(vector, vector.tensor_copy, out=xh[:, cb, 3:515], in_=p_pr[cb][:])
                b1_evac_marks.append(vE.n)
                if len(b1_silu_marks) >= 2:
                    vector.wait_ge(sE.h, b1_silu_marks[-2])
                for h in range(4):
                    cwc = lambda tap: s_cw[:, p * 16 + h * 4 + tap:p * 16 + h * 4 + tap + 1]
                    yc = s_yc[p % 2]
                    vop(vector, vector.tensor_scalar, out=yc[:, h, :], in0=xh[:, h, 3:515],
                        scalar1=cwc(3), scalar2=None, op0=ALU.mult)
                    for j in (1, 2, 3):
                        vop(vector, vector.scalar_tensor_tensor, out=yc[:, h, :],
                            in0=xh[:, h, 3 - j:515 - j], scalar=cwc(3 - j),
                            in1=yc[:, h, :], op0=ALU.mult, op1=ALU.add)
                vop(vector, vector.tensor_copy, out=xh[:, :, 0:3], in_=xh[:, :, 512:515])
            @blk.scalar
            def _(scalar, p=p):
                vE.wait(scalar)
                if t5 >= 2:
                    rBst.sem(t5).wait(scalar)
                sop(scalar, scalar.activation, out=s_out4[p][t5 % 2][:, :, :],
                    in_=s_yc[p % 2][:, :, :], func=AF.Silu)
                b1_silu_marks.append(sE.n)
                if p == 3:
                    sop(scalar, scalar.activation, out=s_beta[t5 % 2][:], in_=p_bc2[0:4, :],
                        func=AF.Tanh, scale=0.5)
                    beta_tanh_marks[t5] = sE.n
        @blk.vector
        def _(vector):
            sE.wait(vector)
            vop(vector, vector.tensor_scalar, out=s_beta[t5 % 2][:], in0=s_beta[t5 % 2][:],
                scalar1=0.5, scalar2=0.5, op0=ALU.mult, op1=ALU.add)
        prev_evac_marks[("beta_v", t5)] = vE.n

    def b1_tail(t5):
        par = t5 % 2
        ft = s_ft[par]
        gt1 = s_gt1[par]
        out4w = s_out4[3][par]
        @blk.scalar
        def _(scalar):
            if t5 > 0:
                rBst.sem(t5 - 1).wait(scalar)
            sop(scalar, scalar.square, s_wsq[:], out4w[:])
        for h in range(4):
            @blk.tensor
            def _(tensor, h=h):
                tensor.wait_ge(sE.h, sE.n)
                tE.inc(tensor.matmul(p_msq[:], s_onescb[:], s_wsq[:, h, :], start=True, stop=True))
            @blk.scalar
            def _(scalar, h=h):
                tE.wait(scalar)
                sop(scalar, scalar.activation, out=s_rsr[:, h * 512:(h + 1) * 512], in_=p_msq[:],
                    func=AF.Ln, bias=ccol("l2_eps")[0:1, :])
                sop(scalar, scalar.activation, out=s_rsr[:, h * 512:(h + 1) * 512],
                    in_=s_rsr[:, h * 512:(h + 1) * 512], func=AF.Exp, scale=-0.5)
        for h in range(4):
            @blk.tensor
            def _(tensor, h=h):
                tensor.wait_ge(sE.h, sE.n)
                tensor.wait_ge(vE.h, vE.n)
                tensor.wait_ge(vE.h, prev_evac_marks[("beta_v", t5)])
                if h == 0 and (t5 + 1) in beta_tanh_marks:
                    tensor.wait_ge(sE.h, beta_tanh_marks[t5 + 1])
                tE.inc(tensor.matmul(p_bc1[:], s_ones1b[:], s_rsr[:, h * 512:(h + 1) * 512],
                              start=True, stop=True))
                tE.inc(tensor.matmul(p_bc2[:], s_sel4[:, h * 128:(h + 1) * 128], s_beta[par][:],
                              start=True, stop=True))
            @blk.vector
            def _(vector, h=h):
                tE.wait(vector)
                vop(vector, vector.tensor_tensor, out=s_wbt[:, h, :], in0=out4w[:, h, :],
                    in1=p_bc1[:], op=ALU.mult)
                vop(vector, vector.tensor_tensor, out=s_wbt[:, h, :], in0=s_wbt[:, h, :],
                    in1=p_bc2[:], op=ALU.mult)
        for tt in range(4):
            @blk.tensor
            def _(tensor, tt=tt):
                tensor.wait_ge(sE.h, sE.n)
                tE.inc(tensor.matmul(p_zg[:], ft[:, tt * 128:(tt + 1) * 128], s_wf2[:],
                              start=True, stop=True))
            @blk.scalar
            def _(scalar, tt=tt):
                tE.wait(scalar)
                sop(scalar, scalar.copy, out=s_zgt[:, tt, :], in_=p_zg[:])
        for cb in range(4):
            @blk.tensor
            def _(tensor, cb=cb):
                tensor.wait_ge(sE.h, sE.n)
                tensor.wait_ge(vE.h, vE.n)
                tE.inc(tensor.matmul(p_bc1[:], s_wg2[:, cb * 128:(cb + 1) * 128], gt1[:],
                              start=True, stop=True))
            @blk.scalar
            def _(scalar, cb=cb):
                tE.wait(scalar)
                sop(scalar, scalar.activation, out=s_sgt2[:, cb, :], in_=p_bc1[:], func=AF.Tanh,
                    scale=0.5, bias=s_bg2h[:, cb:cb + 1])
        @blk.vector
        def _(vector):
            sE.wait(vector)
            vop(vector, vector.tensor_scalar, out=s_sgt2[:, :, :], in0=s_sgt2[:, :, :], scalar1=0.5,
                scalar2=0.5, op0=ALU.mult, op1=ALU.add)
        prev_evac_marks[("mmdone", t5)] = tE.n
        @blk.sync
        def _(sync):
            sync.wait_ge(vE.h, vE.n)
            sync.wait_ge(sE.h, sE.n)
            span = slice(t5 * 512, (t5 + 1) * 512)
            for p in range(3):
                for h in range(4):
                    rBst.sem(t5).inc(sync.dma_start(out=pr_scr[p][h, :, span], in_=s_out4[p][par][:, h, :]), 16)
            for h in range(4):
                rBst.sem(t5).inc(sync.dma_start(out=pr_scr[3][h, :, span], in_=s_wbt[:, h, :]), 16)
                rBst.sem(t5).inc(sync.dma_start(out=sg_scr[h, :, span], in_=s_sgt2[:, h, :]), 16)
            rBst.sem(t5).inc(sync.dma_start(out=zg_scr[span, :].rearrange("(tt p) m -> p tt m", p=128),
                           in_=s_zgt[:]), 16)

    for t5 in range(NT5B):
        b1_projs(t5)
        if t5 >= 1:
            b1_tail(t5 - 1)
    b1_tail(NT5B - 1)

    if taps:
        full_barrier()
        @blk.sync
        def _(sync):
            rOut.sem(0).inc(sync.dma_start(out=tp_xnT, in_=agout), 16)
            rOut.sem(0).inc(sync.dma_start(out=tp_fuse, in_=fuse_scr), 16)
            for p in range(4):
                rOut.sem(0).inc(sync.dma_start(out=tp_pr[p], in_=pr_scr[p]), 16)
            rOut.sem(0).inc(sync.dma_start(out=tp_zg, in_=zg_scr), 16)
            rOut.sem(0).inc(sync.dma_start(out=tp_sg, in_=sg_scr), 16)

    full_barrier()
    stB1.close()
    cur_stack[0] = ctx

    scope("pB2")
    stB2 = open_phase()
    rCh = Ring(nc, "rCh", 3)
    rO2 = Ring(nc, "rO2", 2)
    s_ones4 = sb("s_ones4", [4, 128], F32)
    s_qf = [sb(f"s_qf{i}", [128, 4, 128], BF16) for i in range(3)]
    s_kf = [sb(f"s_kf{i}", [128, 4, 128], BF16) for i in range(3)]
    s_wf = [sb(f"s_wbf{i}", [128, 4, 128], BF16) for i in range(3)]
    s_kt = [sb(f"s_kt{i}", [128, 4, 128], BF16) for i in range(3)]
    s_vt = [sb(f"s_vt{i}", [128, 4, 128], BF16) for i in range(3)]
    s_wt = [sb(f"s_wt{i}", [128, 4, 128], BF16) for i in range(3)]
    s_zg2 = [sb(f"s_zg2{i}", [128, 512], F32) for i in range(3)]
    s_sg2 = [sb(f"s_sg2{i}", [128, 4, 128], BF16) for i in range(3)]
    s_St = sb("s_St", [128, 4, 256], F32)
    s_Stb = sb("s_Stb", [128, 4, 256], BF16)
    s_SkT = sb("s_SkT", [128, 4, 128], BF16)
    s_sp = [sb(f"s_sp{i}", [128, 512], F32) for i in range(2)]
    s_Ltok = [sb(f"s_Ltok{i}", [128, 512], F32) for i in range(2)]
    s_Lf = [sb(f"s_Lf{i}", [128, 4, 128], F32) for i in range(2)]
    s_Ep = [sb(f"s_Ep{i}", [128, 512], F32) for i in range(2)]
    s_Emt = [sb(f"s_Emt{i}", [128, 512], F32) for i in range(2)]
    s_Emf = [sb(f"s_Emf{i}", [128, 4, 128], F32) for i in range(2)]
    s_Ptok = [sb(f"s_Ptok{i}", [128, 512], BF16) for i in range(2)]
    s_Pf = [sb(f"s_Pf{i}", [128, 4, 128], BF16) for i in range(2)]
    s_Am = [sb(f"s_Am{i}", [128, 4, 128], BF16) for i in range(2)]
    s_Bm = sb("s_Bm", [128, 4, 128], BF16)
    s_t1 = sb("s_t1", [128, 512], F32)
    s_scl = sb("s_scl", [128, 512], F32)
    s_expo = sb("s_expo", [128, 512], F32)
    s_dn = sb("s_dn", [128, 4, 1], F32)
    s_rdn = sb("s_rdn", [128, 4, 1], F32)
    s_t2 = sb("s_t2", [128, 512], F32)
    s_eopf = sb("s_eopf", [128, 512], F32)
    s_eop = sb("s_eop", [128, 4, 128], BF16)
    s_ehat = sb("s_ehat", [128, 4, 128], BF16)
    s_rc = sb("s_rc", [128, 4, 1], F32)
    s_rexp = sb("s_rexp", [128, 4, 1], F32)
    s_tail = sb("s_tail", [128, 4, 1], F32)
    s_osq = sb("s_osq", [128, 512], BF16)
    s_r2 = sb("s_r2", [1, 512], F32)
    s_rsb2 = sb("s_rsb2", [128, 512], F32)
    s_t3 = sb("s_t3", [128, 512], F32)
    s_ofin = [sb(f"s_ofin{i}", [128, 4, 128], BF16) for i in range(2)]
    p_misc = ps("p_misc", [128, 512])
    p_lf = ps("p_lf", [128, 4, 128])
    p_ab = [ps(f"p_ab{i}", [128, 4, 128]) for i in range(2)]
    p_so = ps("p_so", [128, 512])
    p_add = ps("p_add", [128, 512])
    p_et = ps("p_et", [128, 4, 128], BF16)

    rDw = Ring(nc, "rDw", 1)
    rDf = Ring(nc, "rDf", 2)
    rDst = Ring(nc, "rDst", 2)
    s_sdnb = sb("s_sdnb", [128, HID], F32)
    s_wdnp = sb("s_wdnp", [128, 64, 512], BF16)
    s_fuT = [sb(f"s_fuT{i}", [128, 64, 128], BF16) for i in range(2)]
    s_do = [sb(f"s_do{i}", [128, 512], F32) for i in range(2)]
    p_dn = ps("p_dn", [128, 512])

    @blk.gpsimd
    def _(gpsimd):
        gop(gpsimd, gpsimd.memset, s_ones4[:], 1.0)
        gop(gpsimd, gpsimd.memset, s_St[:], 0.0)
        gop(gpsimd, gpsimd.memset, s_Stb[:], 0.0)
        gop(gpsimd, gpsimd.memset, s_tail[:], 0.0)

    for dp in range(4):
        @blk.tensor
        def _(tensor, dp=dp):
            if dp == 0:
                rI.sem(0).wait(tensor)
            else:
                tensor.wait_ge(sE.h, sE.n)
            tE.inc(tensor.matmul(p_dn[:], s_ones1f[:], s_sdnr[:, dp * 512:(dp + 1) * 512],
                                 start=True, stop=True))
        @blk.scalar
        def _(scalar, dp=dp):
            tE.wait(scalar)
            sop(scalar, scalar.copy, out=s_sdnb[:, dp * 512:(dp + 1) * 512], in_=p_dn[:])
    sdnb_mark = sE.n

    dn_marks = {}
    dn_evac_marks = {}

    def dn_wload(dp):
        @blk.gpsimd
        def _(gpsimd):
            if dp > 0:
                gpsimd.wait_ge(tE.h, dn_marks[(dp - 1, NT1 - 1)])
            rDw.sem(0).inc(gpsimd.dma_start(out=s_wdnp[:, :, :], in_=wdn[dp]), 16)

    def dn_loads(u):
        if u >= 4 * NT1:
            return
        dp, it = u // NT1, u % NT1
        @blk.sync
        def _(sync):
            if u >= 2:
                sync.wait_ge(tE.h, dn_marks.get((dp, it - 2) if it >= 2 else (dp - 1, NT1 - 2 + it), 0))
            rDf.sem(u).inc(sync.dma_start(
                out=s_fuT[u % 2][:, :, :],
                in_=fuse_scr[:, it * 128:(it + 1) * 128].rearrange("(gi p) t -> p gi t", p=128)), 16)

    def dn_begin(u):
        if u >= 4 * NT1:
            return None
        return {"u": u, "dp": u // NT1, "it": u % NT1, "fslot": s_fuT[u % 2], "ic": 0}

    def dn_fill(st, n):
        if st is None or st["ic"] >= 64:
            return
        u, ic0 = st["u"], st["ic"]
        n = min(n, 64 - ic0)
        @blk.tensor
        def _(tensor):
            if ic0 == 0:
                rDw.sem(0).wait(tensor)
                rDf.sem(u).wait(tensor)
                if u == 0:
                    tensor.wait_ge(sE.h, sdnb_mark)
                if u >= 1:
                    tensor.wait_ge(vE.h, dn_evac_marks[u - 1])
            for ic in range(ic0, ic0 + n):
                mm = tensor.matmul(p_dn[:], st["fslot"][:, ic, :], s_wdnp[:, ic, :],
                                   start=(ic == 0), stop=(ic == 63))
            tE.inc(mm)
        st["ic"] = ic0 + n

    def dn_finish(st):
        if st is None:
            return
        dn_fill(st, 64 - st["ic"])
        u, dp, it = st["u"], st["dp"], st["it"]
        dn_marks[(dp, it)] = tE.n
        if it == NT1 - 1 and dp < 3:
            dn_wload(dp + 1)
        @blk.vector
        def _(vector):
            vector.wait_ge(tE.h, dn_marks[(dp, it)])
            vector.wait_ge(sE.h, sdnb_mark)
            if u >= 2:
                rDst.sem(u).wait(vector)
            vop(vector, vector.tensor_tensor, out=s_do[u % 2][:], in0=p_dn[:],
                in1=s_sdnb[:, dp * 512:(dp + 1) * 512], op=ALU.mult)
        dn_evac_marks[u] = vE.n
        @blk.sync
        def _(sync):
            sync.wait_ge(vE.h, dn_evac_marks[u])
            rDst.sem(u).inc(sync.dma_start(
                out=ffn_scr[it * 128:(it + 1) * 128, dp * 512:(dp + 1) * 512], in_=s_do[u % 2][:]), 16)

    dn_wload(0)
    dn_loads(0)
    dn_loads(1)
    chunk_done = {}
    clamp_marks = {}
    rc_marks = {}
    epi_marks = {}
    t1_marks = {}
    sp_marks = {}
    lcmm_marks = {}

    def b2_prefix(ch):
        par = ch % 2
        sl4 = ch % 3
        span = slice(ch * 128, (ch + 1) * 128)
        qf, kf, wf = s_qf[sl4], s_kf[sl4], s_wf[sl4]
        kt, vt, wt = s_kt[sl4], s_vt[sl4], s_wt[sl4]
        zg2, sg2 = s_zg2[sl4], s_sg2[sl4]

        @blk.sync
        def _(sync):
            if ch >= 3:
                d = chunk_done[ch - 3]
                sync.wait_ge(tE.h, d["t"]); sync.wait_ge(vE.h, d["v"]); sync.wait_ge(sE.h, d["s"])
            rCh.sem(ch).inc(sync.dma_start(out=qf[:, :, :], in_=pr_scr[0][:, :, span].rearrange("h c t -> c h t")), 16)
            rCh.sem(ch).inc(sync.dma_start(out=kf[:, :, :], in_=pr_scr[1][:, :, span].rearrange("h c t -> c h t")), 16)
            rCh.sem(ch).inc(sync.dma_start(out=wf[:, :, :], in_=pr_scr[3][:, :, span].rearrange("h c t -> c h t")), 16)
            rCh.sem(ch).inc(sync.dma_start(out=sg2[:, :, :], in_=sg_scr[:, :, span].rearrange("h c t -> c h t")), 16)
            rCh.sem(ch).inc(sync.dma_start(out=zg2[:], in_=zg_scr[span, :]), 16)
            rCh.sem(ch).inc(sync.dma_start(out=kt[:, :, :].rearrange("t h c -> t (h c)"),
                                           in_=pr_scr[1][:, :, span].rearrange("h c t -> (h c) t"),
                                           transpose=True), 16)
            rCh.sem(ch).inc(sync.dma_start(out=vt[:, :, :].rearrange("t h c -> t (h c)"),
                                           in_=pr_scr[2][:, :, span].rearrange("h c t -> (h c) t"),
                                           transpose=True), 16)
            rCh.sem(ch).inc(sync.dma_start(out=wt[:, :, :].rearrange("t h c -> t (h c)"),
                                           in_=pr_scr[3][:, :, span].rearrange("h c t -> (h c) t"),
                                           transpose=True), 16)

        @blk.scalar
        def _(scalar):
            rCh.sem(ch).wait(scalar)
            sop(scalar, scalar.activation, out=s_sp[par][:], in_=zg2[:], func=AF.Exp, scale=-1.0)
            sop(scalar, scalar.activation, out=s_sp[par][:], in_=s_sp[par][:], func=AF.Ln, bias=ccol("one"))
        sp_marks[ch] = sE.n

        @blk.tensor
        def _(tensor):
            tensor.wait_ge(sE.h, sp_marks[ch])
            if ch >= 1:
                tensor.wait_ge(vE.h, clamp_marks[ch - 1])
            if ch == 0:
                gE.wait(tensor)
            tE.inc(tensor.matmul(p_misc[:], s_mcx[:, 0:128], s_sp[par][:], start=True, stop=True))
            for h in range(4):
                tE.inc(tensor.matmul(p_lf[:, h, :], s_sp[par][:, h * 128:(h + 1) * 128],
                                     s_mcx[:, 0:128], start=True, stop=True))
        lmm_mark = tE.n

        @blk.vector
        def _(vector):
            vector.wait_ge(tE.h, lmm_mark)
            vop(vector, vector.tensor_scalar, out=s_Ltok[par][:], in0=p_misc[:], scalar1=80.0,
                scalar2=-80.0, op0=ALU.min, op1=ALU.max)
            vop(vector, vector.tensor_scalar, out=s_Lf[par][:, :, :], in0=p_lf[:, :, :], scalar1=80.0,
                scalar2=-80.0, op0=ALU.min, op1=ALU.max)
        clamp_marks[ch] = vE.n

        @blk.scalar
        def _(scalar):
            scalar.wait_ge(vE.h, clamp_marks[ch])
            sop(scalar, scalar.activation, out=s_Ep[par][:], in_=s_Ltok[par][:], func=AF.Exp, scale=-1.0)
            sop(scalar, scalar.activation, out=s_Emt[par][:], in_=s_Ltok[par][:], func=AF.Exp)
            sop(scalar, scalar.activation, out=s_Emf[par][:, :, :], in_=s_Lf[par][:, :, :], func=AF.Exp)
        eexp_mark = sE.n

        @blk.vector
        def _(vector):
            vector.wait_ge(sE.h, eexp_mark)
            vop(vector, vector.tensor_tensor, out=s_Ptok[par][:],
                in0=wt[:, :, :].rearrange("s h m -> s (h m)"), in1=s_Emt[par][:], op=ALU.mult)
            vop(vector, vector.tensor_tensor, out=s_Pf[par][:, :, :], in0=wf[:, :, :],
                in1=s_Emf[par][:, :, :], op=ALU.mult)
        p_marks_ch = vE.n

        @blk.tensor
        def _(tensor):
            rCh.sem(ch).wait(tensor)
            if ch >= 2:
                tensor.wait_ge(vE.h, epi_marks[ch - 2])
            for h in range(4):
                tE.inc(tensor.matmul(p_ab[par][:, h, :], kf[:, h, :], qf[:, h, :], start=True, stop=True))
        amm_mark = tE.n
        @blk.vector
        def _(vector):
            vector.wait_ge(tE.h, amm_mark)
            vop(vector, vector.tensor_copy, out=s_Am[par][:, :, :], in_=p_ab[par][:, :, :])
        amcp_mark = vE.n
        @blk.gpsimd
        def _(gpsimd):
            gpsimd.wait_ge(vE.h, amcp_mark)
            gop(gpsimd, gpsimd.affine_select, out=s_Am[par][:, :, :], in_=s_Am[par][:, :, :],
                pattern=[[0, 4], [1, 128]], compare_op=ALU.is_ge, fill=0.0, base=0, channel_multiplier=-1)
        return {"p": p_marks_ch}

    def b2_suffix(ch, pm):
        par = ch % 2
        sl4 = ch % 3
        span = slice(ch * 128, (ch + 1) * 128)
        qf = s_qf[sl4]
        kt, vt = s_kt[sl4], s_vt[sl4]
        sg2 = s_sg2[sl4]
        ofin = s_ofin[par]

        s_base = sE.n
        rexp_mark = s_base + 1
        expo_mark = s_base + 2
        eopf_mark = s_base + 3
        osq_mark = s_base + 4
        r2_mark = s_base + 6

        dnst = dn_begin(ch)
        dn_fill(dnst, 6)
        @blk.tensor
        def _(tensor):
            if ch >= 1:
                tensor.wait_ge(vE.h, t1_marks[ch - 1])
            for h in range(4):
                tE.inc(tensor.matmul(p_so[:, h:h + 1], s_sp[par][:, h * 128:(h + 1) * 128],
                                     s_mcx[:, 128:129], start=True, stop=True))
        lcmm_marks[ch] = tE.n
        @blk.vector
        def _(vector):
            vector.wait_ge(tE.h, lcmm_marks[ch])
            vop(vector, vector.tensor_tensor, out=s_rc[:, :, :],
                in0=bass.AP(p_so, 0, [[512, 128], [1, 4], [1, 1]]),
                in1=s_tail[:, :, :], op=ALU.add)
        rc_marks[ch] = vE.n
        @blk.vector
        def _(vector):
            vector.wait_ge(sE.h, rexp_mark)
            if ch > 0:
                for h in range(4):
                    vop(vector, vector.tensor_scalar, out=s_St[:, h, :], in0=s_St[:, h, :],
                        scalar1=s_rexp[:, h, 0:1].opt(), scalar2=None, op0=ALU.mult)
            vop(vector, vector.tensor_copy, out=s_Stb[:, :, :], in_=s_St[:, :, :])
        resc_mark = vE.n
        dn_fill(dnst, 6)
        @blk.tensor
        def _(tensor):
            tensor.wait_ge(vE.h, resc_mark)
            if ch >= 1:
                tensor.wait_ge(vE.h, ehat_marks[ch - 1])
            for h in range(4):
                tE.inc(tensor.transpose(p_et[:, h, :], s_Stb[:, h, 0:128], s_identb[:]))
        sktr_mark = tE.n
        @blk.vector
        def _(vector):
            vector.wait_ge(tE.h, sktr_mark)
            vop(vector, vector.tensor_copy, out=s_SkT[:, :, :], in_=p_et[:, :, :])
        skt_mark = vE.n

        dn_fill(dnst, 8)
        @blk.tensor
        def _(tensor):
            tensor.wait_ge(vE.h, skt_mark)
            tensor.wait_ge(gE.h, gE.n)
            tensor.wait_ge(vE.h, pm["p"])
            for h in range(4):
                hs = slice(h * 128, (h + 1) * 128)
                tensor.matmul(p_so[:, hs], qf[:, h, :], s_SkT[:, h, :], start=True, stop=False)
                tE.inc(tensor.matmul(p_so[:, hs], s_Am[par][:, h, :], s_Ptok[par][:, hs], start=False, stop=True))
        smm_mark = tE.n

        @blk.vector
        def _(vector):
            vector.wait_ge(tE.h, smm_mark)
            vop(vector, vector.tensor_tensor, out=s_t1[:], in0=p_so[:], in1=s_Ep[par][:], op=ALU.mult)
            vop(vector, vector.tensor_scalar, out=s_scl[:], in0=s_t1[:], scalar1=SCALE,
                scalar2=70.0, op0=ALU.mult, op1=ALU.min)
        t1_marks[ch] = vE.n
        @blk.vector
        def _(vector):
            vector.wait_ge(sE.h, expo_mark)
            vop(vector, vector.tensor_reduce, out=s_dn[:, :, :],
                in_=s_expo[:].rearrange("t (h m) -> t h m", m=128), axis=AX.X, op=ALU.add)
            vop(vector, vector.reciprocal, out=s_rdn[:, :, :], in_=s_dn[:, :, :])
            vop(vector, vector.tensor_tensor, out=s_t2[:], in0=s_scl[:], in1=s_Ltok[par][:], op=ALU.subtract)
            vop(vector, vector.tensor_scalar, out=s_t2[:], in0=s_t2[:], scalar1=85.0, scalar2=None,
                op0=ALU.min)
        t2_mark = vE.n
        @blk.vector
        def _(vector):
            vector.wait_ge(sE.h, eopf_mark)
            for h in range(4):
                hs = slice(h * 128, (h + 1) * 128)
                vop(vector, vector.tensor_scalar, out=s_eop[:, h, :], in0=s_eopf[:, hs],
                    scalar1=s_rdn[:, h, 0:1].opt(), scalar2=None, op0=ALU.mult)
        eop_mark = vE.n
        dn_fill(dnst, 8)
        @blk.tensor
        def _(tensor):
            tensor.wait_ge(vE.h, eop_mark)
            tensor.wait_ge(vE.h, skt_mark)
            for h in range(4):
                tE.inc(tensor.transpose(p_et[:, h, :], s_eop[:, h, :], s_identb[:]))
        etr_mark = tE.n
        @blk.vector
        def _(vector):
            vector.wait_ge(tE.h, etr_mark)
            vop(vector, vector.tensor_copy, out=s_ehat[:, :, :], in_=p_et[:, :, :])
        ehat_marks[ch] = vE.n

        dn_fill(dnst, 6)
        @blk.tensor
        def _(tensor):
            tensor.wait_ge(vE.h, ehat_marks[ch])
            tensor.wait_ge(vE.h, pm["p"])
            for h in range(4):
                tE.inc(tensor.matmul(p_ab[par][:, h, :], s_Pf[par][:, h, :], s_ehat[:, h, :],
                                     start=True, stop=True))
        bmm_mark = tE.n
        @blk.vector
        def _(vector):
            vector.wait_ge(tE.h, bmm_mark)
            vop(vector, vector.tensor_copy, out=s_Bm[:, :, :], in_=p_ab[par][:, :, :])
        bmcp_mark = vE.n
        @blk.gpsimd
        def _(gpsimd):
            gpsimd.wait_ge(vE.h, bmcp_mark)
            gop(gpsimd, gpsimd.affine_select, out=s_Bm[:, :, :], in_=s_Bm[:, :, :],
                pattern=[[0, 4], [1, 128]], compare_op=ALU.is_ge, fill=0.0, base=0, channel_multiplier=-1)
        dn_fill(dnst, 8)
        @blk.tensor
        def _(tensor):
            tensor.wait_ge(gE.h, gE.n)
            for h in range(4):
                tensor.matmul(p_ab[par][:, h, :], s_Stb[:, h, 128:256], s_ehat[:, h, :],
                              start=True, stop=False)
                tE.inc(tensor.matmul(p_ab[par][:, h, :], vt[:, h, :], s_Bm[:, h, :],
                                     start=False, stop=True))
        omm_mark = tE.n
        dn_fill(dnst, 8)
        @blk.tensor
        def _(tensor):
            tensor.wait_ge(sE.h, osq_mark)
            if ch >= 1:
                tensor.wait_ge(vE.h, svadd_marks[ch - 1])
            tE.inc(tensor.matmul(p_add[0:1, 0:512], s_onescb[:], s_osq[:], start=True, stop=True))
        msq_mark = tE.n
        dn_fill(dnst, 6)
        @blk.tensor
        def _(tensor):
            tensor.wait_ge(sE.h, r2_mark)
            tE.inc(tensor.matmul(p_add[:], s_ones1f[:], s_r2[:], start=True, stop=True))
        rsb_mark = tE.n
        @blk.vector
        def _(vector):
            vector.wait_ge(tE.h, rsb_mark)
            if ch >= 2:
                rO2.sem(ch).wait(vector)
            vop(vector, vector.tensor_copy, out=s_rsb2[:], in_=p_add[:])
            vop(vector, vector.tensor_tensor, out=s_t3[:],
                in0=p_ab[par][:, :, :].rearrange("e h t -> e (h t)"), in1=s_rsb2[:], op=ALU.mult)
            vop(vector, vector.tensor_tensor, out=ofin[:, :, :].rearrange("e h t -> e (h t)"),
                in0=s_t3[:], in1=sg2[:, :, :].rearrange("e h t -> e (h t)"), op=ALU.mult)
        epi_marks[ch] = vE.n

        dn_fill(dnst, 4)
        @blk.tensor
        def _(tensor):
            tensor.wait_ge(vE.h, epi_marks[ch])
            for h in range(4):
                hs = slice(h * 128, (h + 1) * 128)
                tE.inc(tensor.matmul(p_add[:, hs], s_Ptok[par][:, hs], kt[:, h, :],
                                     start=True, stop=True))
        kadd_mark = tE.n
        @blk.vector
        def _(vector):
            vector.wait_ge(tE.h, kadd_mark)
            vop(vector, vector.tensor_tensor, out=s_St[:, :, 0:128], in0=s_St[:, :, 0:128],
                in1=p_add[:].rearrange("m (h d) -> m h d", d=128), op=ALU.add)
        skadd_mark = vE.n
        dn_fill(dnst, 4)
        @blk.tensor
        def _(tensor):
            tensor.wait_ge(vE.h, skadd_mark)
            for h in range(4):
                hs = slice(h * 128, (h + 1) * 128)
                tE.inc(tensor.matmul(p_add[:, hs], s_Ptok[par][:, hs], vt[:, h, :],
                                     start=True, stop=True))
        vadd_mark = tE.n
        @blk.vector
        def _(vector):
            vector.wait_ge(tE.h, vadd_mark)
            vop(vector, vector.tensor_tensor, out=s_St[:, :, 128:256], in0=s_St[:, :, 128:256],
                in1=p_add[:].rearrange("m (h e) -> m h e", e=128), op=ALU.add)
            vop(vector, vector.tensor_copy, out=s_tail[:, :, :], in_=s_Lf[par][:, :, 127:128])
        svadd_marks[ch] = vE.n

        @blk.scalar
        def _(scalar):
            scalar.wait_ge(vE.h, rc_marks[ch])
            sop(scalar, scalar.activation, out=s_rexp[:, :, :], in_=s_rc[:, :, :], func=AF.Exp, scale=-1.0)
            scalar.wait_ge(vE.h, t1_marks[ch])
            sop(scalar, scalar.activation, out=s_expo[:], in_=s_scl[:], func=AF.Exp, bias=ccol("neg_shift"))
            scalar.wait_ge(vE.h, t2_mark)
            sop(scalar, scalar.activation, out=s_eopf[:], in_=s_t2[:], func=AF.Exp, bias=ccol("neg_shift"))
            scalar.wait_ge(tE.h, omm_mark)
            sop(scalar, scalar.square, s_osq[:], p_ab[par][:, :, :].rearrange("e h t -> e (h t)"))
            scalar.wait_ge(tE.h, msq_mark)
            sop(scalar, scalar.activation, out=s_r2[:], in_=p_add[0:1, 0:512], func=AF.Ln,
                scale=1.0 / 128.0, bias=ccol("rms_eps")[0:1, :])
            sop(scalar, scalar.activation, out=s_r2[:], in_=s_r2[:], func=AF.Exp, scale=-0.5)
        assert sE.n == s_base + 6, (sE.n, s_base)

        jq = (ch * 128) // S
        cspan = slice((ch * 128) % S, (ch * 128) % S + 128)
        @blk.sync
        def _(sync):
            sync.wait_ge(vE.h, epi_marks[ch])
            rO2.sem(ch).inc(sync.dma_start(
                out=agin2[jq * 512:(jq + 1) * 512, cspan].rearrange("(h c) t -> c h t", c=128),
                in_=ofin[:, :, :]), 16)
        dn_finish(dnst)
        dn_loads(ch + 2)
        chunk_done[ch] = {"t": tE.n, "v": vE.n, "s": sE.n}

    ehat_marks = {}
    svadd_marks = {}
    pm_store = {}
    for ch in range(NCH):
        pm_store[ch] = b2_prefix(ch)
        if ch >= 1:
            b2_suffix(ch - 1, pm_store[ch - 1])
    b2_suffix(NCH - 1, pm_store[NCH - 1])

    @blk.gpsimd
    def _(gpsimd):
        for ch in range(max(0, NCH - 2), NCH):
            rO2.sem(ch).wait(gpsimd)
        gpsimd.collective_compute("AllGather", ALU.bypass, replica_groups=rg,
                                  ins=[agin2.opt()], outs=[agout2.opt()]).then_inc(cS.h, 1)
    cS.n += 1

    full_barrier()
    stB2.close()

    scope("pC2")
    stC2 = open_phase()
    rCw = Ring(nc, "rCw", 1)
    rCq = Ring(nc, "rCq", 1)
    rCo = Ring(nc, "rCo", 2)
    rCx = Ring(nc, "rCx", 2)
    rCf = Ring(nc, "rCf", 2)
    rOq = Ring(nc, "rOq", 2)
    s_qoff = sb("s_qoff", [1, 1], mybir.dt.uint32)
    s_wos = sb("s_wos", [128, 16, 2048], BF16)
    s_ot = [sb(f"s_ot{i}", [128, 16, 128], BF16) for i in range(2)]
    s_xc = [sb(f"s_xc{i}", [128, HID], F32) for i in range(2)]
    s_fc = [sb(f"s_fc{i}", [128, HID], F32) for i in range(2)]
    s_oq = [sb(f"s_oq{i}", [128, HID], F32) for i in range(2)]
    p_c2 = [ps(f"p_c2{i}", [128, 512]) for i in range(2)]

    @blk.gpsimd
    def _(gpsimd):
        rCw.sem(0).inc(gpsimd.dma_start(out=s_wos[:, :, :], in_=wo), 16)
    @blk.sync
    def _(sync):
        rCq.sem(0).inc(sync.dma_start(out=s_qoff[:], in_=qoff), 16)
    rSel = Ring(nc, "rSel", 1)
    @blk.gpsimd
    def _(gpsimd):
        from concourse.expressions import RuntimeValue
        rCq.sem(0).wait(gpsimd)
        cS.wait(gpsimd)
        rq = gpsimd.alloc_register("qreg")
        gpsimd.reg_load(rq, s_qoff[0:1, 0:1])
        gpsimd.reg_mul(rq, rq, 512 * S)
        for g in range(4):
            dyn = bass.AP(agout2.tensor, RuntimeValue(rq), [[S, 512], [1, S]])
            rSel.sem(0).inc(gpsimd.dma_start(out=osel[g * 512:(g + 1) * 512, :], in_=dyn), 16)
            if g < 3:
                gpsimd.reg_add(rq, rq, HID * S)
    @blk.vector
    def _(vector):
        rCw.sem(0).wait(vector)
        rI.sem(0).wait(vector)
        vop(vector, vector.tensor_scalar, out=s_wos[:, :, :], in0=s_wos[:, :, :],
            scalar1=s_normw[:, 0:1], scalar2=None, op0=ALU.mult)
    wos_mark = vE.n

    c2_marks = {}
    for it in range(NT1):
        ot, xc, fc, oq = s_ot[it % 2], s_xc[it % 2], s_fc[it % 2], s_oq[it % 2]
        @blk.sync
        def _(sync):
            if it >= 2:
                sync.wait_ge(tE.h, c2_marks[it - 2])
            rSel.sem(0).wait(sync)
            for g in range(4):
                rCo.sem(it).inc(sync.dma_start(
                    out=ot[:, g * 4:(g + 1) * 4, :],
                    in_=osel[g * 512:(g + 1) * 512,
                             it * 128:(it + 1) * 128].rearrange("(dc p) t -> p dc t", p=128)), 16)
            rCx.sem(it).inc(sync.dma_start(out=xc[:], in_=x_q[it * 128:(it + 1) * 128, :]), 16)
            rCf.sem(it).inc(sync.dma_start(out=fc[:], in_=ffn_scr[it * 128:(it + 1) * 128, :]), 16)
        for dp in range(4):
            @blk.tensor
            def _(tensor, dp=dp):
                rCo.sem(it).wait(tensor)
                if it == 0 and dp == 0:
                    tensor.wait_ge(vE.h, wos_mark)
                if it > 0 or dp > 1:
                    tensor.wait_ge(vE.h, vE.n)
                for dc in range(16):
                    mm = tensor.matmul(p_c2[dp % 2][:], ot[:, dc, :],
                                       s_wos[:, dc, dp * 512:(dp + 1) * 512],
                                       start=(dc == 0), stop=(dc == 15))
                tE.inc(mm)
            @blk.vector
            def _(vector, dp=dp):
                tE.wait(vector)
                rCx.sem(it).wait(vector)
                rCf.sem(it).wait(vector)
                if it >= 2 and dp == 0:
                    rOq.sem(it).wait(vector)
                dsl = slice(dp * 512, (dp + 1) * 512)
                vop(vector, vector.tensor_tensor, out=oq[:, dsl], in0=p_c2[dp % 2][:],
                    in1=fc[:, dsl], op=ALU.add)
                vop(vector, vector.tensor_tensor, out=oq[:, dsl], in0=oq[:, dsl],
                    in1=xc[:, dsl], op=ALU.add)
        c2_marks[it] = tE.n
        @blk.sync
        def _(sync):
            sync.wait_ge(vE.h, vE.n)
            rOq.sem(it).inc(sync.dma_start(out=out_q[it * 128:(it + 1) * 128, :], in_=oq[:]), 16)

    if taps:
        full_barrier()
        @blk.sync
        def _(sync):
            rOut.sem(0).inc(sync.dma_start(out=tp_o, in_=agin2), 16)
            rOut.sem(0).inc(sync.dma_start(out=tp_ffn, in_=ffn_scr), 16)

    full_barrier()
    @blk.sync
    def _(sync):
        for it in range(NT1):
            rOq.sem(it).wait(sync)
        rOut.sem(0).wait(sync)
    stC2.close()

    if scope_stack:
        nc.leave_named_scope(scope_stack[-1][0], scope_stack[-1][1], False)
        scope_stack.pop()
    ctx.close()
    return nc


def shard_inputs(inputs, T=16384):
    S = T // 4
    f32 = np.float32
    x = np.ascontiguousarray(np.asarray(inputs["L0_attn_hidden_states"], f32).reshape(2, -1, HID)[:, :T])
    NIB = INTER // 128
    cq, ck, cv = (np.asarray(inputs[f"L0_attn_c{n}"], f32) for n in "qkv")

    def rt16(w):
        C = w.shape[1]
        return np.ascontiguousarray(w.reshape(16, 128, C).transpose(1, 0, 2))

    NIBv = INTER // 128
    wgt_r = np.ascontiguousarray(np.asarray(inputs["L0_ffn_w_gate"])
                                 .reshape(16, 128, NIBv, 128).transpose(2, 1, 0, 3))
    wup_r = np.ascontiguousarray(np.asarray(inputs["L0_ffn_w_up"])
                                 .reshape(16, 128, NIBv, 128).transpose(2, 1, 0, 3))
    wdn_r = np.ascontiguousarray(np.asarray(inputs["L0_ffn_w_down"])
                                 .reshape(64, 128, 4, 512).transpose(2, 1, 0, 3))
    maps = []
    for c in range(8):
        b, hg = c // 4, c % 4
        sl = slice(hg * 512, (hg + 1) * 512)
        cwa = np.stack([cq[sl], ck[sl], cv[sl], cv[sl]])
        cwa = cwa.reshape(4, 4, 128, 4).transpose(2, 0, 1, 3)
        m = {
            "x_q": np.ascontiguousarray(x[b, hg * S:(hg + 1) * S]),
            "wq": rt16(np.asarray(inputs["L0_attn_Wq"], f32)[:, sl]),
            "wk": rt16(np.asarray(inputs["L0_attn_Wk"], f32)[:, sl]),
            "wv": rt16(np.asarray(inputs["L0_attn_Wv"], f32)[:, sl]),
            "ww": rt16(np.asarray(inputs["L0_attn_Ww"], f32)[:, sl]),
            "cw": np.ascontiguousarray(cwa, f32),
            "wf1": rt16(np.asarray(inputs["L0_attn_Wf1"], f32)),
            "wf2": np.ascontiguousarray(np.asarray(inputs["L0_attn_Wf2"], f32)[:, sl]),
            "wbm": rt16(np.asarray(inputs["L0_attn_Wb"], f32)[:, hg * 4:(hg + 1) * 4]),
            "wg1": rt16(np.asarray(inputs["L0_attn_Wg1"], f32)),
            "wg2": np.ascontiguousarray(np.asarray(inputs["L0_attn_Wg2"], f32)[:, sl]),
            "bg2": np.ascontiguousarray(np.asarray(inputs["L0_attn_bg2"], f32)[sl].reshape(4, 128).T),
            "normw": np.ascontiguousarray(np.asarray(inputs["L0_attn_norm_w"], f32).reshape(128, 1)),
            "wo": rt16(np.asarray(inputs["L0_attn_Wo"], f32)),
            "wgt": wgt_r, "wup": wup_r, "wdn": wdn_r,
            "sgt": np.ascontiguousarray(np.asarray(inputs["L0_ffn_s_gate"], f32).reshape(NIB, 128).T),
            "sup": np.ascontiguousarray(np.asarray(inputs["L0_ffn_s_up"], f32).reshape(NIB, 128).T),
            "sdn": np.ascontiguousarray(np.asarray(inputs["L0_ffn_s_down"], f32).reshape(1, HID)),
            "qoff": np.array([[hg]], np.uint32),
        }
        maps.append(m)
    return maps


def kernel(**inputs):
    T = 16384
    nc = build_program(T=T, taps=False)
    in_maps = shard_inputs(inputs, T)
    res = run_bass_kernel_spmd(nc, in_maps, core_ids=list(range(8)))
    S = T // 4
    out = np.zeros((2, T, HID), np.float32)
    for c in range(8):
        b, hg = c // 4, c % 4
        out[b, hg * S:(hg + 1) * S] = res.results[c]["out_q"]
    return out


# version 07
# speedup vs baseline: 1.1583x; 1.1583x over previous
import numpy as np
import ml_dtypes
import concourse.bass as bass
import concourse.mybir as mybir
from concourse.bass_utils import run_bass_kernel_spmd

F32 = mybir.dt.float32
BF16 = mybir.dt.bfloat16
I8 = mybir.dt.int8
AF = mybir.ActivationFunctionType
ALU = mybir.AluOpType
AX = mybir.AxisListType

HID = 2048
HPC = 4
INTER = 8192
CCH = 128
MID = 64
SCALE = 128.0 ** -0.5


ALL_SEMS = []


class Sem:
    def __init__(self, nc, name):
        self.h = nc.alloc_semaphore(name=name)
        self.n = 0
        ALL_SEMS.append(self)

    def inc(self, inst, k=1):
        inst.then_inc(self.h, k)
        self.n += k
        return inst

    def wait(self, eng, val=None):
        eng.wait_ge(self.h, self.n if val is None else val)


class Ring:
    def __init__(self, nc, name, slots):
        self.s = [Sem(nc, f"{name}{i}") for i in range(slots)]
        self.k = slots

    def sem(self, i):
        return self.s[i % self.k]


def build_program(T=16384, taps=False):
    ALL_SEMS.clear()
    S = T // 4
    NT1 = S // 128
    NT5Q = S // 512
    NT5B = T // 512
    NCH = T // CCH
    NIB = INTER // 128

    nc = bass.Bass("TRN2", target_bir_lowering=False)
    ap = lambda t: t.ap()
    x_q = ap(nc.dram_tensor("x_q", [S, HID], F32, kind="ExternalInput"))
    wproj = [ap(nc.dram_tensor(n, [128, 16, 512], F32, kind="ExternalInput")) for n in ("wq", "wk", "wv", "ww")]
    cw = ap(nc.dram_tensor("cw", [128, 4, 4, 4], F32, kind="ExternalInput"))
    wf1 = ap(nc.dram_tensor("wf1", [128, 16, 128], F32, kind="ExternalInput"))
    wf2 = ap(nc.dram_tensor("wf2", [128, 512], F32, kind="ExternalInput"))
    wbm = ap(nc.dram_tensor("wbm", [128, 16, 4], F32, kind="ExternalInput"))
    wg1 = ap(nc.dram_tensor("wg1", [128, 16, 128], F32, kind="ExternalInput"))
    wg2 = ap(nc.dram_tensor("wg2", [128, 512], F32, kind="ExternalInput"))
    bg2 = ap(nc.dram_tensor("bg2", [128, 4], F32, kind="ExternalInput"))
    normw = ap(nc.dram_tensor("normw", [128, 1], F32, kind="ExternalInput"))
    wo = ap(nc.dram_tensor("wo", [128, 16, HID], F32, kind="ExternalInput"))
    wgt = ap(nc.dram_tensor("wgt", [NIB, 128, 16, 128], I8, kind="ExternalInput"))
    wup = ap(nc.dram_tensor("wup", [NIB, 128, 16, 128], I8, kind="ExternalInput"))
    wdn = ap(nc.dram_tensor("wdn", [4, 128, 64, 512], I8, kind="ExternalInput"))
    sgt = ap(nc.dram_tensor("sgt", [128, NIB], F32, kind="ExternalInput"))
    sup = ap(nc.dram_tensor("sup", [128, NIB], F32, kind="ExternalInput"))
    sdn = ap(nc.dram_tensor("sdn", [1, HID], F32, kind="ExternalInput"))
    qoff = ap(nc.dram_tensor("qoff", [1, 1], mybir.dt.uint32, kind="ExternalInput"))
    out_q = ap(nc.dram_tensor("out_q", [S, HID], F32, kind="ExternalOutput"))

    AGR = HID + 256
    agin = ap(nc.dram_tensor("agin", [AGR, S], BF16, kind="Internal"))
    agout = ap(nc.dram_tensor("agout", [4 * AGR, S], BF16, kind="Internal"))
    fuse_scr = ap(nc.dram_tensor("fuse_scr", [INTER, S], BF16, kind="Internal"))
    pr_scr = [ap(nc.dram_tensor(f"pr{i}_scr", [HPC, 128, T], BF16, kind="Internal")) for i in range(4)]
    zg_scr = ap(nc.dram_tensor("zg_scr", [T, 512], F32, kind="Internal"))
    sg_scr = ap(nc.dram_tensor("sg_scr", [HPC, 128, T], BF16, kind="Internal"))
    agin2 = ap(nc.dram_tensor("agin2", [HID, S], BF16, kind="Internal"))
    agout2 = ap(nc.dram_tensor("agout2", [4 * HID, S], BF16, kind="Internal"))
    osel = ap(nc.dram_tensor("osel", [HID, S], BF16, kind="Internal"))
    ffn_scr = ap(nc.dram_tensor("ffn_scr", [S, HID], F32, kind="Internal"))
    if taps:
        tp_xnT = ap(nc.dram_tensor("tp_xnT", [4 * AGR, S], BF16, kind="ExternalOutput"))
        tp_fuse = ap(nc.dram_tensor("tp_fuse", [INTER, S], BF16, kind="ExternalOutput"))
        tp_pr = [ap(nc.dram_tensor(f"tp_pr{i}", [HPC, 128, T], BF16, kind="ExternalOutput")) for i in range(4)]
        tp_zg = ap(nc.dram_tensor("tp_zg", [T, 512], F32, kind="ExternalOutput"))
        tp_sg = ap(nc.dram_tensor("tp_sg", [HPC, 128, T], BF16, kind="ExternalOutput"))
        tp_o = ap(nc.dram_tensor("tp_o", [HID, S], BF16, kind="ExternalOutput"))
        tp_ffn = ap(nc.dram_tensor("tp_ffn", [S, HID], F32, kind="ExternalOutput"))

    rg = [[0, 1, 2, 3], [4, 5, 6, 7]]

    from contextlib import ExitStack
    ctx = ExitStack()
    cur_stack = [ctx]
    sb = lambda name, shape, dt: cur_stack[0].enter_context(nc.sbuf_tensor(name, shape, dt))
    ps = lambda name, shape, dt=F32: cur_stack[0].enter_context(nc.psum_tensor(name, shape, dt))

    s_identb = sb("s_identb", [128, 128], BF16)
    s_identf = sb("s_identf", [128, 128], F32)
    s_mcx = sb("s_mcx", [128, 132], F32)
    s_ones1f = sb("s_ones1f", [1, 128], F32)
    s_ones1b = sb("s_ones1b", [1, 128], BF16)
    s_onescf = sb("s_onescf", [128, 1], F32)
    s_onescb = sb("s_onescb", [128, 1], BF16)
    s_sel4 = sb("s_sel4", [4, 512], F32)
    s_cw = sb("s_cw", [128, 64], F32)
    s_sgt = sb("s_sgt", [128, NIB], F32)
    s_sup = sb("s_sup", [128, NIB], F32)
    s_sdnr = sb("s_sdnr", [1, HID], F32)
    s_normw = sb("s_normw", [128, 1], F32)
    s_bg2h = sb("s_bg2h", [128, 4], F32)
    CVALS = {"ln_eps": 1e-5, "l2_eps": 1e-6, "rms_eps": 1e-5, "one": 1.0,
             "neg_shift": -15.0, "clamp_l": 80.0, "neg_clamp_l": -80.0}
    s_const = sb("s_const", [128, len(CVALS)], F32)
    CIDX = {k: i for i, k in enumerate(CVALS)}
    ccol = lambda k: s_const[:, CIDX[k]:CIDX[k] + 1]

    tE = Sem(nc, "tE")
    vE = Sem(nc, "vE")
    sE = Sem(nc, "sE")
    gE = Sem(nc, "gE")
    cS = Sem(nc, "cS")
    rI = Ring(nc, "rI", 1)
    rWf = Ring(nc, "rWf", 1)
    rX = Ring(nc, "rX", 2)
    rAg = Ring(nc, "rAg", 1)
    rA2w = Ring(nc, "rA2w", 2)
    rFu = Ring(nc, "rFu", 2)
    rB1w = Ring(nc, "rB1w", 1)
    rBx = Ring(nc, "rBx", 2)
    rBst = Ring(nc, "rBst", 2)
    rOut = Ring(nc, "rOut", 1)

    def vop(vector, fn, *a, **k):
        vector.wait_ge(vE.h, vE.n)
        return vE.inc(fn(*a, **k))

    def sop(scalar, fn, *a, **k):
        scalar.wait_ge(sE.h, sE.n)
        return sE.inc(fn(*a, **k))

    def gop(gpsimd, fn, *a, **k):
        gpsimd.wait_ge(gE.h, gE.n)
        return gE.inc(fn(*a, **k))

    blk = ctx.enter_context(nc.Block())

    def full_barrier():
        for mk in ("tensor", "vector", "scalar", "gpsimd", "sync"):
            @getattr(blk, mk)
            def _(eng):
                for s in ALL_SEMS:
                    if s.n:
                        eng.wait_ge(s.h, s.n)

    def open_phase():
        st = ExitStack()
        cur_stack[0] = st
        return st

    scope_stack = []
    def scope(name):
        if scope_stack:
            nc.leave_named_scope(scope_stack[-1][0], scope_stack[-1][1], False)
            scope_stack.pop()
        sid, _ = nc.enter_named_scope(name, False)
        scope_stack.append((name, sid))

    scope("pA")
    @blk.sync
    def _(sync):
        for d, s_ in [(cw.rearrange("c p h t -> c (p h t)"), s_cw), (sgt, s_sgt), (sup, s_sup),
                      (sdn, s_sdnr), (normw, s_normw), (bg2, s_bg2h)]:
            rI.sem(0).inc(sync.dma_start(out=s_[:], in_=d), 16)

    @blk.gpsimd
    def _(gpsimd):
        gop(gpsimd, gpsimd.memset, s_identb[:], 1.0)
        gop(gpsimd, gpsimd.affine_select, out=s_identb[:], in_=s_identb[:], pattern=[[-1, 128]],
            compare_op=ALU.is_ge, fill=0.0, base=0, channel_multiplier=1)
        gop(gpsimd, gpsimd.affine_select, out=s_identb[:], in_=s_identb[:], pattern=[[1, 128]],
            compare_op=ALU.is_ge, fill=0.0, base=0, channel_multiplier=-1)
        gop(gpsimd, gpsimd.memset, s_identf[:], 1.0)
        gop(gpsimd, gpsimd.affine_select, out=s_identf[:], in_=s_identf[:], pattern=[[-1, 128]],
            compare_op=ALU.is_ge, fill=0.0, base=0, channel_multiplier=1)
        gop(gpsimd, gpsimd.affine_select, out=s_identf[:], in_=s_identf[:], pattern=[[1, 128]],
            compare_op=ALU.is_ge, fill=0.0, base=0, channel_multiplier=-1)
        gop(gpsimd, gpsimd.memset, s_mcx[:, 0:128], 1.0)
        gop(gpsimd, gpsimd.affine_select, out=s_mcx[:, 0:128], in_=s_mcx[:, 0:128], pattern=[[1, 128]],
            compare_op=ALU.is_ge, fill=0.0, base=0, channel_multiplier=-1)
        gop(gpsimd, gpsimd.memset, s_mcx[0:MID, 128:129], 1.0)
        gop(gpsimd, gpsimd.memset, s_mcx[MID:128, 128:129], 0.0)
        gop(gpsimd, gpsimd.memset, s_mcx[0:128, 129:130], 0.0)
        gop(gpsimd, gpsimd.memset, s_mcx[64:128, 129:130], 0.0)
        gop(gpsimd, gpsimd.memset, s_ones1f[:], 1.0)
        gop(gpsimd, gpsimd.memset, s_ones1b[:], 1.0)
        gop(gpsimd, gpsimd.memset, s_onescf[:], 1.0)
        gop(gpsimd, gpsimd.memset, s_onescb[:], 1.0)
        for k, v in CVALS.items():
            gop(gpsimd, gpsimd.memset, ccol(k), v)
        gop(gpsimd, gpsimd.memset, s_sel4[:], 1.0)
        sel3 = s_sel4.rearrange("k (h j) -> k h j", j=128)
        gop(gpsimd, gpsimd.affine_select, out=sel3, in_=sel3, pattern=[[1, 4], [0, 128]],
            compare_op=ALU.is_ge, fill=0.0, base=0, channel_multiplier=-1)
        gop(gpsimd, gpsimd.affine_select, out=sel3, in_=sel3, pattern=[[-1, 4], [0, 128]],
            compare_op=ALU.is_ge, fill=0.0, base=0, channel_multiplier=1)

    @blk.vector
    def _(vector):
        rI.sem(0).wait(vector)
        gE.wait(vector)
        vop(vector, vector.tensor_scalar, out=s_mcx[:, 0:128], in0=s_mcx[:, 0:128],
            scalar1=s_mcx[:, 128:129], scalar2=None, op0=ALU.subtract)
        vop(vector, vector.tensor_scalar_mul, s_bg2h[:], s_bg2h[:], 0.5)
    init_v_mark = vE.n

    stA = open_phase()
    s_x = [sb(f"s_x{i}", [128, HID], F32) for i in range(2)]
    s_xnt_all = sb("s_xnt_all", [128, 16, S], BF16)
    s_f1t = sb("s_f1t", [128, S], BF16)
    s_g1t = sb("s_g1t", [128, S], BF16)
    s_xn = sb("s_xn", [128, HID], BF16)
    s_mcol = sb("s_mcol", [128, 1], F32)
    s_vcol = sb("s_vcol", [128, 1], F32)
    s_rstd = sb("s_rstd", [128, 1], F32)
    s_wf1 = sb("s_wf1", [128, 16, 128], BF16)
    s_wg1 = sb("s_wg1", [128, 16, 128], BF16)
    p_tr = ps("p_tr", [128, 4, 128], BF16)
    p_f1 = ps("p_f1", [128, 128])
    p_g1 = ps("p_g1", [128, 128])

    @blk.gpsimd
    def _(gpsimd):
        rWf.sem(0).inc(gpsimd.dma_start(out=s_wf1[:, :, :], in_=wf1), 16)
        rWf.sem(0).inc(gpsimd.dma_start(out=s_wg1[:, :, :], in_=wg1), 16)

    apply_marks = []
    tr_marks = []
    for it in range(NT1):
        buf = s_x[it % 2]
        xnb = s_xn
        @blk.sync
        def _(sync):
            if it >= 2:
                sync.wait_ge(vE.h, apply_marks[it - 2])
            rX.sem(it).inc(sync.dma_start(out=buf[:], in_=x_q[it * 128:(it + 1) * 128, :]), 16)

        @blk.scalar
        def _(scalar):
            rX.sem(it).wait(scalar)
            if it >= 1:
                scalar.wait_ge(tE.h, tr_marks[it - 1])
            sop(scalar, scalar.activation, out=s_xn[:], in_=buf[:], func=AF.Square,
                accum_out=s_vcol[:])

        @blk.vector
        def _(vector):
            rX.sem(it).wait(vector)
            vop(vector, vector.reduce_sum, s_mcol[:], buf[:], axis=AX.X)
            vop(vector, vector.tensor_scalar_mul, s_mcol[:], s_mcol[:], 1.0 / HID)
            sE.wait(vector)
            vop(vector, vector.tensor_scalar_mul, s_vcol[:], s_vcol[:], 1.0 / HID)
            vop(vector, vector.scalar_tensor_tensor, out=s_vcol[:], in0=s_mcol[:],
                scalar=s_mcol[:, 0:1], in1=s_vcol[:], op0=ALU.mult, op1=ALU.subtract)

        @blk.scalar
        def _(scalar):
            vE.wait(scalar)
            sop(scalar, scalar.activation, out=s_rstd[:], in_=s_vcol[:], func=AF.Ln,
                scale=-1.0, bias=ccol("ln_eps"))
            sop(scalar, scalar.activation, out=s_rstd[:], in_=s_rstd[:], func=AF.Exp, scale=-0.5)

        @blk.vector
        def _(vector):
            sE.wait(vector)
            if it >= 1:
                vector.wait_ge(tE.h, tr_marks[it - 1])
            vop(vector, vector.tensor_scalar, out=xnb[:], in0=buf[:], scalar1=s_mcol[:, 0:1],
                scalar2=s_rstd[:, 0:1], op0=ALU.subtract, op1=ALU.mult)
        apply_marks.append(vE.n)

        for grp in range(4):
            @blk.tensor
            def _(tensor, grp=grp):
                tensor.wait_ge(vE.h, apply_marks[-1])
                if tE.n > 0:
                    tensor.wait_ge(sE.h, sE.n)
                for j in range(4):
                    dc = grp * 4 + j
                    tE.inc(tensor.transpose(p_tr[:, j, :], xnb[:, dc * 128:(dc + 1) * 128], s_identb[:]))
            tE.n += 0
            @blk.scalar
            def _(scalar, grp=grp):
                tE.wait(scalar)
                sop(scalar, scalar.copy,
                    out=s_xnt_all[:, grp * 4:(grp + 1) * 4, it * 128:(it + 1) * 128], in_=p_tr[:])
        tr_marks.append(tE.n)

        @blk.tensor
        def _(tensor):
            tensor.wait_ge(sE.h, sE.n)
            rWf.sem(0).wait(tensor)
            for dc in range(16):
                xt = s_xnt_all[:, dc, it * 128:(it + 1) * 128]
                mm = tensor.matmul(p_f1[:], s_wf1[:, dc, :], xt, start=(dc == 0), stop=(dc == 15))
            tE.inc(mm)
            for dc in range(16):
                xt = s_xnt_all[:, dc, it * 128:(it + 1) * 128]
                mm = tensor.matmul(p_g1[:], s_wg1[:, dc, :], xt, start=(dc == 0), stop=(dc == 15))
            tE.inc(mm)
        @blk.scalar
        def _(scalar):
            tE.wait(scalar)
            sop(scalar, scalar.copy, out=s_f1t[:, it * 128:(it + 1) * 128], in_=p_f1[:])
            sop(scalar, scalar.copy, out=s_g1t[:, it * 128:(it + 1) * 128], in_=p_g1[:])

    @blk.sync
    def _(sync):
        sync.wait_ge(sE.h, sE.n)
        for dc in range(16):
            rAg.sem(0).inc(sync.dma_start(out=agin[dc * 128:(dc + 1) * 128, :], in_=s_xnt_all[:, dc, :]), 16)
        rAg.sem(0).inc(sync.dma_start(out=agin[HID:HID + 128, :], in_=s_f1t[:]), 16)
        rAg.sem(0).inc(sync.dma_start(out=agin[HID + 128:HID + 256, :], in_=s_g1t[:]), 16)

    @blk.gpsimd
    def _(gpsimd):
        rAg.sem(0).wait(gpsimd)
        gpsimd.collective_compute("AllGather", ALU.bypass, replica_groups=rg,
                                  ins=[agin.opt()], outs=[agout.opt()]).then_inc(cS.h, 1)
    cS.n += 1

    scope("pA2")
    s_wgi = [sb(f"s_wgi{i}", [128, 16, 128], BF16) for i in range(2)]
    s_wui = [sb(f"s_wui{i}", [128, 16, 128], BF16) for i in range(2)]
    s_gtb = [sb(f"s_gtb{i}", [128, 512], BF16) for i in range(2)]
    s_upb = [sb(f"s_upb{i}", [128, 512], BF16) for i in range(2)]
    s_fu = [sb(f"s_fu{i}", [128, 512], BF16) for i in range(2)]
    p_gt = [ps(f"p_gt{i}", [128, 512]) for i in range(2)]
    p_up = [ps(f"p_up{i}", [128, 512]) for i in range(2)]

    fuse_dma_marks = []
    a2_mm_marks = []
    a2_evac_marks = []
    a2_tt_marks = []
    for ib in range(NIB):
        wgb, wub = s_wgi[ib % 2], s_wui[ib % 2]
        @blk.gpsimd
        def _(gpsimd):
            if ib >= 2:
                gpsimd.wait_ge(tE.h, a2_mm_marks[ib - 2])
            rA2w.sem(ib).inc(gpsimd.dma_start(out=wgb[:, :, :], in_=wgt[ib]), 16)
            rA2w.sem(ib).inc(gpsimd.dma_start(out=wub[:, :, :], in_=wup[ib]), 16)
        fub = s_fu[ib % 2]
        for t5 in range(NT5Q):
            u = (ib * NT5Q + t5)
            pg, pu = p_gt[u % 2], p_up[u % 2]
            @blk.tensor
            def _(tensor, t5=t5, pg=pg, pu=pu):
                rA2w.sem(ib).wait(tensor)
                if u >= 2:
                    tensor.wait_ge(sE.h, a2_evac_marks[u - 2])
                for dc in range(16):
                    xt = s_xnt_all[:, dc, t5 * 512:(t5 + 1) * 512]
                    mm = tensor.matmul(pg[:], wgb[:, dc, :], xt, start=(dc == 0), stop=(dc == 15))
                tE.inc(mm)
                for dc in range(16):
                    xt = s_xnt_all[:, dc, t5 * 512:(t5 + 1) * 512]
                    mm = tensor.matmul(pu[:], wub[:, dc, :], xt, start=(dc == 0), stop=(dc == 15))
                tE.inc(mm)
            @blk.scalar
            def _(scalar, t5=t5, pg=pg, pu=pu):
                tE.wait(scalar)
                if u >= 2:
                    scalar.wait_ge(vE.h, a2_tt_marks[u - 2])
                sop(scalar, scalar.activation, out=s_gtb[u % 2][:], in_=pg[:], func=AF.Silu,
                    scale=s_sgt[:, ib:ib + 1])
                sop(scalar, scalar.activation, out=s_upb[u % 2][:], in_=pu[:], func=AF.Copy,
                    scale=s_sup[:, ib:ib + 1])
                a2_evac_marks.append(sE.n)
            fub = s_fu[t5 % 2]
            @blk.vector
            def _(vector, t5=t5, fub=fub):
                sE.wait(vector)
                if ib > 0 or t5 >= 2:
                    rFu.sem(t5).wait(vector)
                vop(vector, vector.tensor_tensor, out=fub[:],
                    in0=s_gtb[u % 2][:], in1=s_upb[u % 2][:], op=ALU.mult)
                a2_tt_marks.append(vE.n)
            @blk.sync
            def _(sync, t5=t5, fub=fub):
                sync.wait_ge(vE.h, vE.n)
                rFu.sem(t5).inc(sync.dma_start(
                    out=fuse_scr[ib * 128:(ib + 1) * 128, t5 * 512:(t5 + 1) * 512], in_=fub[:]), 16)
        a2_mm_marks.append(tE.n)

    full_barrier()
    stA.close()

    scope("pB1")
    stB1 = open_phase()
    s_wp = [sb(f"s_wp{p}", [128, 16, 512], BF16) for p in range(4)]
    s_wf2 = sb("s_wf2", [128, 512], BF16)
    s_wg2 = sb("s_wg2", [128, 512], BF16)
    s_wbs = sb("s_wbs", [128, 16, 4], BF16)
    s_xt = [sb(f"s_xt{i}", [128, 16, 512], BF16) for i in range(2)]
    s_ft = [sb(f"s_ft{i}", [128, 512], BF16) for i in range(2)]
    s_gt1 = [sb(f"s_gt1{i}", [128, 512], BF16) for i in range(2)]
    s_xh = [sb(f"s_xh{p}", [128, 4, 515], BF16) for p in range(4)]
    s_yc = [sb(f"s_yc{i}", [128, 4, 512], BF16) for i in range(2)]
    s_out4 = [[sb(f"s_out4{p}_{i}", [128, 4, 512], BF16) for i in range(2)] for p in range(4)]
    s_wsq = sb("s_wsq", [128, 4, 512], BF16)
    s_beta = [sb(f"s_beta{i}", [4, 512], F32) for i in range(2)]
    s_rsr = sb("s_rsr", [1, 2048], BF16)
    s_rsb = sb("s_rsb", [128, 512], F32)
    s_wbt = sb("s_wbt", [128, 4, 512], BF16)
    s_zgt = sb("s_zgt", [128, 4, 512], F32)
    s_sgt2 = sb("s_sgt2", [128, 4, 512], BF16)
    p_pr = [ps(f"p_pr{i}", [128, 512]) for i in range(4)]
    p_msq = ps("p_msq", [1, 512])
    p_bc1 = ps("p_bc1", [128, 512])
    p_bc2 = ps("p_bc2", [128, 512])
    p_zg = ps("p_zg", [128, 512])

    @blk.gpsimd
    def _(gpsimd):
        for p in range(4):
            rB1w.sem(0).inc(gpsimd.dma_start(out=s_wp[p][:, :, :], in_=wproj[p]), 16)
        rB1w.sem(0).inc(gpsimd.dma_start(out=s_wf2[:], in_=wf2), 16)
        rB1w.sem(0).inc(gpsimd.dma_start(out=s_wg2[:], in_=wg2), 16)
        rB1w.sem(0).inc(gpsimd.dma_start(out=s_wbs[:, :, :], in_=wbm), 16)

    @blk.sync
    def _(sync):
        cS.wait(sync)

    prev_evac_marks = {}
    b1_evac_marks = []
    b1_silu_marks = []
    beta_tanh_marks = {}

    def b1_projs(t5):
        rk = (t5 * 512) // S
        toff = (t5 * 512) % S
        xt = s_xt[t5 % 2]
        ft = s_ft[t5 % 2]
        gt1 = s_gt1[t5 % 2]
        @blk.sync
        def _(sync):
            if t5 >= 2:
                sync.wait_ge(tE.h, prev_evac_marks.get(("mmdone", t5 - 2), 0))
            base = rk * AGR
            for dcg in range(4):
                rBx.sem(t5).inc(sync.dma_start(out=xt[:, dcg * 4:(dcg + 1) * 4, :],
                               in_=agout[base + dcg * 512:base + (dcg + 1) * 512,
                                         toff:toff + 512].rearrange("(dc p) t -> p dc t", p=128)), 16)
            rBx.sem(t5).inc(sync.dma_start(out=ft[:], in_=agout[base + HID:base + HID + 128, toff:toff + 512]), 16)
            rBx.sem(t5).inc(sync.dma_start(out=gt1[:], in_=agout[base + HID + 128:base + HID + 256, toff:toff + 512]), 16)

        for p in range(4):
            @blk.tensor
            def _(tensor, p=p):
                rBx.sem(t5).wait(tensor)
                if p == 0:
                    rB1w.sem(0).wait(tensor)
                if b1_evac_marks:
                    tensor.wait_ge(vE.h, b1_evac_marks[-1])
                for cb in range(4):
                    for dc in range(16):
                        mm = tensor.matmul(p_pr[cb][:], s_wp[p][:, dc, cb * 128:(cb + 1) * 128],
                                      xt[:, dc, :], start=(dc == 0), stop=(dc == 15))
                    tE.inc(mm)
                if p == 3:
                    tensor.wait_ge(vE.h, vE.n)
                    for dc in range(16):
                        mm = tensor.matmul(p_bc2[0:4, :], s_wbs[:, dc, :], xt[:, dc, :],
                                      start=(dc == 0), stop=(dc == 15))
                    tE.inc(mm)
            mm_mark = tE.n

            xh = s_xh[p]
            @blk.gpsimd
            def _(gpsimd, p=p):
                if t5 == 0:
                    gop(gpsimd, gpsimd.memset, xh[:, :, 0:3], 0.0)
            @blk.vector
            def _(vector, p=p):
                if t5 == 0:
                    gE.wait(vector)
                vector.wait_ge(tE.h, mm_mark)
                for cb in range(4):
                    vop(vector, vector.tensor_copy, out=xh[:, cb, 3:515], in_=p_pr[cb][:])
                b1_evac_marks.append(vE.n)
                if len(b1_silu_marks) >= 2:
                    vector.wait_ge(sE.h, b1_silu_marks[-2])
                for h in range(4):
                    cwc = lambda tap: s_cw[:, p * 16 + h * 4 + tap:p * 16 + h * 4 + tap + 1]
                    yc = s_yc[p % 2]
                    vop(vector, vector.tensor_scalar, out=yc[:, h, :], in0=xh[:, h, 3:515],
                        scalar1=cwc(3), scalar2=None, op0=ALU.mult)
                    for j in (1, 2, 3):
                        vop(vector, vector.scalar_tensor_tensor, out=yc[:, h, :],
                            in0=xh[:, h, 3 - j:515 - j], scalar=cwc(3 - j),
                            in1=yc[:, h, :], op0=ALU.mult, op1=ALU.add)
                vop(vector, vector.tensor_copy, out=xh[:, :, 0:3], in_=xh[:, :, 512:515])
            @blk.scalar
            def _(scalar, p=p):
                vE.wait(scalar)
                if t5 >= 2:
                    rBst.sem(t5).wait(scalar)
                sop(scalar, scalar.activation, out=s_out4[p][t5 % 2][:, :, :],
                    in_=s_yc[p % 2][:, :, :], func=AF.Silu)
                b1_silu_marks.append(sE.n)
                if p == 3:
                    sop(scalar, scalar.activation, out=s_beta[t5 % 2][:], in_=p_bc2[0:4, :],
                        func=AF.Tanh, scale=0.5)
                    beta_tanh_marks[t5] = sE.n
        @blk.vector
        def _(vector):
            sE.wait(vector)
            vop(vector, vector.tensor_scalar, out=s_beta[t5 % 2][:], in0=s_beta[t5 % 2][:],
                scalar1=0.5, scalar2=0.5, op0=ALU.mult, op1=ALU.add)
        prev_evac_marks[("beta_v", t5)] = vE.n

    def b1_tail(t5):
        par = t5 % 2
        ft = s_ft[par]
        gt1 = s_gt1[par]
        out4w = s_out4[3][par]
        @blk.scalar
        def _(scalar):
            if t5 > 0:
                rBst.sem(t5 - 1).wait(scalar)
            sop(scalar, scalar.square, s_wsq[:], out4w[:])
        for h in range(4):
            @blk.tensor
            def _(tensor, h=h):
                tensor.wait_ge(sE.h, sE.n)
                tE.inc(tensor.matmul(p_msq[:], s_onescb[:], s_wsq[:, h, :], start=True, stop=True))
            @blk.scalar
            def _(scalar, h=h):
                tE.wait(scalar)
                sop(scalar, scalar.activation, out=s_rsr[:, h * 512:(h + 1) * 512], in_=p_msq[:],
                    func=AF.Ln, bias=ccol("l2_eps")[0:1, :])
                sop(scalar, scalar.activation, out=s_rsr[:, h * 512:(h + 1) * 512],
                    in_=s_rsr[:, h * 512:(h + 1) * 512], func=AF.Exp, scale=-0.5)
        for h in range(4):
            @blk.tensor
            def _(tensor, h=h):
                tensor.wait_ge(sE.h, sE.n)
                tensor.wait_ge(vE.h, vE.n)
                tensor.wait_ge(vE.h, prev_evac_marks[("beta_v", t5)])
                if h == 0 and (t5 + 1) in beta_tanh_marks:
                    tensor.wait_ge(sE.h, beta_tanh_marks[t5 + 1])
                tE.inc(tensor.matmul(p_bc1[:], s_ones1b[:], s_rsr[:, h * 512:(h + 1) * 512],
                              start=True, stop=True))
                tE.inc(tensor.matmul(p_bc2[:], s_sel4[:, h * 128:(h + 1) * 128], s_beta[par][:],
                              start=True, stop=True))
            @blk.vector
            def _(vector, h=h):
                tE.wait(vector)
                vop(vector, vector.tensor_tensor, out=s_wbt[:, h, :], in0=out4w[:, h, :],
                    in1=p_bc1[:], op=ALU.mult)
                vop(vector, vector.tensor_tensor, out=s_wbt[:, h, :], in0=s_wbt[:, h, :],
                    in1=p_bc2[:], op=ALU.mult)
        for tt in range(4):
            @blk.tensor
            def _(tensor, tt=tt):
                tensor.wait_ge(sE.h, sE.n)
                tE.inc(tensor.matmul(p_zg[:], ft[:, tt * 128:(tt + 1) * 128], s_wf2[:],
                              start=True, stop=True))
            @blk.scalar
            def _(scalar, tt=tt):
                tE.wait(scalar)
                sop(scalar, scalar.copy, out=s_zgt[:, tt, :], in_=p_zg[:])
        for cb in range(4):
            @blk.tensor
            def _(tensor, cb=cb):
                tensor.wait_ge(sE.h, sE.n)
                tensor.wait_ge(vE.h, vE.n)
                tE.inc(tensor.matmul(p_bc1[:], s_wg2[:, cb * 128:(cb + 1) * 128], gt1[:],
                              start=True, stop=True))
            @blk.scalar
            def _(scalar, cb=cb):
                tE.wait(scalar)
                sop(scalar, scalar.activation, out=s_sgt2[:, cb, :], in_=p_bc1[:], func=AF.Tanh,
                    scale=0.5, bias=s_bg2h[:, cb:cb + 1])
        @blk.vector
        def _(vector):
            sE.wait(vector)
            vop(vector, vector.tensor_scalar, out=s_sgt2[:, :, :], in0=s_sgt2[:, :, :], scalar1=0.5,
                scalar2=0.5, op0=ALU.mult, op1=ALU.add)
        prev_evac_marks[("mmdone", t5)] = tE.n
        @blk.sync
        def _(sync):
            sync.wait_ge(vE.h, vE.n)
            sync.wait_ge(sE.h, sE.n)
            span = slice(t5 * 512, (t5 + 1) * 512)
            for p in range(3):
                for h in range(4):
                    rBst.sem(t5).inc(sync.dma_start(out=pr_scr[p][h, :, span], in_=s_out4[p][par][:, h, :]), 16)
            for h in range(4):
                rBst.sem(t5).inc(sync.dma_start(out=pr_scr[3][h, :, span], in_=s_wbt[:, h, :]), 16)
                rBst.sem(t5).inc(sync.dma_start(out=sg_scr[h, :, span], in_=s_sgt2[:, h, :]), 16)
            rBst.sem(t5).inc(sync.dma_start(out=zg_scr[span, :].rearrange("(tt p) m -> p tt m", p=128),
                           in_=s_zgt[:]), 16)

    for t5 in range(NT5B):
        b1_projs(t5)
        if t5 >= 1:
            b1_tail(t5 - 1)
    b1_tail(NT5B - 1)

    if taps:
        full_barrier()
        @blk.sync
        def _(sync):
            rOut.sem(0).inc(sync.dma_start(out=tp_xnT, in_=agout), 16)
            rOut.sem(0).inc(sync.dma_start(out=tp_fuse, in_=fuse_scr), 16)
            for p in range(4):
                rOut.sem(0).inc(sync.dma_start(out=tp_pr[p], in_=pr_scr[p]), 16)
            rOut.sem(0).inc(sync.dma_start(out=tp_zg, in_=zg_scr), 16)
            rOut.sem(0).inc(sync.dma_start(out=tp_sg, in_=sg_scr), 16)

    full_barrier()
    stB1.close()
    cur_stack[0] = ctx

    scope("pB2")
    stB2 = open_phase()
    rCh = Ring(nc, "rCh", 3)
    rO2 = Ring(nc, "rO2", 2)
    s_ones4 = sb("s_ones4", [4, 128], F32)
    s_qf = [sb(f"s_qf{i}", [128, 4, 128], BF16) for i in range(3)]
    s_kf = [sb(f"s_kf{i}", [128, 4, 128], BF16) for i in range(3)]
    s_wf = [sb(f"s_wbf{i}", [128, 4, 128], BF16) for i in range(3)]
    s_kt = [sb(f"s_kt{i}", [128, 4, 128], BF16) for i in range(3)]
    s_vt = [sb(f"s_vt{i}", [128, 4, 128], BF16) for i in range(3)]
    s_wt = [sb(f"s_wt{i}", [128, 4, 128], BF16) for i in range(3)]
    s_zg2 = [sb(f"s_zg2{i}", [128, 512], F32) for i in range(3)]
    s_sg2 = [sb(f"s_sg2{i}", [128, 4, 128], BF16) for i in range(3)]
    s_St = sb("s_St", [128, 4, 256], F32)
    s_Stb = sb("s_Stb", [128, 4, 256], BF16)
    s_SkT = sb("s_SkT", [128, 4, 128], BF16)
    s_sp = [sb(f"s_sp{i}", [128, 512], F32) for i in range(2)]
    s_Ltok = [sb(f"s_Ltok{i}", [128, 512], F32) for i in range(2)]
    s_Lf = [sb(f"s_Lf{i}", [128, 4, 128], F32) for i in range(2)]
    s_Ep = [sb(f"s_Ep{i}", [128, 512], F32) for i in range(2)]
    s_Emt = [sb(f"s_Emt{i}", [128, 512], F32) for i in range(2)]
    s_Emf = [sb(f"s_Emf{i}", [128, 4, 128], F32) for i in range(2)]
    s_Ptok = [sb(f"s_Ptok{i}", [128, 512], BF16) for i in range(2)]
    s_Pf = [sb(f"s_Pf{i}", [128, 4, 128], BF16) for i in range(2)]
    s_Am = [sb(f"s_Am{i}", [128, 4, 128], BF16) for i in range(2)]
    s_Bm = sb("s_Bm", [128, 4, 128], BF16)
    s_t1 = sb("s_t1", [128, 512], F32)
    s_scl = sb("s_scl", [128, 512], F32)
    s_expo = sb("s_expo", [128, 512], F32)
    s_dn = sb("s_dn", [128, 4, 1], F32)
    s_rdn = sb("s_rdn", [128, 4, 1], F32)
    s_t2 = sb("s_t2", [128, 512], F32)
    s_eopf = sb("s_eopf", [128, 512], F32)
    s_eop = sb("s_eop", [128, 4, 128], BF16)
    s_ehat = sb("s_ehat", [128, 4, 128], BF16)
    s_rc = sb("s_rc", [128, 4, 1], F32)
    s_rexp = sb("s_rexp", [128, 4, 1], F32)
    s_tail = sb("s_tail", [128, 4, 1], F32)
    s_osq = sb("s_osq", [128, 512], BF16)
    s_r2 = sb("s_r2", [1, 512], F32)
    s_rsb2 = sb("s_rsb2", [128, 512], F32)
    s_t3 = sb("s_t3", [128, 512], F32)
    s_ofin = [sb(f"s_ofin{i}", [128, 4, 128], BF16) for i in range(2)]
    p_misc = ps("p_misc", [128, 512])
    p_lf = ps("p_lf", [128, 4, 128])
    p_ab = [ps(f"p_ab{i}", [128, 4, 128]) for i in range(2)]
    p_so = ps("p_so", [128, 512])
    p_add = ps("p_add", [128, 512])
    p_et = ps("p_et", [128, 4, 128], BF16)

    rDw = Ring(nc, "rDw", 1)
    rDf = Ring(nc, "rDf", 2)
    rDst = Ring(nc, "rDst", 2)
    s_sdnb = sb("s_sdnb", [128, HID], F32)
    s_wdnp = sb("s_wdnp", [128, 64, 512], BF16)
    s_fuT = [sb(f"s_fuT{i}", [128, 64, 128], BF16) for i in range(2)]
    s_do = [sb(f"s_do{i}", [128, 512], F32) for i in range(2)]
    p_dn = ps("p_dn", [128, 512])

    @blk.gpsimd
    def _(gpsimd):
        gop(gpsimd, gpsimd.memset, s_ones4[:], 1.0)
        gop(gpsimd, gpsimd.memset, s_St[:], 0.0)
        gop(gpsimd, gpsimd.memset, s_Stb[:], 0.0)
        gop(gpsimd, gpsimd.memset, s_tail[:], 0.0)

    for dp in range(4):
        @blk.tensor
        def _(tensor, dp=dp):
            if dp == 0:
                rI.sem(0).wait(tensor)
            else:
                tensor.wait_ge(sE.h, sE.n)
            tE.inc(tensor.matmul(p_dn[:], s_ones1f[:], s_sdnr[:, dp * 512:(dp + 1) * 512],
                                 start=True, stop=True))
        @blk.scalar
        def _(scalar, dp=dp):
            tE.wait(scalar)
            sop(scalar, scalar.copy, out=s_sdnb[:, dp * 512:(dp + 1) * 512], in_=p_dn[:])
    sdnb_mark = sE.n

    dn_marks = {}
    dn_evac_marks = {}

    def dn_wload(dp):
        @blk.gpsimd
        def _(gpsimd):
            if dp > 0:
                gpsimd.wait_ge(tE.h, dn_marks[(dp - 1, NT1 - 1)])
            rDw.sem(0).inc(gpsimd.dma_start(out=s_wdnp[:, :, :], in_=wdn[dp]), 16)

    def dn_loads(u):
        if u >= 4 * NT1:
            return
        dp, it = u // NT1, u % NT1
        @blk.sync
        def _(sync):
            if u >= 2:
                sync.wait_ge(tE.h, dn_marks.get((dp, it - 2) if it >= 2 else (dp - 1, NT1 - 2 + it), 0))
            rDf.sem(u).inc(sync.dma_start(
                out=s_fuT[u % 2][:, :, :],
                in_=fuse_scr[:, it * 128:(it + 1) * 128].rearrange("(gi p) t -> p gi t", p=128)), 16)

    def dn_begin(u):
        if u >= 4 * NT1:
            return None
        return {"u": u, "dp": u // NT1, "it": u % NT1, "fslot": s_fuT[u % 2], "ic": 0}

    def dn_fill(st, n):
        if st is None or st["ic"] >= 64:
            return
        u, ic0 = st["u"], st["ic"]
        n = min(n, 64 - ic0)
        @blk.tensor
        def _(tensor):
            if ic0 == 0:
                rDw.sem(0).wait(tensor)
                rDf.sem(u).wait(tensor)
                if u == 0:
                    tensor.wait_ge(sE.h, sdnb_mark)
                if u >= 1:
                    tensor.wait_ge(vE.h, dn_evac_marks[u - 1])
            for ic in range(ic0, ic0 + n):
                mm = tensor.matmul(p_dn[:], st["fslot"][:, ic, :], s_wdnp[:, ic, :],
                                   start=(ic == 0), stop=(ic == 63))
            tE.inc(mm)
        st["ic"] = ic0 + n

    def dn_finish(st):
        if st is None:
            return
        dn_fill(st, 64 - st["ic"])
        u, dp, it = st["u"], st["dp"], st["it"]
        dn_marks[(dp, it)] = tE.n
        if it == NT1 - 1 and dp < 3:
            dn_wload(dp + 1)
        @blk.vector
        def _(vector):
            vector.wait_ge(tE.h, dn_marks[(dp, it)])
            vector.wait_ge(sE.h, sdnb_mark)
            if u >= 2:
                rDst.sem(u).wait(vector)
            vop(vector, vector.tensor_tensor, out=s_do[u % 2][:], in0=p_dn[:],
                in1=s_sdnb[:, dp * 512:(dp + 1) * 512], op=ALU.mult)
        dn_evac_marks[u] = vE.n
        @blk.sync
        def _(sync):
            sync.wait_ge(vE.h, dn_evac_marks[u])
            rDst.sem(u).inc(sync.dma_start(
                out=ffn_scr[it * 128:(it + 1) * 128, dp * 512:(dp + 1) * 512], in_=s_do[u % 2][:]), 16)

    dn_wload(0)
    dn_loads(0)
    dn_loads(1)
    chunk_done = {}
    clamp_marks = {}
    rc_marks = {}
    epi_marks = {}
    t1_marks = {}
    sp_marks = {}
    lcmm_marks = {}

    def b2_prefix(ch):
        par = ch % 2
        sl4 = ch % 3
        span = slice(ch * 128, (ch + 1) * 128)
        qf, kf, wf = s_qf[sl4], s_kf[sl4], s_wf[sl4]
        kt, vt, wt = s_kt[sl4], s_vt[sl4], s_wt[sl4]
        zg2, sg2 = s_zg2[sl4], s_sg2[sl4]

        @blk.sync
        def _(sync):
            if ch >= 3:
                d = chunk_done[ch - 3]
                sync.wait_ge(tE.h, d["t"]); sync.wait_ge(vE.h, d["v"]); sync.wait_ge(sE.h, d["s"])
            rCh.sem(ch).inc(sync.dma_start(out=qf[:, :, :], in_=pr_scr[0][:, :, span].rearrange("h c t -> c h t")), 16)
            rCh.sem(ch).inc(sync.dma_start(out=kf[:, :, :], in_=pr_scr[1][:, :, span].rearrange("h c t -> c h t")), 16)
            rCh.sem(ch).inc(sync.dma_start(out=wf[:, :, :], in_=pr_scr[3][:, :, span].rearrange("h c t -> c h t")), 16)
            rCh.sem(ch).inc(sync.dma_start(out=sg2[:, :, :], in_=sg_scr[:, :, span].rearrange("h c t -> c h t")), 16)
            rCh.sem(ch).inc(sync.dma_start(out=zg2[:], in_=zg_scr[span, :]), 16)
            rCh.sem(ch).inc(sync.dma_start(out=kt[:, :, :].rearrange("t h c -> t (h c)"),
                                           in_=pr_scr[1][:, :, span].rearrange("h c t -> (h c) t"),
                                           transpose=True), 16)
            rCh.sem(ch).inc(sync.dma_start(out=vt[:, :, :].rearrange("t h c -> t (h c)"),
                                           in_=pr_scr[2][:, :, span].rearrange("h c t -> (h c) t"),
                                           transpose=True), 16)
            rCh.sem(ch).inc(sync.dma_start(out=wt[:, :, :].rearrange("t h c -> t (h c)"),
                                           in_=pr_scr[3][:, :, span].rearrange("h c t -> (h c) t"),
                                           transpose=True), 16)

        @blk.scalar
        def _(scalar):
            rCh.sem(ch).wait(scalar)
            sop(scalar, scalar.activation, out=s_sp[par][:], in_=zg2[:], func=AF.Exp, scale=-1.0)
            sop(scalar, scalar.activation, out=s_sp[par][:], in_=s_sp[par][:], func=AF.Ln, bias=ccol("one"))
        sp_marks[ch] = sE.n

        @blk.tensor
        def _(tensor):
            tensor.wait_ge(sE.h, sp_marks[ch])
            if ch >= 1:
                tensor.wait_ge(vE.h, clamp_marks[ch - 1])
            if ch == 0:
                gE.wait(tensor)
            tE.inc(tensor.matmul(p_misc[:], s_mcx[:, 0:128], s_sp[par][:], start=True, stop=True))
            for h in range(4):
                tE.inc(tensor.matmul(p_lf[:, h, :], s_sp[par][:, h * 128:(h + 1) * 128],
                                     s_mcx[:, 0:128], start=True, stop=True))
        lmm_mark = tE.n

        @blk.vector
        def _(vector):
            vector.wait_ge(tE.h, lmm_mark)
            vop(vector, vector.tensor_scalar, out=s_Ltok[par][:], in0=p_misc[:], scalar1=80.0,
                scalar2=-80.0, op0=ALU.min, op1=ALU.max)
            vop(vector, vector.tensor_scalar, out=s_Lf[par][:, :, :], in0=p_lf[:, :, :], scalar1=80.0,
                scalar2=-80.0, op0=ALU.min, op1=ALU.max)
        clamp_marks[ch] = vE.n

        @blk.tensor
        def _(tensor):
            rCh.sem(ch).wait(tensor)
            if ch >= 2:
                tensor.wait_ge(vE.h, epi_marks[ch - 2])
            for h in range(4):
                tE.inc(tensor.matmul(p_ab[par][:, h, :], kf[:, h, :], qf[:, h, :], start=True, stop=True))
        amm_mark = tE.n
        @blk.vector
        def _(vector):
            vector.wait_ge(tE.h, amm_mark)
            vop(vector, vector.tensor_copy, out=s_Am[par][:, :, :], in_=p_ab[par][:, :, :])
        amcp_mark = vE.n
        @blk.gpsimd
        def _(gpsimd):
            gpsimd.wait_ge(vE.h, amcp_mark)
            gop(gpsimd, gpsimd.affine_select, out=s_Am[par][:, :, :], in_=s_Am[par][:, :, :],
                pattern=[[0, 4], [1, 128]], compare_op=ALU.is_ge, fill=0.0, base=0, channel_multiplier=-1)
        return {}

    def b2_suffix(ch, pm):
        par = ch % 2
        sl4 = ch % 3
        span = slice(ch * 128, (ch + 1) * 128)
        qf = s_qf[sl4]
        kt, vt = s_kt[sl4], s_vt[sl4]
        sg2 = s_sg2[sl4]
        ofin = s_ofin[par]

        s_base = sE.n
        rexp_mark = s_base + 1
        eexp_mark = s_base + 4
        expo_mark = s_base + 5
        eopf_mark = s_base + 6
        osq_mark = s_base + 7
        r2_mark = s_base + 9

        dnst = dn_begin(ch)
        dn_fill(dnst, 6)
        @blk.tensor
        def _(tensor):
            if ch >= 1:
                tensor.wait_ge(vE.h, t1_marks[ch - 1])
            for h in range(4):
                tE.inc(tensor.matmul(p_so[:, h:h + 1], s_sp[par][:, h * 128:(h + 1) * 128],
                                     s_mcx[:, 128:129], start=True, stop=True))
        lcmm_marks[ch] = tE.n
        @blk.vector
        def _(vector):
            vector.wait_ge(tE.h, lcmm_marks[ch])
            vop(vector, vector.tensor_tensor, out=s_rc[:, :, :],
                in0=bass.AP(p_so, 0, [[512, 128], [1, 4], [1, 1]]),
                in1=s_tail[:, :, :], op=ALU.add)
        rc_marks[ch] = vE.n
        @blk.vector
        def _(vector):
            vector.wait_ge(sE.h, rexp_mark)
            if ch > 0:
                for h in range(4):
                    vop(vector, vector.tensor_scalar, out=s_St[:, h, :], in0=s_St[:, h, :],
                        scalar1=s_rexp[:, h, 0:1].opt(), scalar2=None, op0=ALU.mult)
            vop(vector, vector.tensor_copy, out=s_Stb[:, :, :], in_=s_St[:, :, :])
        resc_mark = vE.n
        dn_fill(dnst, 6)
        @blk.tensor
        def _(tensor):
            tensor.wait_ge(vE.h, resc_mark)
            if ch >= 1:
                tensor.wait_ge(vE.h, ehat_marks[ch - 1])
            for h in range(4):
                tE.inc(tensor.transpose(p_et[:, h, :], s_Stb[:, h, 0:128], s_identb[:]))
        sktr_mark = tE.n
        @blk.vector
        def _(vector):
            vector.wait_ge(tE.h, sktr_mark)
            vop(vector, vector.tensor_copy, out=s_SkT[:, :, :], in_=p_et[:, :, :])
        skt_mark = vE.n
        wt, wf = s_wt[sl4], s_wf[sl4]
        @blk.vector
        def _(vector):
            vector.wait_ge(sE.h, eexp_mark)
            vop(vector, vector.tensor_tensor, out=s_Ptok[par][:],
                in0=wt[:, :, :].rearrange("s h m -> s (h m)"), in1=s_Emt[par][:], op=ALU.mult)
            vop(vector, vector.tensor_tensor, out=s_Pf[par][:, :, :], in0=wf[:, :, :],
                in1=s_Emf[par][:, :, :], op=ALU.mult)
        p_mark = vE.n

        dn_fill(dnst, 8)
        @blk.tensor
        def _(tensor):
            tensor.wait_ge(vE.h, skt_mark)
            tensor.wait_ge(gE.h, gE.n)
            tensor.wait_ge(vE.h, p_mark)
            for h in range(4):
                hs = slice(h * 128, (h + 1) * 128)
                tensor.matmul(p_so[:, hs], qf[:, h, :], s_SkT[:, h, :], start=True, stop=False)
                tE.inc(tensor.matmul(p_so[:, hs], s_Am[par][:, h, :], s_Ptok[par][:, hs], start=False, stop=True))
        smm_mark = tE.n

        @blk.vector
        def _(vector):
            vector.wait_ge(tE.h, smm_mark)
            vop(vector, vector.tensor_tensor, out=s_t1[:], in0=p_so[:], in1=s_Ep[par][:], op=ALU.mult)
            vop(vector, vector.tensor_scalar, out=s_scl[:], in0=s_t1[:], scalar1=SCALE,
                scalar2=70.0, op0=ALU.mult, op1=ALU.min)
        t1_marks[ch] = vE.n
        @blk.vector
        def _(vector):
            vector.wait_ge(sE.h, expo_mark)
            vop(vector, vector.tensor_reduce, out=s_dn[:, :, :],
                in_=s_expo[:].rearrange("t (h m) -> t h m", m=128), axis=AX.X, op=ALU.add)
            vop(vector, vector.reciprocal, out=s_rdn[:, :, :], in_=s_dn[:, :, :])
            vop(vector, vector.tensor_tensor, out=s_t2[:], in0=s_scl[:], in1=s_Ltok[par][:], op=ALU.subtract)
            vop(vector, vector.tensor_scalar, out=s_t2[:], in0=s_t2[:], scalar1=85.0, scalar2=None,
                op0=ALU.min)
        t2_mark = vE.n
        @blk.vector
        def _(vector):
            vector.wait_ge(sE.h, eopf_mark)
            for h in range(4):
                hs = slice(h * 128, (h + 1) * 128)
                vop(vector, vector.tensor_scalar, out=s_eop[:, h, :], in0=s_eopf[:, hs],
                    scalar1=s_rdn[:, h, 0:1].opt(), scalar2=None, op0=ALU.mult)
        eop_mark = vE.n
        dn_fill(dnst, 8)
        @blk.tensor
        def _(tensor):
            tensor.wait_ge(vE.h, eop_mark)
            tensor.wait_ge(vE.h, skt_mark)
            for h in range(4):
                tE.inc(tensor.transpose(p_et[:, h, :], s_eop[:, h, :], s_identb[:]))
        etr_mark = tE.n
        @blk.vector
        def _(vector):
            vector.wait_ge(tE.h, etr_mark)
            vop(vector, vector.tensor_copy, out=s_ehat[:, :, :], in_=p_et[:, :, :])
        ehat_marks[ch] = vE.n

        dn_fill(dnst, 6)
        @blk.tensor
        def _(tensor):
            tensor.wait_ge(vE.h, ehat_marks[ch])
            tensor.wait_ge(vE.h, p_mark)
            for h in range(4):
                tE.inc(tensor.matmul(p_ab[par][:, h, :], s_Pf[par][:, h, :], s_ehat[:, h, :],
                                     start=True, stop=True))
        bmm_mark = tE.n
        @blk.vector
        def _(vector):
            vector.wait_ge(tE.h, bmm_mark)
            vop(vector, vector.tensor_copy, out=s_Bm[:, :, :], in_=p_ab[par][:, :, :])
        bmcp_mark = vE.n
        @blk.gpsimd
        def _(gpsimd):
            gpsimd.wait_ge(vE.h, bmcp_mark)
            gop(gpsimd, gpsimd.affine_select, out=s_Bm[:, :, :], in_=s_Bm[:, :, :],
                pattern=[[0, 4], [1, 128]], compare_op=ALU.is_ge, fill=0.0, base=0, channel_multiplier=-1)
        dn_fill(dnst, 8)
        @blk.tensor
        def _(tensor):
            tensor.wait_ge(gE.h, gE.n)
            for h in range(4):
                tensor.matmul(p_ab[par][:, h, :], s_Stb[:, h, 128:256], s_ehat[:, h, :],
                              start=True, stop=False)
                tE.inc(tensor.matmul(p_ab[par][:, h, :], vt[:, h, :], s_Bm[:, h, :],
                                     start=False, stop=True))
        omm_mark = tE.n
        dn_fill(dnst, 8)
        @blk.tensor
        def _(tensor):
            tensor.wait_ge(sE.h, osq_mark)
            if ch >= 1:
                tensor.wait_ge(vE.h, svadd_marks[ch - 1])
            tE.inc(tensor.matmul(p_add[0:1, 0:512], s_onescb[:], s_osq[:], start=True, stop=True))
        msq_mark = tE.n
        dn_fill(dnst, 6)
        @blk.tensor
        def _(tensor):
            tensor.wait_ge(sE.h, r2_mark)
            tE.inc(tensor.matmul(p_add[:], s_ones1f[:], s_r2[:], start=True, stop=True))
        rsb_mark = tE.n
        @blk.vector
        def _(vector):
            vector.wait_ge(tE.h, rsb_mark)
            if ch >= 2:
                rO2.sem(ch).wait(vector)
            vop(vector, vector.tensor_copy, out=s_rsb2[:], in_=p_add[:])
            vop(vector, vector.tensor_tensor, out=s_t3[:],
                in0=p_ab[par][:, :, :].rearrange("e h t -> e (h t)"), in1=s_rsb2[:], op=ALU.mult)
            vop(vector, vector.tensor_tensor, out=ofin[:, :, :].rearrange("e h t -> e (h t)"),
                in0=s_t3[:], in1=sg2[:, :, :].rearrange("e h t -> e (h t)"), op=ALU.mult)
        epi_marks[ch] = vE.n

        dn_fill(dnst, 4)
        @blk.tensor
        def _(tensor):
            tensor.wait_ge(vE.h, epi_marks[ch])
            for h in range(4):
                hs = slice(h * 128, (h + 1) * 128)
                tE.inc(tensor.matmul(p_add[:, hs], s_Ptok[par][:, hs], kt[:, h, :],
                                     start=True, stop=True))
        kadd_mark = tE.n
        @blk.vector
        def _(vector):
            vector.wait_ge(tE.h, kadd_mark)
            vop(vector, vector.tensor_tensor, out=s_St[:, :, 0:128], in0=s_St[:, :, 0:128],
                in1=p_add[:].rearrange("m (h d) -> m h d", d=128), op=ALU.add)
        skadd_mark = vE.n
        dn_fill(dnst, 4)
        @blk.tensor
        def _(tensor):
            tensor.wait_ge(vE.h, skadd_mark)
            for h in range(4):
                hs = slice(h * 128, (h + 1) * 128)
                tE.inc(tensor.matmul(p_add[:, hs], s_Ptok[par][:, hs], vt[:, h, :],
                                     start=True, stop=True))
        vadd_mark = tE.n
        @blk.vector
        def _(vector):
            vector.wait_ge(tE.h, vadd_mark)
            vop(vector, vector.tensor_tensor, out=s_St[:, :, 128:256], in0=s_St[:, :, 128:256],
                in1=p_add[:].rearrange("m (h e) -> m h e", e=128), op=ALU.add)
            vop(vector, vector.tensor_copy, out=s_tail[:, :, :], in_=s_Lf[par][:, :, 127:128])
        svadd_marks[ch] = vE.n

        @blk.scalar
        def _(scalar):
            scalar.wait_ge(vE.h, rc_marks[ch])
            sop(scalar, scalar.activation, out=s_rexp[:, :, :], in_=s_rc[:, :, :], func=AF.Exp, scale=-1.0)
            scalar.wait_ge(vE.h, clamp_marks[ch])
            sop(scalar, scalar.activation, out=s_Ep[par][:], in_=s_Ltok[par][:], func=AF.Exp, scale=-1.0)
            sop(scalar, scalar.activation, out=s_Emt[par][:], in_=s_Ltok[par][:], func=AF.Exp)
            sop(scalar, scalar.activation, out=s_Emf[par][:, :, :], in_=s_Lf[par][:, :, :], func=AF.Exp)
            scalar.wait_ge(vE.h, t1_marks[ch])
            sop(scalar, scalar.activation, out=s_expo[:], in_=s_scl[:], func=AF.Exp, bias=ccol("neg_shift"))
            scalar.wait_ge(vE.h, t2_mark)
            sop(scalar, scalar.activation, out=s_eopf[:], in_=s_t2[:], func=AF.Exp, bias=ccol("neg_shift"))
            scalar.wait_ge(tE.h, omm_mark)
            sop(scalar, scalar.square, s_osq[:], p_ab[par][:, :, :].rearrange("e h t -> e (h t)"))
            scalar.wait_ge(tE.h, msq_mark)
            sop(scalar, scalar.activation, out=s_r2[:], in_=p_add[0:1, 0:512], func=AF.Ln,
                scale=1.0 / 128.0, bias=ccol("rms_eps")[0:1, :])
            sop(scalar, scalar.activation, out=s_r2[:], in_=s_r2[:], func=AF.Exp, scale=-0.5)
        assert sE.n == s_base + 9, (sE.n, s_base)

        jq = (ch * 128) // S
        cspan = slice((ch * 128) % S, (ch * 128) % S + 128)
        @blk.sync
        def _(sync):
            sync.wait_ge(vE.h, epi_marks[ch])
            rO2.sem(ch).inc(sync.dma_start(
                out=agin2[jq * 512:(jq + 1) * 512, cspan].rearrange("(h c) t -> c h t", c=128),
                in_=ofin[:, :, :]), 16)
        dn_finish(dnst)
        dn_loads(ch + 2)
        chunk_done[ch] = {"t": tE.n, "v": vE.n, "s": sE.n}

    ehat_marks = {}
    svadd_marks = {}
    pm_store = {}
    for ch in range(NCH):
        pm_store[ch] = b2_prefix(ch)
        if ch >= 1:
            b2_suffix(ch - 1, pm_store[ch - 1])
    b2_suffix(NCH - 1, pm_store[NCH - 1])

    @blk.gpsimd
    def _(gpsimd):
        for ch in range(max(0, NCH - 2), NCH):
            rO2.sem(ch).wait(gpsimd)
        gpsimd.collective_compute("AllGather", ALU.bypass, replica_groups=rg,
                                  ins=[agin2.opt()], outs=[agout2.opt()]).then_inc(cS.h, 1)
    cS.n += 1

    full_barrier()
    stB2.close()

    scope("pC2")
    stC2 = open_phase()
    rCw = Ring(nc, "rCw", 1)
    rCq = Ring(nc, "rCq", 1)
    rCo = Ring(nc, "rCo", 2)
    rCx = Ring(nc, "rCx", 2)
    rCf = Ring(nc, "rCf", 2)
    rOq = Ring(nc, "rOq", 2)
    s_qoff = sb("s_qoff", [1, 1], mybir.dt.uint32)
    s_wos = sb("s_wos", [128, 16, 2048], BF16)
    s_ot = [sb(f"s_ot{i}", [128, 16, 128], BF16) for i in range(2)]
    s_xc = [sb(f"s_xc{i}", [128, HID], F32) for i in range(2)]
    s_fc = [sb(f"s_fc{i}", [128, HID], F32) for i in range(2)]
    s_oq = [sb(f"s_oq{i}", [128, HID], F32) for i in range(2)]
    p_c2 = [ps(f"p_c2{i}", [128, 512]) for i in range(2)]

    @blk.gpsimd
    def _(gpsimd):
        rCw.sem(0).inc(gpsimd.dma_start(out=s_wos[:, :, :], in_=wo), 16)
    @blk.sync
    def _(sync):
        rCq.sem(0).inc(sync.dma_start(out=s_qoff[:], in_=qoff), 16)
    rSel = Ring(nc, "rSel", 1)
    @blk.gpsimd
    def _(gpsimd):
        from concourse.expressions import RuntimeValue
        rCq.sem(0).wait(gpsimd)
        cS.wait(gpsimd)
        rq = gpsimd.alloc_register("qreg")
        gpsimd.reg_load(rq, s_qoff[0:1, 0:1])
        gpsimd.reg_mul(rq, rq, 512 * S)
        for g in range(4):
            dyn = bass.AP(agout2.tensor, RuntimeValue(rq), [[S, 512], [1, S]])
            rSel.sem(0).inc(gpsimd.dma_start(out=osel[g * 512:(g + 1) * 512, :], in_=dyn), 16)
            if g < 3:
                gpsimd.reg_add(rq, rq, HID * S)
    @blk.vector
    def _(vector):
        rCw.sem(0).wait(vector)
        rI.sem(0).wait(vector)
        vop(vector, vector.tensor_scalar, out=s_wos[:, :, :], in0=s_wos[:, :, :],
            scalar1=s_normw[:, 0:1], scalar2=None, op0=ALU.mult)
    wos_mark = vE.n

    c2_marks = {}
    for it in range(NT1):
        ot, xc, fc, oq = s_ot[it % 2], s_xc[it % 2], s_fc[it % 2], s_oq[it % 2]
        @blk.sync
        def _(sync):
            if it >= 2:
                sync.wait_ge(tE.h, c2_marks[it - 2])
            rSel.sem(0).wait(sync)
            for g in range(4):
                rCo.sem(it).inc(sync.dma_start(
                    out=ot[:, g * 4:(g + 1) * 4, :],
                    in_=osel[g * 512:(g + 1) * 512,
                             it * 128:(it + 1) * 128].rearrange("(dc p) t -> p dc t", p=128)), 16)
            rCx.sem(it).inc(sync.dma_start(out=xc[:], in_=x_q[it * 128:(it + 1) * 128, :]), 16)
            rCf.sem(it).inc(sync.dma_start(out=fc[:], in_=ffn_scr[it * 128:(it + 1) * 128, :]), 16)
        for dp in range(4):
            @blk.tensor
            def _(tensor, dp=dp):
                rCo.sem(it).wait(tensor)
                if it == 0 and dp == 0:
                    tensor.wait_ge(vE.h, wos_mark)
                if it > 0 or dp > 1:
                    tensor.wait_ge(vE.h, vE.n)
                for dc in range(16):
                    mm = tensor.matmul(p_c2[dp % 2][:], ot[:, dc, :],
                                       s_wos[:, dc, dp * 512:(dp + 1) * 512],
                                       start=(dc == 0), stop=(dc == 15))
                tE.inc(mm)
            @blk.vector
            def _(vector, dp=dp):
                tE.wait(vector)
                rCx.sem(it).wait(vector)
                rCf.sem(it).wait(vector)
                if it >= 2 and dp == 0:
                    rOq.sem(it).wait(vector)
                dsl = slice(dp * 512, (dp + 1) * 512)
                vop(vector, vector.tensor_tensor, out=oq[:, dsl], in0=p_c2[dp % 2][:],
                    in1=fc[:, dsl], op=ALU.add)
                vop(vector, vector.tensor_tensor, out=oq[:, dsl], in0=oq[:, dsl],
                    in1=xc[:, dsl], op=ALU.add)
        c2_marks[it] = tE.n
        @blk.sync
        def _(sync):
            sync.wait_ge(vE.h, vE.n)
            rOq.sem(it).inc(sync.dma_start(out=out_q[it * 128:(it + 1) * 128, :], in_=oq[:]), 16)

    if taps:
        full_barrier()
        @blk.sync
        def _(sync):
            rOut.sem(0).inc(sync.dma_start(out=tp_o, in_=agin2), 16)
            rOut.sem(0).inc(sync.dma_start(out=tp_ffn, in_=ffn_scr), 16)

    full_barrier()
    @blk.sync
    def _(sync):
        for it in range(NT1):
            rOq.sem(it).wait(sync)
        rOut.sem(0).wait(sync)
    stC2.close()

    if scope_stack:
        nc.leave_named_scope(scope_stack[-1][0], scope_stack[-1][1], False)
        scope_stack.pop()
    ctx.close()
    return nc


def shard_inputs(inputs, T=16384):
    S = T // 4
    f32 = np.float32
    x = np.ascontiguousarray(np.asarray(inputs["L0_attn_hidden_states"], f32).reshape(2, -1, HID)[:, :T])
    NIB = INTER // 128
    cq, ck, cv = (np.asarray(inputs[f"L0_attn_c{n}"], f32) for n in "qkv")

    def rt16(w):
        C = w.shape[1]
        return np.ascontiguousarray(w.reshape(16, 128, C).transpose(1, 0, 2))

    NIBv = INTER // 128
    wgt_r = np.ascontiguousarray(np.asarray(inputs["L0_ffn_w_gate"])
                                 .reshape(16, 128, NIBv, 128).transpose(2, 1, 0, 3))
    wup_r = np.ascontiguousarray(np.asarray(inputs["L0_ffn_w_up"])
                                 .reshape(16, 128, NIBv, 128).transpose(2, 1, 0, 3))
    wdn_r = np.ascontiguousarray(np.asarray(inputs["L0_ffn_w_down"])
                                 .reshape(64, 128, 4, 512).transpose(2, 1, 0, 3))
    maps = []
    for c in range(8):
        b, hg = c // 4, c % 4
        sl = slice(hg * 512, (hg + 1) * 512)
        cwa = np.stack([cq[sl], ck[sl], cv[sl], cv[sl]])
        cwa = cwa.reshape(4, 4, 128, 4).transpose(2, 0, 1, 3)
        m = {
            "x_q": np.ascontiguousarray(x[b, hg * S:(hg + 1) * S]),
            "wq": rt16(np.asarray(inputs["L0_attn_Wq"], f32)[:, sl]),
            "wk": rt16(np.asarray(inputs["L0_attn_Wk"], f32)[:, sl]),
            "wv": rt16(np.asarray(inputs["L0_attn_Wv"], f32)[:, sl]),
            "ww": rt16(np.asarray(inputs["L0_attn_Ww"], f32)[:, sl]),
            "cw": np.ascontiguousarray(cwa, f32),
            "wf1": rt16(np.asarray(inputs["L0_attn_Wf1"], f32)),
            "wf2": np.ascontiguousarray(np.asarray(inputs["L0_attn_Wf2"], f32)[:, sl]),
            "wbm": rt16(np.asarray(inputs["L0_attn_Wb"], f32)[:, hg * 4:(hg + 1) * 4]),
            "wg1": rt16(np.asarray(inputs["L0_attn_Wg1"], f32)),
            "wg2": np.ascontiguousarray(np.asarray(inputs["L0_attn_Wg2"], f32)[:, sl]),
            "bg2": np.ascontiguousarray(np.asarray(inputs["L0_attn_bg2"], f32)[sl].reshape(4, 128).T),
            "normw": np.ascontiguousarray(np.asarray(inputs["L0_attn_norm_w"], f32).reshape(128, 1)),
            "wo": rt16(np.asarray(inputs["L0_attn_Wo"], f32)),
            "wgt": wgt_r, "wup": wup_r, "wdn": wdn_r,
            "sgt": np.ascontiguousarray(np.asarray(inputs["L0_ffn_s_gate"], f32).reshape(NIB, 128).T),
            "sup": np.ascontiguousarray(np.asarray(inputs["L0_ffn_s_up"], f32).reshape(NIB, 128).T),
            "sdn": np.ascontiguousarray(np.asarray(inputs["L0_ffn_s_down"], f32).reshape(1, HID)),
            "qoff": np.array([[hg]], np.uint32),
        }
        maps.append(m)
    return maps


def kernel(**inputs):
    T = 16384
    nc = build_program(T=T, taps=False)
    in_maps = shard_inputs(inputs, T)
    res = run_bass_kernel_spmd(nc, in_maps, core_ids=list(range(8)))
    S = T // 4
    out = np.zeros((2, T, HID), np.float32)
    for c in range(8):
        b, hg = c // 4, c % 4
        out[b, hg * S:(hg + 1) * S] = res.results[c]["out_q"]
    return out


# version 08
# speedup vs baseline: 1.2011x; 1.0369x over previous
import numpy as np
import ml_dtypes
import concourse.bass as bass
import concourse.mybir as mybir
from concourse.bass_utils import run_bass_kernel_spmd

F32 = mybir.dt.float32
BF16 = mybir.dt.bfloat16
I8 = mybir.dt.int8
AF = mybir.ActivationFunctionType
ALU = mybir.AluOpType
AX = mybir.AxisListType

HID = 2048
HPC = 4
INTER = 8192
CCH = 128
MID = 64
SCALE = 128.0 ** -0.5


ALL_SEMS = []


class Sem:
    def __init__(self, nc, name):
        self.h = nc.alloc_semaphore(name=name)
        self.n = 0
        ALL_SEMS.append(self)

    def inc(self, inst, k=1):
        inst.then_inc(self.h, k)
        self.n += k
        return inst

    def wait(self, eng, val=None):
        eng.wait_ge(self.h, self.n if val is None else val)


class Ring:
    def __init__(self, nc, name, slots):
        self.s = [Sem(nc, f"{name}{i}") for i in range(slots)]
        self.k = slots

    def sem(self, i):
        return self.s[i % self.k]


def build_program(T=16384, taps=False):
    ALL_SEMS.clear()
    S = T // 4
    NT1 = S // 128
    NT5Q = S // 512
    NT5B = T // 512
    NCH = T // CCH
    NIB = INTER // 128

    nc = bass.Bass("TRN2", target_bir_lowering=False)
    ap = lambda t: t.ap()
    x_q = ap(nc.dram_tensor("x_q", [S, HID], F32, kind="ExternalInput"))
    wproj = [ap(nc.dram_tensor(n, [128, 16, 512], F32, kind="ExternalInput")) for n in ("wq", "wk", "wv", "ww")]
    cw = ap(nc.dram_tensor("cw", [128, 4, 4, 4], F32, kind="ExternalInput"))
    wf1 = ap(nc.dram_tensor("wf1", [128, 16, 128], F32, kind="ExternalInput"))
    wf2 = ap(nc.dram_tensor("wf2", [128, 512], F32, kind="ExternalInput"))
    wbm = ap(nc.dram_tensor("wbm", [128, 16, 4], F32, kind="ExternalInput"))
    wg1 = ap(nc.dram_tensor("wg1", [128, 16, 128], F32, kind="ExternalInput"))
    wg2 = ap(nc.dram_tensor("wg2", [128, 512], F32, kind="ExternalInput"))
    bg2 = ap(nc.dram_tensor("bg2", [128, 4], F32, kind="ExternalInput"))
    normw = ap(nc.dram_tensor("normw", [128, 1], F32, kind="ExternalInput"))
    wo = ap(nc.dram_tensor("wo", [128, 16, HID], F32, kind="ExternalInput"))
    wgt = ap(nc.dram_tensor("wgt", [NIB, 128, 16, 128], I8, kind="ExternalInput"))
    wup = ap(nc.dram_tensor("wup", [NIB, 128, 16, 128], I8, kind="ExternalInput"))
    wdn = ap(nc.dram_tensor("wdn", [4, 128, 64, 512], I8, kind="ExternalInput"))
    sgt = ap(nc.dram_tensor("sgt", [128, NIB], F32, kind="ExternalInput"))
    sup = ap(nc.dram_tensor("sup", [128, NIB], F32, kind="ExternalInput"))
    sdn = ap(nc.dram_tensor("sdn", [1, HID], F32, kind="ExternalInput"))
    qoff = ap(nc.dram_tensor("qoff", [1, 1], mybir.dt.uint32, kind="ExternalInput"))
    out_q = ap(nc.dram_tensor("out_q", [S, HID], F32, kind="ExternalOutput"))

    AGR = HID + 256
    agin = ap(nc.dram_tensor("agin", [AGR, S], BF16, kind="Internal"))
    agout = ap(nc.dram_tensor("agout", [4 * AGR, S], BF16, kind="Internal"))
    fuse_scr = ap(nc.dram_tensor("fuse_scr", [INTER, S], BF16, kind="Internal"))
    pr_scr = [ap(nc.dram_tensor(f"pr{i}_scr", [HPC, 128, T], BF16, kind="Internal")) for i in range(4)]
    zg_scr = ap(nc.dram_tensor("zg_scr", [T, 512], F32, kind="Internal"))
    sg_scr = ap(nc.dram_tensor("sg_scr", [HPC, 128, T], BF16, kind="Internal"))
    agin2 = ap(nc.dram_tensor("agin2", [HID, S], BF16, kind="Internal"))
    agout2 = ap(nc.dram_tensor("agout2", [4 * HID, S], BF16, kind="Internal"))
    osel = ap(nc.dram_tensor("osel", [HID, S], BF16, kind="Internal"))
    ffn_scr = ap(nc.dram_tensor("ffn_scr", [S, HID], F32, kind="Internal"))
    if taps:
        tp_xnT = ap(nc.dram_tensor("tp_xnT", [4 * AGR, S], BF16, kind="ExternalOutput"))
        tp_fuse = ap(nc.dram_tensor("tp_fuse", [INTER, S], BF16, kind="ExternalOutput"))
        tp_pr = [ap(nc.dram_tensor(f"tp_pr{i}", [HPC, 128, T], BF16, kind="ExternalOutput")) for i in range(4)]
        tp_zg = ap(nc.dram_tensor("tp_zg", [T, 512], F32, kind="ExternalOutput"))
        tp_sg = ap(nc.dram_tensor("tp_sg", [HPC, 128, T], BF16, kind="ExternalOutput"))
        tp_o = ap(nc.dram_tensor("tp_o", [HID, S], BF16, kind="ExternalOutput"))
        tp_ffn = ap(nc.dram_tensor("tp_ffn", [S, HID], F32, kind="ExternalOutput"))

    rg = [[0, 1, 2, 3], [4, 5, 6, 7]]

    from contextlib import ExitStack
    ctx = ExitStack()
    cur_stack = [ctx]
    sb = lambda name, shape, dt: cur_stack[0].enter_context(nc.sbuf_tensor(name, shape, dt))
    ps = lambda name, shape, dt=F32: cur_stack[0].enter_context(nc.psum_tensor(name, shape, dt))

    s_identb = sb("s_identb", [128, 128], BF16)
    s_identf = sb("s_identf", [128, 128], F32)
    s_mcx = sb("s_mcx", [128, 132], F32)
    s_ones1f = sb("s_ones1f", [1, 128], F32)
    s_ones1b = sb("s_ones1b", [1, 128], BF16)
    s_onescf = sb("s_onescf", [128, 1], F32)
    s_onescb = sb("s_onescb", [128, 1], BF16)
    s_sel4 = sb("s_sel4", [4, 512], F32)
    s_cw = sb("s_cw", [128, 64], F32)
    s_sgt = sb("s_sgt", [128, NIB], F32)
    s_sup = sb("s_sup", [128, NIB], F32)
    s_sdnr = sb("s_sdnr", [1, HID], F32)
    s_normw = sb("s_normw", [128, 1], F32)
    s_bg2h = sb("s_bg2h", [128, 4], F32)
    CVALS = {"ln_eps": 1e-5, "l2_eps": 1e-6, "rms_eps": 1e-5, "one": 1.0,
             "neg_shift": -15.0, "clamp_l": 80.0, "neg_clamp_l": -80.0}
    s_const = sb("s_const", [128, len(CVALS)], F32)
    CIDX = {k: i for i, k in enumerate(CVALS)}
    ccol = lambda k: s_const[:, CIDX[k]:CIDX[k] + 1]

    tE = Sem(nc, "tE")
    vE = Sem(nc, "vE")
    sE = Sem(nc, "sE")
    gE = Sem(nc, "gE")
    cS = Sem(nc, "cS")
    rI = Ring(nc, "rI", 1)
    rWf = Ring(nc, "rWf", 1)
    rX = Ring(nc, "rX", 2)
    rAg = Ring(nc, "rAg", 1)
    rA2w = Ring(nc, "rA2w", 2)
    rFu = Ring(nc, "rFu", 2)
    rB1w = Ring(nc, "rB1w", 1)
    rBx = Ring(nc, "rBx", 2)
    rBst = Ring(nc, "rBst", 2)
    rOut = Ring(nc, "rOut", 1)

    def vop(vector, fn, *a, **k):
        vector.wait_ge(vE.h, vE.n)
        return vE.inc(fn(*a, **k))

    def sop(scalar, fn, *a, **k):
        scalar.wait_ge(sE.h, sE.n)
        return sE.inc(fn(*a, **k))

    def gop(gpsimd, fn, *a, **k):
        gpsimd.wait_ge(gE.h, gE.n)
        return gE.inc(fn(*a, **k))

    blk = ctx.enter_context(nc.Block())

    def full_barrier():
        for mk in ("tensor", "vector", "scalar", "gpsimd", "sync"):
            @getattr(blk, mk)
            def _(eng):
                for s in ALL_SEMS:
                    if s.n:
                        eng.wait_ge(s.h, s.n)

    def open_phase():
        st = ExitStack()
        cur_stack[0] = st
        return st

    scope_stack = []
    def scope(name):
        if scope_stack:
            nc.leave_named_scope(scope_stack[-1][0], scope_stack[-1][1], False)
            scope_stack.pop()
        sid, _ = nc.enter_named_scope(name, False)
        scope_stack.append((name, sid))

    scope("pA")
    @blk.sync
    def _(sync):
        for d, s_ in [(cw.rearrange("c p h t -> c (p h t)"), s_cw), (sgt, s_sgt), (sup, s_sup),
                      (sdn, s_sdnr), (normw, s_normw), (bg2, s_bg2h)]:
            rI.sem(0).inc(sync.dma_start(out=s_[:], in_=d), 16)

    @blk.gpsimd
    def _(gpsimd):
        gop(gpsimd, gpsimd.memset, s_identb[:], 1.0)
        gop(gpsimd, gpsimd.affine_select, out=s_identb[:], in_=s_identb[:], pattern=[[-1, 128]],
            compare_op=ALU.is_ge, fill=0.0, base=0, channel_multiplier=1)
        gop(gpsimd, gpsimd.affine_select, out=s_identb[:], in_=s_identb[:], pattern=[[1, 128]],
            compare_op=ALU.is_ge, fill=0.0, base=0, channel_multiplier=-1)
        gop(gpsimd, gpsimd.memset, s_identf[:], 1.0)
        gop(gpsimd, gpsimd.affine_select, out=s_identf[:], in_=s_identf[:], pattern=[[-1, 128]],
            compare_op=ALU.is_ge, fill=0.0, base=0, channel_multiplier=1)
        gop(gpsimd, gpsimd.affine_select, out=s_identf[:], in_=s_identf[:], pattern=[[1, 128]],
            compare_op=ALU.is_ge, fill=0.0, base=0, channel_multiplier=-1)
        gop(gpsimd, gpsimd.memset, s_mcx[:, 0:128], 1.0)
        gop(gpsimd, gpsimd.affine_select, out=s_mcx[:, 0:128], in_=s_mcx[:, 0:128], pattern=[[1, 128]],
            compare_op=ALU.is_ge, fill=0.0, base=0, channel_multiplier=-1)
        gop(gpsimd, gpsimd.memset, s_mcx[0:MID, 128:129], 1.0)
        gop(gpsimd, gpsimd.memset, s_mcx[MID:128, 128:129], 0.0)
        gop(gpsimd, gpsimd.memset, s_mcx[0:128, 129:130], 0.0)
        gop(gpsimd, gpsimd.memset, s_mcx[64:128, 129:130], 0.0)
        gop(gpsimd, gpsimd.memset, s_ones1f[:], 1.0)
        gop(gpsimd, gpsimd.memset, s_ones1b[:], 1.0)
        gop(gpsimd, gpsimd.memset, s_onescf[:], 1.0)
        gop(gpsimd, gpsimd.memset, s_onescb[:], 1.0)
        for k, v in CVALS.items():
            gop(gpsimd, gpsimd.memset, ccol(k), v)
        gop(gpsimd, gpsimd.memset, s_sel4[:], 1.0)
        sel3 = s_sel4.rearrange("k (h j) -> k h j", j=128)
        gop(gpsimd, gpsimd.affine_select, out=sel3, in_=sel3, pattern=[[1, 4], [0, 128]],
            compare_op=ALU.is_ge, fill=0.0, base=0, channel_multiplier=-1)
        gop(gpsimd, gpsimd.affine_select, out=sel3, in_=sel3, pattern=[[-1, 4], [0, 128]],
            compare_op=ALU.is_ge, fill=0.0, base=0, channel_multiplier=1)

    @blk.vector
    def _(vector):
        rI.sem(0).wait(vector)
        gE.wait(vector)
        vop(vector, vector.tensor_scalar, out=s_mcx[:, 0:128], in0=s_mcx[:, 0:128],
            scalar1=s_mcx[:, 128:129], scalar2=None, op0=ALU.subtract)
        vop(vector, vector.tensor_scalar_mul, s_bg2h[:], s_bg2h[:], 0.5)
    init_v_mark = vE.n

    stA = open_phase()
    s_x = [sb(f"s_x{i}", [128, HID], F32) for i in range(2)]
    s_xnt_all = sb("s_xnt_all", [128, 16, S], BF16)
    s_f1t = sb("s_f1t", [128, S], BF16)
    s_g1t = sb("s_g1t", [128, S], BF16)
    s_xn = sb("s_xn", [128, HID], BF16)
    s_mcol = sb("s_mcol", [128, 1], F32)
    s_vcol = sb("s_vcol", [128, 1], F32)
    s_rstd = sb("s_rstd", [128, 1], F32)
    s_wf1 = sb("s_wf1", [128, 16, 128], BF16)
    s_wg1 = sb("s_wg1", [128, 16, 128], BF16)
    p_tr = ps("p_tr", [128, 4, 128], BF16)
    p_f1 = ps("p_f1", [128, 128])
    p_g1 = ps("p_g1", [128, 128])

    @blk.gpsimd
    def _(gpsimd):
        rWf.sem(0).inc(gpsimd.dma_start(out=s_wf1[:, :, :], in_=wf1), 16)
        rWf.sem(0).inc(gpsimd.dma_start(out=s_wg1[:, :, :], in_=wg1), 16)

    apply_marks = []
    tr_marks = []
    for it in range(NT1):
        buf = s_x[it % 2]
        xnb = s_xn
        @blk.sync
        def _(sync):
            if it >= 2:
                sync.wait_ge(vE.h, apply_marks[it - 2])
            rX.sem(it).inc(sync.dma_start(out=buf[:], in_=x_q[it * 128:(it + 1) * 128, :]), 16)

        @blk.scalar
        def _(scalar):
            rX.sem(it).wait(scalar)
            if it >= 1:
                scalar.wait_ge(tE.h, tr_marks[it - 1])
            sop(scalar, scalar.activation, out=s_xn[:], in_=buf[:], func=AF.Square,
                accum_out=s_vcol[:])

        @blk.vector
        def _(vector):
            rX.sem(it).wait(vector)
            vop(vector, vector.reduce_sum, s_mcol[:], buf[:], axis=AX.X)
            vop(vector, vector.tensor_scalar_mul, s_mcol[:], s_mcol[:], 1.0 / HID)
            sE.wait(vector)
            vop(vector, vector.tensor_scalar_mul, s_vcol[:], s_vcol[:], 1.0 / HID)
            vop(vector, vector.scalar_tensor_tensor, out=s_vcol[:], in0=s_mcol[:],
                scalar=s_mcol[:, 0:1], in1=s_vcol[:], op0=ALU.mult, op1=ALU.subtract)

        @blk.scalar
        def _(scalar):
            vE.wait(scalar)
            sop(scalar, scalar.activation, out=s_rstd[:], in_=s_vcol[:], func=AF.Ln,
                scale=-1.0, bias=ccol("ln_eps"))
            sop(scalar, scalar.activation, out=s_rstd[:], in_=s_rstd[:], func=AF.Exp, scale=-0.5)

        @blk.vector
        def _(vector):
            sE.wait(vector)
            if it >= 1:
                vector.wait_ge(tE.h, tr_marks[it - 1])
            vop(vector, vector.tensor_scalar, out=xnb[:], in0=buf[:], scalar1=s_mcol[:, 0:1],
                scalar2=s_rstd[:, 0:1], op0=ALU.subtract, op1=ALU.mult)
        apply_marks.append(vE.n)

        for grp in range(4):
            @blk.tensor
            def _(tensor, grp=grp):
                tensor.wait_ge(vE.h, apply_marks[-1])
                if tE.n > 0:
                    tensor.wait_ge(sE.h, sE.n)
                for j in range(4):
                    dc = grp * 4 + j
                    tE.inc(tensor.transpose(p_tr[:, j, :], xnb[:, dc * 128:(dc + 1) * 128], s_identb[:]))
            tE.n += 0
            @blk.scalar
            def _(scalar, grp=grp):
                tE.wait(scalar)
                sop(scalar, scalar.copy,
                    out=s_xnt_all[:, grp * 4:(grp + 1) * 4, it * 128:(it + 1) * 128], in_=p_tr[:])
        tr_marks.append(tE.n)

        @blk.tensor
        def _(tensor):
            tensor.wait_ge(sE.h, sE.n)
            rWf.sem(0).wait(tensor)
            for dc in range(16):
                xt = s_xnt_all[:, dc, it * 128:(it + 1) * 128]
                mm = tensor.matmul(p_f1[:], s_wf1[:, dc, :], xt, start=(dc == 0), stop=(dc == 15))
            tE.inc(mm)
            for dc in range(16):
                xt = s_xnt_all[:, dc, it * 128:(it + 1) * 128]
                mm = tensor.matmul(p_g1[:], s_wg1[:, dc, :], xt, start=(dc == 0), stop=(dc == 15))
            tE.inc(mm)
        @blk.scalar
        def _(scalar):
            tE.wait(scalar)
            sop(scalar, scalar.copy, out=s_f1t[:, it * 128:(it + 1) * 128], in_=p_f1[:])
            sop(scalar, scalar.copy, out=s_g1t[:, it * 128:(it + 1) * 128], in_=p_g1[:])

    @blk.sync
    def _(sync):
        sync.wait_ge(sE.h, sE.n)
        for dc in range(16):
            rAg.sem(0).inc(sync.dma_start(out=agin[dc * 128:(dc + 1) * 128, :], in_=s_xnt_all[:, dc, :]), 16)
        rAg.sem(0).inc(sync.dma_start(out=agin[HID:HID + 128, :], in_=s_f1t[:]), 16)
        rAg.sem(0).inc(sync.dma_start(out=agin[HID + 128:HID + 256, :], in_=s_g1t[:]), 16)

    @blk.gpsimd
    def _(gpsimd):
        rAg.sem(0).wait(gpsimd)
        gpsimd.collective_compute("AllGather", ALU.bypass, replica_groups=rg,
                                  ins=[agin.opt()], outs=[agout.opt()]).then_inc(cS.h, 1)
    cS.n += 1

    scope("pA2")
    s_wgi = [sb(f"s_wgi{i}", [128, 16, 128], BF16) for i in range(2)]
    s_wui = [sb(f"s_wui{i}", [128, 16, 128], BF16) for i in range(2)]
    s_gtb = [sb(f"s_gtb{i}", [128, 512], BF16) for i in range(2)]
    s_upb = [sb(f"s_upb{i}", [128, 512], BF16) for i in range(2)]
    s_fu = [sb(f"s_fu{i}", [128, 512], BF16) for i in range(2)]
    p_gt = [ps(f"p_gt{i}", [128, 512]) for i in range(2)]
    p_up = [ps(f"p_up{i}", [128, 512]) for i in range(2)]

    fuse_dma_marks = []
    a2_mm_marks = []
    a2_evac_marks = []
    a2_tt_marks = []
    for ib in range(NIB):
        wgb, wub = s_wgi[ib % 2], s_wui[ib % 2]
        @blk.gpsimd
        def _(gpsimd):
            if ib >= 2:
                gpsimd.wait_ge(tE.h, a2_mm_marks[ib - 2])
            rA2w.sem(ib).inc(gpsimd.dma_start(out=wgb[:, :, :], in_=wgt[ib]), 16)
            rA2w.sem(ib).inc(gpsimd.dma_start(out=wub[:, :, :], in_=wup[ib]), 16)
        fub = s_fu[ib % 2]
        for t5 in range(NT5Q):
            u = (ib * NT5Q + t5)
            pg, pu = p_gt[u % 2], p_up[u % 2]
            @blk.tensor
            def _(tensor, t5=t5, pg=pg, pu=pu):
                rA2w.sem(ib).wait(tensor)
                if u >= 2:
                    tensor.wait_ge(sE.h, a2_evac_marks[u - 2])
                for dc in range(16):
                    xt = s_xnt_all[:, dc, t5 * 512:(t5 + 1) * 512]
                    mm = tensor.matmul(pg[:], wgb[:, dc, :], xt, start=(dc == 0), stop=(dc == 15))
                tE.inc(mm)
                for dc in range(16):
                    xt = s_xnt_all[:, dc, t5 * 512:(t5 + 1) * 512]
                    mm = tensor.matmul(pu[:], wub[:, dc, :], xt, start=(dc == 0), stop=(dc == 15))
                tE.inc(mm)
            @blk.scalar
            def _(scalar, t5=t5, pg=pg, pu=pu):
                tE.wait(scalar)
                if u >= 2:
                    scalar.wait_ge(vE.h, a2_tt_marks[u - 2])
                sop(scalar, scalar.activation, out=s_gtb[u % 2][:], in_=pg[:], func=AF.Silu,
                    scale=s_sgt[:, ib:ib + 1])
                sop(scalar, scalar.activation, out=s_upb[u % 2][:], in_=pu[:], func=AF.Copy,
                    scale=s_sup[:, ib:ib + 1])
                a2_evac_marks.append(sE.n)
            fub = s_fu[t5 % 2]
            @blk.vector
            def _(vector, t5=t5, fub=fub):
                sE.wait(vector)
                if ib > 0 or t5 >= 2:
                    rFu.sem(t5).wait(vector)
                vop(vector, vector.tensor_tensor, out=fub[:],
                    in0=s_gtb[u % 2][:], in1=s_upb[u % 2][:], op=ALU.mult)
                a2_tt_marks.append(vE.n)
            @blk.sync
            def _(sync, t5=t5, fub=fub):
                sync.wait_ge(vE.h, vE.n)
                rFu.sem(t5).inc(sync.dma_start(
                    out=fuse_scr[ib * 128:(ib + 1) * 128, t5 * 512:(t5 + 1) * 512], in_=fub[:]), 16)
        a2_mm_marks.append(tE.n)

    full_barrier()
    stA.close()

    scope("pB1")
    stB1 = open_phase()
    s_wp = [sb(f"s_wp{p}", [128, 16, 512], BF16) for p in range(4)]
    s_wf2 = sb("s_wf2", [128, 512], BF16)
    s_wg2 = sb("s_wg2", [128, 512], BF16)
    s_wbs = sb("s_wbs", [128, 16, 4], BF16)
    s_xt = [sb(f"s_xt{i}", [128, 16, 512], BF16) for i in range(2)]
    s_ft = [sb(f"s_ft{i}", [128, 512], BF16) for i in range(2)]
    s_gt1 = [sb(f"s_gt1{i}", [128, 512], BF16) for i in range(2)]
    s_xh = [sb(f"s_xh{p}", [128, 4, 515], BF16) for p in range(4)]
    s_yc = [sb(f"s_yc{i}", [128, 4, 512], BF16) for i in range(2)]
    s_out4 = [[sb(f"s_out4{p}_{i}", [128, 4, 512], BF16) for i in range(2)] for p in range(4)]
    s_wsq = sb("s_wsq", [128, 4, 512], BF16)
    s_beta = [sb(f"s_beta{i}", [4, 512], F32) for i in range(2)]
    s_rsr = sb("s_rsr", [1, 2048], BF16)
    s_rsb = sb("s_rsb", [128, 512], F32)
    s_wbt = sb("s_wbt", [128, 4, 512], BF16)
    s_zgt = sb("s_zgt", [128, 4, 512], F32)
    s_sgt2 = sb("s_sgt2", [128, 4, 512], BF16)
    p_pr = [ps(f"p_pr{i}", [128, 512]) for i in range(4)]
    p_msq = ps("p_msq", [1, 512])
    p_bc1 = ps("p_bc1", [128, 512])
    p_bc2 = ps("p_bc2", [128, 512])
    p_zg = ps("p_zg", [128, 512])

    @blk.gpsimd
    def _(gpsimd):
        for p in range(4):
            rB1w.sem(0).inc(gpsimd.dma_start(out=s_wp[p][:, :, :], in_=wproj[p]), 16)
        rB1w.sem(0).inc(gpsimd.dma_start(out=s_wf2[:], in_=wf2), 16)
        rB1w.sem(0).inc(gpsimd.dma_start(out=s_wg2[:], in_=wg2), 16)
        rB1w.sem(0).inc(gpsimd.dma_start(out=s_wbs[:, :, :], in_=wbm), 16)

    @blk.sync
    def _(sync):
        cS.wait(sync)

    prev_evac_marks = {}
    b1_evac_marks = []
    b1_silu_marks = []
    beta_tanh_marks = {}

    def b1_projs(t5):
        rk = (t5 * 512) // S
        toff = (t5 * 512) % S
        xt = s_xt[t5 % 2]
        ft = s_ft[t5 % 2]
        gt1 = s_gt1[t5 % 2]
        @blk.sync
        def _(sync):
            if t5 >= 2:
                sync.wait_ge(tE.h, prev_evac_marks.get(("mmdone", t5 - 2), 0))
            base = rk * AGR
            for dcg in range(4):
                rBx.sem(t5).inc(sync.dma_start(out=xt[:, dcg * 4:(dcg + 1) * 4, :],
                               in_=agout[base + dcg * 512:base + (dcg + 1) * 512,
                                         toff:toff + 512].rearrange("(dc p) t -> p dc t", p=128)), 16)
            rBx.sem(t5).inc(sync.dma_start(out=ft[:], in_=agout[base + HID:base + HID + 128, toff:toff + 512]), 16)
            rBx.sem(t5).inc(sync.dma_start(out=gt1[:], in_=agout[base + HID + 128:base + HID + 256, toff:toff + 512]), 16)

        for p in range(4):
            @blk.tensor
            def _(tensor, p=p):
                rBx.sem(t5).wait(tensor)
                if p == 0:
                    rB1w.sem(0).wait(tensor)
                if b1_evac_marks:
                    tensor.wait_ge(vE.h, b1_evac_marks[-1])
                for cb in range(4):
                    for dc in range(16):
                        mm = tensor.matmul(p_pr[cb][:], s_wp[p][:, dc, cb * 128:(cb + 1) * 128],
                                      xt[:, dc, :], start=(dc == 0), stop=(dc == 15))
                    tE.inc(mm)
                if p == 3:
                    tensor.wait_ge(vE.h, vE.n)
                    for dc in range(16):
                        mm = tensor.matmul(p_bc2[0:4, :], s_wbs[:, dc, :], xt[:, dc, :],
                                      start=(dc == 0), stop=(dc == 15))
                    tE.inc(mm)
            mm_mark = tE.n

            xh = s_xh[p]
            @blk.gpsimd
            def _(gpsimd, p=p):
                if t5 == 0:
                    gop(gpsimd, gpsimd.memset, xh[:, :, 0:3], 0.0)
            @blk.vector
            def _(vector, p=p):
                if t5 == 0:
                    gE.wait(vector)
                vector.wait_ge(tE.h, mm_mark)
                for cb in range(4):
                    vop(vector, vector.tensor_copy, out=xh[:, cb, 3:515], in_=p_pr[cb][:])
                b1_evac_marks.append(vE.n)
                if len(b1_silu_marks) >= 2:
                    vector.wait_ge(sE.h, b1_silu_marks[-2])
                for h in range(4):
                    cwc = lambda tap: s_cw[:, p * 16 + h * 4 + tap:p * 16 + h * 4 + tap + 1]
                    yc = s_yc[p % 2]
                    vop(vector, vector.tensor_scalar, out=yc[:, h, :], in0=xh[:, h, 3:515],
                        scalar1=cwc(3), scalar2=None, op0=ALU.mult)
                    for j in (1, 2, 3):
                        vop(vector, vector.scalar_tensor_tensor, out=yc[:, h, :],
                            in0=xh[:, h, 3 - j:515 - j], scalar=cwc(3 - j),
                            in1=yc[:, h, :], op0=ALU.mult, op1=ALU.add)
                vop(vector, vector.tensor_copy, out=xh[:, :, 0:3], in_=xh[:, :, 512:515])
            @blk.scalar
            def _(scalar, p=p):
                vE.wait(scalar)
                if t5 >= 2:
                    rBst.sem(t5).wait(scalar)
                sop(scalar, scalar.activation, out=s_out4[p][t5 % 2][:, :, :],
                    in_=s_yc[p % 2][:, :, :], func=AF.Silu)
                b1_silu_marks.append(sE.n)
                if p == 3:
                    sop(scalar, scalar.activation, out=s_beta[t5 % 2][:], in_=p_bc2[0:4, :],
                        func=AF.Tanh, scale=0.5)
                    beta_tanh_marks[t5] = sE.n
        @blk.vector
        def _(vector):
            sE.wait(vector)
            vop(vector, vector.tensor_scalar, out=s_beta[t5 % 2][:], in0=s_beta[t5 % 2][:],
                scalar1=0.5, scalar2=0.5, op0=ALU.mult, op1=ALU.add)
        prev_evac_marks[("beta_v", t5)] = vE.n

    def b1_tail(t5):
        par = t5 % 2
        ft = s_ft[par]
        gt1 = s_gt1[par]
        out4w = s_out4[3][par]
        @blk.scalar
        def _(scalar):
            if t5 > 0:
                rBst.sem(t5 - 1).wait(scalar)
            sop(scalar, scalar.square, s_wsq[:], out4w[:])
        for h in range(4):
            @blk.tensor
            def _(tensor, h=h):
                tensor.wait_ge(sE.h, sE.n)
                tE.inc(tensor.matmul(p_msq[:], s_onescb[:], s_wsq[:, h, :], start=True, stop=True))
            @blk.scalar
            def _(scalar, h=h):
                tE.wait(scalar)
                sop(scalar, scalar.activation, out=s_rsr[:, h * 512:(h + 1) * 512], in_=p_msq[:],
                    func=AF.Ln, bias=ccol("l2_eps")[0:1, :])
                sop(scalar, scalar.activation, out=s_rsr[:, h * 512:(h + 1) * 512],
                    in_=s_rsr[:, h * 512:(h + 1) * 512], func=AF.Exp, scale=-0.5)
        for h in range(4):
            @blk.tensor
            def _(tensor, h=h):
                tensor.wait_ge(sE.h, sE.n)
                tensor.wait_ge(vE.h, vE.n)
                tensor.wait_ge(vE.h, prev_evac_marks[("beta_v", t5)])
                if h == 0 and (t5 + 1) in beta_tanh_marks:
                    tensor.wait_ge(sE.h, beta_tanh_marks[t5 + 1])
                tE.inc(tensor.matmul(p_bc1[:], s_ones1b[:], s_rsr[:, h * 512:(h + 1) * 512],
                              start=True, stop=True))
                tE.inc(tensor.matmul(p_bc2[:], s_sel4[:, h * 128:(h + 1) * 128], s_beta[par][:],
                              start=True, stop=True))
            @blk.vector
            def _(vector, h=h):
                tE.wait(vector)
                vop(vector, vector.tensor_tensor, out=s_wbt[:, h, :], in0=out4w[:, h, :],
                    in1=p_bc1[:], op=ALU.mult)
                vop(vector, vector.tensor_tensor, out=s_wbt[:, h, :], in0=s_wbt[:, h, :],
                    in1=p_bc2[:], op=ALU.mult)
        for tt in range(4):
            @blk.tensor
            def _(tensor, tt=tt):
                tensor.wait_ge(sE.h, sE.n)
                tE.inc(tensor.matmul(p_zg[:], ft[:, tt * 128:(tt + 1) * 128], s_wf2[:],
                              start=True, stop=True))
            @blk.scalar
            def _(scalar, tt=tt):
                tE.wait(scalar)
                sop(scalar, scalar.copy, out=s_zgt[:, tt, :], in_=p_zg[:])
        for cb in range(4):
            @blk.tensor
            def _(tensor, cb=cb):
                tensor.wait_ge(sE.h, sE.n)
                tensor.wait_ge(vE.h, vE.n)
                tE.inc(tensor.matmul(p_bc1[:], s_wg2[:, cb * 128:(cb + 1) * 128], gt1[:],
                              start=True, stop=True))
            @blk.scalar
            def _(scalar, cb=cb):
                tE.wait(scalar)
                sop(scalar, scalar.activation, out=s_sgt2[:, cb, :], in_=p_bc1[:], func=AF.Tanh,
                    scale=0.5, bias=s_bg2h[:, cb:cb + 1])
        @blk.vector
        def _(vector):
            sE.wait(vector)
            vop(vector, vector.tensor_scalar, out=s_sgt2[:, :, :], in0=s_sgt2[:, :, :], scalar1=0.5,
                scalar2=0.5, op0=ALU.mult, op1=ALU.add)
        prev_evac_marks[("mmdone", t5)] = tE.n
        @blk.sync
        def _(sync):
            sync.wait_ge(vE.h, vE.n)
            sync.wait_ge(sE.h, sE.n)
            span = slice(t5 * 512, (t5 + 1) * 512)
            for p in range(3):
                for h in range(4):
                    rBst.sem(t5).inc(sync.dma_start(out=pr_scr[p][h, :, span], in_=s_out4[p][par][:, h, :]), 16)
            for h in range(4):
                rBst.sem(t5).inc(sync.dma_start(out=pr_scr[3][h, :, span], in_=s_wbt[:, h, :]), 16)
                rBst.sem(t5).inc(sync.dma_start(out=sg_scr[h, :, span], in_=s_sgt2[:, h, :]), 16)
            rBst.sem(t5).inc(sync.dma_start(out=zg_scr[span, :].rearrange("(tt p) m -> p tt m", p=128),
                           in_=s_zgt[:]), 16)

    for t5 in range(NT5B):
        b1_projs(t5)
        if t5 >= 1:
            b1_tail(t5 - 1)
    b1_tail(NT5B - 1)

    if taps:
        full_barrier()
        @blk.sync
        def _(sync):
            rOut.sem(0).inc(sync.dma_start(out=tp_xnT, in_=agout), 16)
            rOut.sem(0).inc(sync.dma_start(out=tp_fuse, in_=fuse_scr), 16)
            for p in range(4):
                rOut.sem(0).inc(sync.dma_start(out=tp_pr[p], in_=pr_scr[p]), 16)
            rOut.sem(0).inc(sync.dma_start(out=tp_zg, in_=zg_scr), 16)
            rOut.sem(0).inc(sync.dma_start(out=tp_sg, in_=sg_scr), 16)

    full_barrier()
    stB1.close()
    cur_stack[0] = ctx

    scope("pB2")
    stB2 = open_phase()
    rCh = Ring(nc, "rCh", 3)
    rCt = Ring(nc, "rCt", 3)
    rO2 = Ring(nc, "rO2", 2)
    s_ones4 = sb("s_ones4", [4, 128], F32)
    s_qf = [sb(f"s_qf{i}", [128, 4, 128], BF16) for i in range(3)]
    s_kf = [sb(f"s_kf{i}", [128, 4, 128], BF16) for i in range(3)]
    s_wf = [sb(f"s_wbf{i}", [128, 4, 128], BF16) for i in range(3)]
    s_kt = [sb(f"s_kt{i}", [128, 4, 128], BF16) for i in range(3)]
    s_vt = [sb(f"s_vt{i}", [128, 4, 128], BF16) for i in range(3)]
    s_wt = [sb(f"s_wt{i}", [128, 4, 128], BF16) for i in range(3)]
    s_zg2 = [sb(f"s_zg2{i}", [128, 512], F32) for i in range(3)]
    s_sg2 = [sb(f"s_sg2{i}", [128, 4, 128], BF16) for i in range(3)]
    s_St = sb("s_St", [128, 4, 256], F32)
    s_Stb = sb("s_Stb", [128, 4, 256], BF16)
    s_SkT = sb("s_SkT", [128, 4, 128], BF16)
    s_sp = [sb(f"s_sp{i}", [128, 512], F32) for i in range(2)]
    s_Ltok = [sb(f"s_Ltok{i}", [128, 512], F32) for i in range(2)]
    s_Lf = [sb(f"s_Lf{i}", [128, 4, 128], F32) for i in range(2)]
    s_Ep = [sb(f"s_Ep{i}", [128, 512], F32) for i in range(2)]
    s_Emt = [sb(f"s_Emt{i}", [128, 512], F32) for i in range(2)]
    s_Emf = [sb(f"s_Emf{i}", [128, 4, 128], F32) for i in range(2)]
    s_Ptok = [sb(f"s_Ptok{i}", [128, 512], BF16) for i in range(2)]
    s_Pf = [sb(f"s_Pf{i}", [128, 4, 128], BF16) for i in range(2)]
    s_Am = [sb(f"s_Am{i}", [128, 4, 128], BF16) for i in range(2)]
    s_Bm = sb("s_Bm", [128, 4, 128], BF16)
    s_t1 = sb("s_t1", [128, 512], F32)
    s_scl = sb("s_scl", [128, 512], F32)
    s_expo = sb("s_expo", [128, 512], F32)
    s_dn = sb("s_dn", [128, 4, 1], F32)
    s_rdn = sb("s_rdn", [128, 4, 1], F32)
    s_t2 = sb("s_t2", [128, 512], F32)
    s_eopf = sb("s_eopf", [128, 512], F32)
    s_eop = sb("s_eop", [128, 4, 128], BF16)
    s_ehat = sb("s_ehat", [128, 4, 128], BF16)
    s_rc = sb("s_rc", [128, 4, 1], F32)
    s_rexp = sb("s_rexp", [128, 4, 1], F32)
    s_tail = sb("s_tail", [128, 4, 1], F32)
    s_osq = sb("s_osq", [128, 512], BF16)
    s_r2 = sb("s_r2", [1, 512], F32)
    s_rsb2 = sb("s_rsb2", [128, 512], F32)
    s_t3 = sb("s_t3", [128, 512], F32)
    s_ofin = [sb(f"s_ofin{i}", [128, 4, 128], BF16) for i in range(2)]
    p_misc = ps("p_misc", [128, 512])
    p_lf = ps("p_lf", [128, 4, 128])
    p_ab = [ps(f"p_ab{i}", [128, 4, 128]) for i in range(2)]
    p_so = ps("p_so", [128, 512])
    p_add = ps("p_add", [128, 512])
    p_et = ps("p_et", [128, 4, 128], BF16)

    rDw = Ring(nc, "rDw", 1)
    rDf = Ring(nc, "rDf", 2)
    rDst = Ring(nc, "rDst", 2)
    s_sdnb = sb("s_sdnb", [128, HID], F32)
    s_wdnp = sb("s_wdnp", [128, 64, 512], BF16)
    s_fuT = [sb(f"s_fuT{i}", [128, 64, 128], BF16) for i in range(2)]
    s_do = [sb(f"s_do{i}", [128, 512], F32) for i in range(2)]
    p_dn = ps("p_dn", [128, 512])

    @blk.gpsimd
    def _(gpsimd):
        gop(gpsimd, gpsimd.memset, s_ones4[:], 1.0)
        gop(gpsimd, gpsimd.memset, s_St[:], 0.0)
        gop(gpsimd, gpsimd.memset, s_Stb[:], 0.0)
        gop(gpsimd, gpsimd.memset, s_tail[:], 0.0)

    for dp in range(4):
        @blk.tensor
        def _(tensor, dp=dp):
            if dp == 0:
                rI.sem(0).wait(tensor)
            else:
                tensor.wait_ge(sE.h, sE.n)
            tE.inc(tensor.matmul(p_dn[:], s_ones1f[:], s_sdnr[:, dp * 512:(dp + 1) * 512],
                                 start=True, stop=True))
        @blk.scalar
        def _(scalar, dp=dp):
            tE.wait(scalar)
            sop(scalar, scalar.copy, out=s_sdnb[:, dp * 512:(dp + 1) * 512], in_=p_dn[:])
    sdnb_mark = sE.n

    dn_marks = {}
    dn_evac_marks = {}

    def dn_wload(dp):
        @blk.gpsimd
        def _(gpsimd):
            if dp > 0:
                gpsimd.wait_ge(tE.h, dn_marks[(dp - 1, NT1 - 1)])
            rDw.sem(0).inc(gpsimd.dma_start(out=s_wdnp[:, :, :], in_=wdn[dp]), 16)

    def dn_loads(u):
        if u >= 4 * NT1:
            return
        dp, it = u // NT1, u % NT1
        @blk.scalar
        def _(scalar):
            if u >= 2:
                scalar.wait_ge(tE.h, dn_marks.get((dp, it - 2) if it >= 2 else (dp - 1, NT1 - 2 + it), 0))
            rDf.sem(u).inc(scalar.dma_start(
                out=s_fuT[u % 2][:, :, :],
                in_=fuse_scr[:, it * 128:(it + 1) * 128].rearrange("(gi p) t -> p gi t", p=128)), 16)

    def dn_begin(u):
        if u >= 4 * NT1:
            return None
        return {"u": u, "dp": u // NT1, "it": u % NT1, "fslot": s_fuT[u % 2], "ic": 0}

    def dn_fill(st, n):
        if st is None or st["ic"] >= 64:
            return
        u, ic0 = st["u"], st["ic"]
        n = min(n, 64 - ic0)
        @blk.tensor
        def _(tensor):
            if ic0 == 0:
                rDw.sem(0).wait(tensor)
                rDf.sem(u).wait(tensor)
                if u == 0:
                    tensor.wait_ge(sE.h, sdnb_mark)
                if u >= 1:
                    tensor.wait_ge(vE.h, dn_evac_marks[u - 1])
            for ic in range(ic0, ic0 + n):
                mm = tensor.matmul(p_dn[:], st["fslot"][:, ic, :], s_wdnp[:, ic, :],
                                   start=(ic == 0), stop=(ic == 63))
            tE.inc(mm)
        st["ic"] = ic0 + n

    def dn_finish(st):
        if st is None:
            return
        dn_fill(st, 64 - st["ic"])
        u, dp, it = st["u"], st["dp"], st["it"]
        dn_marks[(dp, it)] = tE.n
        if it == NT1 - 1 and dp < 3:
            dn_wload(dp + 1)
        @blk.vector
        def _(vector):
            vector.wait_ge(tE.h, dn_marks[(dp, it)])
            vector.wait_ge(sE.h, sdnb_mark)
            if u >= 2:
                rDst.sem(u).wait(vector)
            vop(vector, vector.tensor_tensor, out=s_do[u % 2][:], in0=p_dn[:],
                in1=s_sdnb[:, dp * 512:(dp + 1) * 512], op=ALU.mult)
        dn_evac_marks[u] = vE.n
        @blk.sync
        def _(sync):
            sync.wait_ge(vE.h, dn_evac_marks[u])
            rDst.sem(u).inc(sync.dma_start(
                out=ffn_scr[it * 128:(it + 1) * 128, dp * 512:(dp + 1) * 512], in_=s_do[u % 2][:]), 16)

    dn_wload(0)
    dn_loads(0)
    dn_loads(1)
    chunk_done = {}
    clamp_marks = {}
    rc_marks = {}
    epi_marks = {}
    t1_marks = {}
    sp_marks = {}
    lcmm_marks = {}

    def b2_prefix(ch):
        par = ch % 2
        sl4 = ch % 3
        span = slice(ch * 128, (ch + 1) * 128)
        qf, kf, wf = s_qf[sl4], s_kf[sl4], s_wf[sl4]
        kt, vt, wt = s_kt[sl4], s_vt[sl4], s_wt[sl4]
        zg2, sg2 = s_zg2[sl4], s_sg2[sl4]

        @blk.sync
        def _(sync):
            if ch >= 3:
                d = chunk_done[ch - 3]
                sync.wait_ge(tE.h, d["t"]); sync.wait_ge(vE.h, d["v"]); sync.wait_ge(sE.h, d["s"])
            rCh.sem(ch).inc(sync.dma_start(out=zg2[:], in_=zg_scr[span, :]), 16)
            rCh.sem(ch).inc(sync.dma_start(out=qf[:, :, :], in_=pr_scr[0][:, :, span].rearrange("h c t -> c h t")), 16)
            rCh.sem(ch).inc(sync.dma_start(out=kf[:, :, :], in_=pr_scr[1][:, :, span].rearrange("h c t -> c h t")), 16)
            rCh.sem(ch).inc(sync.dma_start(out=wf[:, :, :], in_=pr_scr[3][:, :, span].rearrange("h c t -> c h t")), 16)
            rCh.sem(ch).inc(sync.dma_start(out=sg2[:, :, :], in_=sg_scr[:, :, span].rearrange("h c t -> c h t")), 16)
            rCt.sem(ch).inc(sync.dma_start(out=kt[:, :, :].rearrange("t h c -> t (h c)"),
                                           in_=pr_scr[1][:, :, span].rearrange("h c t -> (h c) t"),
                                           transpose=True), 16)
            rCt.sem(ch).inc(sync.dma_start(out=vt[:, :, :].rearrange("t h c -> t (h c)"),
                                           in_=pr_scr[2][:, :, span].rearrange("h c t -> (h c) t"),
                                           transpose=True), 16)
            rCt.sem(ch).inc(sync.dma_start(out=wt[:, :, :].rearrange("t h c -> t (h c)"),
                                           in_=pr_scr[3][:, :, span].rearrange("h c t -> (h c) t"),
                                           transpose=True), 16)

        @blk.scalar
        def _(scalar):
            rCh.sem(ch).wait(scalar)
            sop(scalar, scalar.activation, out=s_sp[par][:], in_=zg2[:], func=AF.Exp, scale=-1.0)
            sop(scalar, scalar.activation, out=s_sp[par][:], in_=s_sp[par][:], func=AF.Ln, bias=ccol("one"))
        sp_marks[ch] = sE.n

        @blk.tensor
        def _(tensor):
            tensor.wait_ge(sE.h, sp_marks[ch])
            if ch >= 1:
                tensor.wait_ge(vE.h, clamp_marks[ch - 1])
            if ch == 0:
                gE.wait(tensor)
            tE.inc(tensor.matmul(p_misc[:], s_mcx[:, 0:128], s_sp[par][:], start=True, stop=True))
            for h in range(4):
                tE.inc(tensor.matmul(p_lf[:, h, :], s_sp[par][:, h * 128:(h + 1) * 128],
                                     s_mcx[:, 0:128], start=True, stop=True))
        lmm_mark = tE.n

        @blk.vector
        def _(vector):
            vector.wait_ge(tE.h, lmm_mark)
            vop(vector, vector.tensor_scalar, out=s_Ltok[par][:], in0=p_misc[:], scalar1=80.0,
                scalar2=-80.0, op0=ALU.min, op1=ALU.max)
            vop(vector, vector.tensor_scalar, out=s_Lf[par][:, :, :], in0=p_lf[:, :, :], scalar1=80.0,
                scalar2=-80.0, op0=ALU.min, op1=ALU.max)
        clamp_marks[ch] = vE.n

        @blk.tensor
        def _(tensor):
            rCh.sem(ch).wait(tensor)
            if ch >= 2:
                tensor.wait_ge(vE.h, epi_marks[ch - 2])
            for h in range(4):
                tE.inc(tensor.matmul(p_ab[par][:, h, :], kf[:, h, :], qf[:, h, :], start=True, stop=True))
        amm_mark = tE.n
        @blk.vector
        def _(vector):
            vector.wait_ge(tE.h, amm_mark)
            vop(vector, vector.tensor_copy, out=s_Am[par][:, :, :], in_=p_ab[par][:, :, :])
        amcp_mark = vE.n
        @blk.gpsimd
        def _(gpsimd):
            gpsimd.wait_ge(vE.h, amcp_mark)
            gop(gpsimd, gpsimd.affine_select, out=s_Am[par][:, :, :], in_=s_Am[par][:, :, :],
                pattern=[[0, 4], [1, 128]], compare_op=ALU.is_ge, fill=0.0, base=0, channel_multiplier=-1)
        return {}

    def b2_suffix(ch, pm):
        par = ch % 2
        sl4 = ch % 3
        span = slice(ch * 128, (ch + 1) * 128)
        qf = s_qf[sl4]
        kt, vt = s_kt[sl4], s_vt[sl4]
        sg2 = s_sg2[sl4]
        ofin = s_ofin[par]

        s_base = sE.n
        rexp_mark = s_base + 1
        eexp_mark = s_base + 4
        expo_mark = s_base + 5
        eopf_mark = s_base + 6
        osq_mark = s_base + 7
        r2_mark = s_base + 9

        dnst = dn_begin(ch)
        dn_fill(dnst, 6)
        @blk.tensor
        def _(tensor):
            if ch >= 1:
                tensor.wait_ge(vE.h, t1_marks[ch - 1])
            for h in range(4):
                tE.inc(tensor.matmul(p_so[:, h:h + 1], s_sp[par][:, h * 128:(h + 1) * 128],
                                     s_mcx[:, 128:129], start=True, stop=True))
        lcmm_marks[ch] = tE.n
        @blk.vector
        def _(vector):
            vector.wait_ge(tE.h, lcmm_marks[ch])
            vop(vector, vector.tensor_tensor, out=s_rc[:, :, :],
                in0=bass.AP(p_so, 0, [[512, 128], [1, 4], [1, 1]]),
                in1=s_tail[:, :, :], op=ALU.add)
        rc_marks[ch] = vE.n
        @blk.vector
        def _(vector):
            vector.wait_ge(sE.h, rexp_mark)
            if ch > 0:
                for h in range(4):
                    vop(vector, vector.tensor_scalar, out=s_St[:, h, :], in0=s_St[:, h, :],
                        scalar1=s_rexp[:, h, 0:1].opt(), scalar2=None, op0=ALU.mult)
            vop(vector, vector.tensor_copy, out=s_Stb[:, :, :], in_=s_St[:, :, :])
        resc_mark = vE.n
        dn_fill(dnst, 6)
        @blk.tensor
        def _(tensor):
            tensor.wait_ge(vE.h, resc_mark)
            if ch >= 1:
                tensor.wait_ge(vE.h, ehat_marks[ch - 1])
            for h in range(4):
                tE.inc(tensor.transpose(p_et[:, h, :], s_Stb[:, h, 0:128], s_identb[:]))
        sktr_mark = tE.n
        @blk.vector
        def _(vector):
            vector.wait_ge(tE.h, sktr_mark)
            vop(vector, vector.tensor_copy, out=s_SkT[:, :, :], in_=p_et[:, :, :])
        skt_mark = vE.n
        wt, wf = s_wt[sl4], s_wf[sl4]
        @blk.vector
        def _(vector):
            rCt.sem(ch).wait(vector)
            vector.wait_ge(sE.h, eexp_mark)
            vop(vector, vector.tensor_tensor, out=s_Ptok[par][:],
                in0=wt[:, :, :].rearrange("s h m -> s (h m)"), in1=s_Emt[par][:], op=ALU.mult)
            vop(vector, vector.tensor_tensor, out=s_Pf[par][:, :, :], in0=wf[:, :, :],
                in1=s_Emf[par][:, :, :], op=ALU.mult)
        p_mark = vE.n

        dn_fill(dnst, 8)
        @blk.tensor
        def _(tensor):
            tensor.wait_ge(vE.h, skt_mark)
            tensor.wait_ge(gE.h, gE.n)
            tensor.wait_ge(vE.h, p_mark)
            for h in range(4):
                hs = slice(h * 128, (h + 1) * 128)
                tensor.matmul(p_so[:, hs], qf[:, h, :], s_SkT[:, h, :], start=True, stop=False)
                tE.inc(tensor.matmul(p_so[:, hs], s_Am[par][:, h, :], s_Ptok[par][:, hs], start=False, stop=True))
        smm_mark = tE.n

        @blk.vector
        def _(vector):
            vector.wait_ge(tE.h, smm_mark)
            vop(vector, vector.tensor_tensor, out=s_t1[:], in0=p_so[:], in1=s_Ep[par][:], op=ALU.mult)
            vop(vector, vector.tensor_scalar, out=s_scl[:], in0=s_t1[:], scalar1=SCALE,
                scalar2=70.0, op0=ALU.mult, op1=ALU.min)
        t1_marks[ch] = vE.n
        @blk.vector
        def _(vector):
            vector.wait_ge(sE.h, expo_mark)
            vop(vector, vector.tensor_reduce, out=s_dn[:, :, :],
                in_=s_expo[:].rearrange("t (h m) -> t h m", m=128), axis=AX.X, op=ALU.add)
            vop(vector, vector.reciprocal, out=s_rdn[:, :, :], in_=s_dn[:, :, :])
            vop(vector, vector.tensor_tensor, out=s_t2[:], in0=s_scl[:], in1=s_Ltok[par][:], op=ALU.subtract)
            vop(vector, vector.tensor_scalar, out=s_t2[:], in0=s_t2[:], scalar1=85.0, scalar2=None,
                op0=ALU.min)
        t2_mark = vE.n
        @blk.vector
        def _(vector):
            vector.wait_ge(sE.h, eopf_mark)
            for h in range(4):
                hs = slice(h * 128, (h + 1) * 128)
                vop(vector, vector.tensor_scalar, out=s_eop[:, h, :], in0=s_eopf[:, hs],
                    scalar1=s_rdn[:, h, 0:1].opt(), scalar2=None, op0=ALU.mult)
        eop_mark = vE.n
        dn_fill(dnst, 8)
        @blk.tensor
        def _(tensor):
            tensor.wait_ge(vE.h, eop_mark)
            tensor.wait_ge(vE.h, skt_mark)
            for h in range(4):
                tE.inc(tensor.transpose(p_et[:, h, :], s_eop[:, h, :], s_identb[:]))
        etr_mark = tE.n
        @blk.vector
        def _(vector):
            vector.wait_ge(tE.h, etr_mark)
            vop(vector, vector.tensor_copy, out=s_ehat[:, :, :], in_=p_et[:, :, :])
        ehat_marks[ch] = vE.n

        dn_fill(dnst, 6)
        @blk.tensor
        def _(tensor):
            tensor.wait_ge(vE.h, ehat_marks[ch])
            tensor.wait_ge(vE.h, p_mark)
            for h in range(4):
                tE.inc(tensor.matmul(p_ab[par][:, h, :], s_Pf[par][:, h, :], s_ehat[:, h, :],
                                     start=True, stop=True))
        bmm_mark = tE.n
        @blk.vector
        def _(vector):
            vector.wait_ge(tE.h, bmm_mark)
            vop(vector, vector.tensor_copy, out=s_Bm[:, :, :], in_=p_ab[par][:, :, :])
        bmcp_mark = vE.n
        @blk.gpsimd
        def _(gpsimd):
            gpsimd.wait_ge(vE.h, bmcp_mark)
            gop(gpsimd, gpsimd.affine_select, out=s_Bm[:, :, :], in_=s_Bm[:, :, :],
                pattern=[[0, 4], [1, 128]], compare_op=ALU.is_ge, fill=0.0, base=0, channel_multiplier=-1)
        dn_fill(dnst, 8)
        @blk.tensor
        def _(tensor):
            tensor.wait_ge(gE.h, gE.n)
            for h in range(4):
                tensor.matmul(p_ab[par][:, h, :], s_Stb[:, h, 128:256], s_ehat[:, h, :],
                              start=True, stop=False)
                tE.inc(tensor.matmul(p_ab[par][:, h, :], vt[:, h, :], s_Bm[:, h, :],
                                     start=False, stop=True))
        omm_mark = tE.n
        dn_fill(dnst, 8)
        @blk.tensor
        def _(tensor):
            tensor.wait_ge(sE.h, osq_mark)
            if ch >= 1:
                tensor.wait_ge(vE.h, svadd_marks[ch - 1])
            tE.inc(tensor.matmul(p_add[0:1, 0:512], s_onescb[:], s_osq[:], start=True, stop=True))
        msq_mark = tE.n
        dn_fill(dnst, 6)
        @blk.tensor
        def _(tensor):
            tensor.wait_ge(sE.h, r2_mark)
            tE.inc(tensor.matmul(p_add[:], s_ones1f[:], s_r2[:], start=True, stop=True))
        rsb_mark = tE.n
        @blk.vector
        def _(vector):
            vector.wait_ge(tE.h, rsb_mark)
            if ch >= 2:
                rO2.sem(ch).wait(vector)
            vop(vector, vector.tensor_copy, out=s_rsb2[:], in_=p_add[:])
            vop(vector, vector.tensor_tensor, out=s_t3[:],
                in0=p_ab[par][:, :, :].rearrange("e h t -> e (h t)"), in1=s_rsb2[:], op=ALU.mult)
            vop(vector, vector.tensor_tensor, out=ofin[:, :, :].rearrange("e h t -> e (h t)"),
                in0=s_t3[:], in1=sg2[:, :, :].rearrange("e h t -> e (h t)"), op=ALU.mult)
        epi_marks[ch] = vE.n

        dn_fill(dnst, 4)
        @blk.tensor
        def _(tensor):
            tensor.wait_ge(vE.h, epi_marks[ch])
            for h in range(4):
                hs = slice(h * 128, (h + 1) * 128)
                tE.inc(tensor.matmul(p_add[:, hs], s_Ptok[par][:, hs], kt[:, h, :],
                                     start=True, stop=True))
        kadd_mark = tE.n
        @blk.vector
        def _(vector):
            vector.wait_ge(tE.h, kadd_mark)
            vop(vector, vector.tensor_tensor, out=s_St[:, :, 0:128], in0=s_St[:, :, 0:128],
                in1=p_add[:].rearrange("m (h d) -> m h d", d=128), op=ALU.add)
        skadd_mark = vE.n
        dn_fill(dnst, 4)
        @blk.tensor
        def _(tensor):
            tensor.wait_ge(vE.h, skadd_mark)
            for h in range(4):
                hs = slice(h * 128, (h + 1) * 128)
                tE.inc(tensor.matmul(p_add[:, hs], s_Ptok[par][:, hs], vt[:, h, :],
                                     start=True, stop=True))
        vadd_mark = tE.n
        @blk.vector
        def _(vector):
            vector.wait_ge(tE.h, vadd_mark)
            vop(vector, vector.tensor_tensor, out=s_St[:, :, 128:256], in0=s_St[:, :, 128:256],
                in1=p_add[:].rearrange("m (h e) -> m h e", e=128), op=ALU.add)
            vop(vector, vector.tensor_copy, out=s_tail[:, :, :], in_=s_Lf[par][:, :, 127:128])
        svadd_marks[ch] = vE.n

        @blk.scalar
        def _(scalar):
            scalar.wait_ge(vE.h, rc_marks[ch])
            sop(scalar, scalar.activation, out=s_rexp[:, :, :], in_=s_rc[:, :, :], func=AF.Exp, scale=-1.0)
            scalar.wait_ge(vE.h, clamp_marks[ch])
            sop(scalar, scalar.activation, out=s_Ep[par][:], in_=s_Ltok[par][:], func=AF.Exp, scale=-1.0)
            sop(scalar, scalar.activation, out=s_Emt[par][:], in_=s_Ltok[par][:], func=AF.Exp)
            sop(scalar, scalar.activation, out=s_Emf[par][:, :, :], in_=s_Lf[par][:, :, :], func=AF.Exp)
            scalar.wait_ge(vE.h, t1_marks[ch])
            sop(scalar, scalar.activation, out=s_expo[:], in_=s_scl[:], func=AF.Exp, bias=ccol("neg_shift"))
            scalar.wait_ge(vE.h, t2_mark)
            sop(scalar, scalar.activation, out=s_eopf[:], in_=s_t2[:], func=AF.Exp, bias=ccol("neg_shift"))
            scalar.wait_ge(tE.h, omm_mark)
            sop(scalar, scalar.square, s_osq[:], p_ab[par][:, :, :].rearrange("e h t -> e (h t)"))
            scalar.wait_ge(tE.h, msq_mark)
            sop(scalar, scalar.activation, out=s_r2[:], in_=p_add[0:1, 0:512], func=AF.Ln,
                scale=1.0 / 128.0, bias=ccol("rms_eps")[0:1, :])
            sop(scalar, scalar.activation, out=s_r2[:], in_=s_r2[:], func=AF.Exp, scale=-0.5)
        assert sE.n == s_base + 9, (sE.n, s_base)

        jq = (ch * 128) // S
        cspan = slice((ch * 128) % S, (ch * 128) % S + 128)
        @blk.sync
        def _(sync):
            sync.wait_ge(vE.h, epi_marks[ch])
            rO2.sem(ch).inc(sync.dma_start(
                out=agin2[jq * 512:(jq + 1) * 512, cspan].rearrange("(h c) t -> c h t", c=128),
                in_=ofin[:, :, :]), 16)
        dn_finish(dnst)
        dn_loads(ch + 2)
        chunk_done[ch] = {"t": tE.n, "v": vE.n, "s": sE.n}

    ehat_marks = {}
    svadd_marks = {}
    pm_store = {}
    for ch in range(NCH):
        pm_store[ch] = b2_prefix(ch)
        if ch >= 1:
            b2_suffix(ch - 1, pm_store[ch - 1])
    b2_suffix(NCH - 1, pm_store[NCH - 1])

    @blk.gpsimd
    def _(gpsimd):
        for ch in range(max(0, NCH - 2), NCH):
            rO2.sem(ch).wait(gpsimd)
        gpsimd.collective_compute("AllGather", ALU.bypass, replica_groups=rg,
                                  ins=[agin2.opt()], outs=[agout2.opt()]).then_inc(cS.h, 1)
    cS.n += 1

    full_barrier()
    stB2.close()

    scope("pC2")
    stC2 = open_phase()
    rCw = Ring(nc, "rCw", 1)
    rCq = Ring(nc, "rCq", 1)
    rCo = Ring(nc, "rCo", 2)
    rCx = Ring(nc, "rCx", 2)
    rCf = Ring(nc, "rCf", 2)
    rOq = Ring(nc, "rOq", 2)
    s_qoff = sb("s_qoff", [1, 1], mybir.dt.uint32)
    s_wos = sb("s_wos", [128, 16, 2048], BF16)
    s_ot = [sb(f"s_ot{i}", [128, 16, 128], BF16) for i in range(2)]
    s_xc = [sb(f"s_xc{i}", [128, HID], F32) for i in range(2)]
    s_fc = [sb(f"s_fc{i}", [128, HID], F32) for i in range(2)]
    s_oq = [sb(f"s_oq{i}", [128, HID], F32) for i in range(2)]
    p_c2 = [ps(f"p_c2{i}", [128, 512]) for i in range(2)]

    @blk.gpsimd
    def _(gpsimd):
        rCw.sem(0).inc(gpsimd.dma_start(out=s_wos[:, :, :], in_=wo), 16)
    @blk.sync
    def _(sync):
        rCq.sem(0).inc(sync.dma_start(out=s_qoff[:], in_=qoff), 16)
    rSel = Ring(nc, "rSel", 1)
    @blk.gpsimd
    def _(gpsimd):
        from concourse.expressions import RuntimeValue
        rCq.sem(0).wait(gpsimd)
        cS.wait(gpsimd)
        rq = gpsimd.alloc_register("qreg")
        gpsimd.reg_load(rq, s_qoff[0:1, 0:1])
        gpsimd.reg_mul(rq, rq, 512 * S)
        for g in range(4):
            dyn = bass.AP(agout2.tensor, RuntimeValue(rq), [[S, 512], [1, S]])
            rSel.sem(0).inc(gpsimd.dma_start(out=osel[g * 512:(g + 1) * 512, :], in_=dyn), 16)
            if g < 3:
                gpsimd.reg_add(rq, rq, HID * S)
    @blk.vector
    def _(vector):
        rCw.sem(0).wait(vector)
        rI.sem(0).wait(vector)
        vop(vector, vector.tensor_scalar, out=s_wos[:, :, :], in0=s_wos[:, :, :],
            scalar1=s_normw[:, 0:1], scalar2=None, op0=ALU.mult)
    wos_mark = vE.n

    c2_marks = {}
    for it in range(NT1):
        ot, xc, fc, oq = s_ot[it % 2], s_xc[it % 2], s_fc[it % 2], s_oq[it % 2]
        @blk.sync
        def _(sync):
            if it >= 2:
                sync.wait_ge(tE.h, c2_marks[it - 2])
            rSel.sem(0).wait(sync)
            for g in range(4):
                rCo.sem(it).inc(sync.dma_start(
                    out=ot[:, g * 4:(g + 1) * 4, :],
                    in_=osel[g * 512:(g + 1) * 512,
                             it * 128:(it + 1) * 128].rearrange("(dc p) t -> p dc t", p=128)), 16)
            rCx.sem(it).inc(sync.dma_start(out=xc[:], in_=x_q[it * 128:(it + 1) * 128, :]), 16)
            rCf.sem(it).inc(sync.dma_start(out=fc[:], in_=ffn_scr[it * 128:(it + 1) * 128, :]), 16)
        for dp in range(4):
            @blk.tensor
            def _(tensor, dp=dp):
                rCo.sem(it).wait(tensor)
                if it == 0 and dp == 0:
                    tensor.wait_ge(vE.h, wos_mark)
                if it > 0 or dp > 1:
                    tensor.wait_ge(vE.h, vE.n)
                for dc in range(16):
                    mm = tensor.matmul(p_c2[dp % 2][:], ot[:, dc, :],
                                       s_wos[:, dc, dp * 512:(dp + 1) * 512],
                                       start=(dc == 0), stop=(dc == 15))
                tE.inc(mm)
            @blk.vector
            def _(vector, dp=dp):
                tE.wait(vector)
                rCx.sem(it).wait(vector)
                rCf.sem(it).wait(vector)
                if it >= 2 and dp == 0:
                    rOq.sem(it).wait(vector)
                dsl = slice(dp * 512, (dp + 1) * 512)
                vop(vector, vector.tensor_tensor, out=oq[:, dsl], in0=p_c2[dp % 2][:],
                    in1=fc[:, dsl], op=ALU.add)
                vop(vector, vector.tensor_tensor, out=oq[:, dsl], in0=oq[:, dsl],
                    in1=xc[:, dsl], op=ALU.add)
        c2_marks[it] = tE.n
        @blk.sync
        def _(sync):
            sync.wait_ge(vE.h, vE.n)
            rOq.sem(it).inc(sync.dma_start(out=out_q[it * 128:(it + 1) * 128, :], in_=oq[:]), 16)

    if taps:
        full_barrier()
        @blk.sync
        def _(sync):
            rOut.sem(0).inc(sync.dma_start(out=tp_o, in_=agin2), 16)
            rOut.sem(0).inc(sync.dma_start(out=tp_ffn, in_=ffn_scr), 16)

    full_barrier()
    @blk.sync
    def _(sync):
        for it in range(NT1):
            rOq.sem(it).wait(sync)
        rOut.sem(0).wait(sync)
    stC2.close()

    if scope_stack:
        nc.leave_named_scope(scope_stack[-1][0], scope_stack[-1][1], False)
        scope_stack.pop()
    ctx.close()
    return nc


def shard_inputs(inputs, T=16384):
    S = T // 4
    f32 = np.float32
    x = np.ascontiguousarray(np.asarray(inputs["L0_attn_hidden_states"], f32).reshape(2, -1, HID)[:, :T])
    NIB = INTER // 128
    cq, ck, cv = (np.asarray(inputs[f"L0_attn_c{n}"], f32) for n in "qkv")

    def rt16(w):
        C = w.shape[1]
        return np.ascontiguousarray(w.reshape(16, 128, C).transpose(1, 0, 2))

    NIBv = INTER // 128
    wgt_r = np.ascontiguousarray(np.asarray(inputs["L0_ffn_w_gate"])
                                 .reshape(16, 128, NIBv, 128).transpose(2, 1, 0, 3))
    wup_r = np.ascontiguousarray(np.asarray(inputs["L0_ffn_w_up"])
                                 .reshape(16, 128, NIBv, 128).transpose(2, 1, 0, 3))
    wdn_r = np.ascontiguousarray(np.asarray(inputs["L0_ffn_w_down"])
                                 .reshape(64, 128, 4, 512).transpose(2, 1, 0, 3))
    maps = []
    for c in range(8):
        b, hg = c // 4, c % 4
        sl = slice(hg * 512, (hg + 1) * 512)
        cwa = np.stack([cq[sl], ck[sl], cv[sl], cv[sl]])
        cwa = cwa.reshape(4, 4, 128, 4).transpose(2, 0, 1, 3)
        m = {
            "x_q": np.ascontiguousarray(x[b, hg * S:(hg + 1) * S]),
            "wq": rt16(np.asarray(inputs["L0_attn_Wq"], f32)[:, sl]),
            "wk": rt16(np.asarray(inputs["L0_attn_Wk"], f32)[:, sl]),
            "wv": rt16(np.asarray(inputs["L0_attn_Wv"], f32)[:, sl]),
            "ww": rt16(np.asarray(inputs["L0_attn_Ww"], f32)[:, sl]),
            "cw": np.ascontiguousarray(cwa, f32),
            "wf1": rt16(np.asarray(inputs["L0_attn_Wf1"], f32)),
            "wf2": np.ascontiguousarray(np.asarray(inputs["L0_attn_Wf2"], f32)[:, sl]),
            "wbm": rt16(np.asarray(inputs["L0_attn_Wb"], f32)[:, hg * 4:(hg + 1) * 4]),
            "wg1": rt16(np.asarray(inputs["L0_attn_Wg1"], f32)),
            "wg2": np.ascontiguousarray(np.asarray(inputs["L0_attn_Wg2"], f32)[:, sl]),
            "bg2": np.ascontiguousarray(np.asarray(inputs["L0_attn_bg2"], f32)[sl].reshape(4, 128).T),
            "normw": np.ascontiguousarray(np.asarray(inputs["L0_attn_norm_w"], f32).reshape(128, 1)),
            "wo": rt16(np.asarray(inputs["L0_attn_Wo"], f32)),
            "wgt": wgt_r, "wup": wup_r, "wdn": wdn_r,
            "sgt": np.ascontiguousarray(np.asarray(inputs["L0_ffn_s_gate"], f32).reshape(NIB, 128).T),
            "sup": np.ascontiguousarray(np.asarray(inputs["L0_ffn_s_up"], f32).reshape(NIB, 128).T),
            "sdn": np.ascontiguousarray(np.asarray(inputs["L0_ffn_s_down"], f32).reshape(1, HID)),
            "qoff": np.array([[hg]], np.uint32),
        }
        maps.append(m)
    return maps


def kernel(**inputs):
    T = 16384
    nc = build_program(T=T, taps=False)
    in_maps = shard_inputs(inputs, T)
    res = run_bass_kernel_spmd(nc, in_maps, core_ids=list(range(8)))
    S = T // 4
    out = np.zeros((2, T, HID), np.float32)
    for c in range(8):
        b, hg = c // 4, c % 4
        out[b, hg * S:(hg + 1) * S] = res.results[c]["out_q"]
    return out


# version 09
# speedup vs baseline: 1.3137x; 1.0938x over previous
import numpy as np
import ml_dtypes
import concourse.bass as bass
import concourse.mybir as mybir
from concourse.bass_utils import run_bass_kernel_spmd

F32 = mybir.dt.float32
BF16 = mybir.dt.bfloat16
I8 = mybir.dt.int8
AF = mybir.ActivationFunctionType
ALU = mybir.AluOpType
AX = mybir.AxisListType

HID = 2048
HPC = 4
INTER = 8192
CCH = 128
MID = 64
SCALE = 128.0 ** -0.5


ALL_SEMS = []


class Sem:
    def __init__(self, nc, name):
        self.h = nc.alloc_semaphore(name=name)
        self.n = 0
        ALL_SEMS.append(self)

    def inc(self, inst, k=1):
        inst.then_inc(self.h, k)
        self.n += k
        return inst

    def wait(self, eng, val=None):
        eng.wait_ge(self.h, self.n if val is None else val)


class Ring:
    def __init__(self, nc, name, slots):
        self.s = [Sem(nc, f"{name}{i}") for i in range(slots)]
        self.k = slots

    def sem(self, i):
        return self.s[i % self.k]


def build_program(T=16384, taps=False):
    ALL_SEMS.clear()
    S = T // 4
    NT1 = S // 128
    NT5Q = S // 512
    NT5B = T // 512
    NCH = T // CCH
    NIB = INTER // 128

    nc = bass.Bass("TRN2", target_bir_lowering=False)
    ap = lambda t: t.ap()
    x_q = ap(nc.dram_tensor("x_q", [S, HID], F32, kind="ExternalInput"))
    wproj = [ap(nc.dram_tensor(n, [128, 16, 512], F32, kind="ExternalInput")) for n in ("wq", "wk", "wv", "ww")]
    cw = ap(nc.dram_tensor("cw", [128, 4, 4, 4], F32, kind="ExternalInput"))
    wf1 = ap(nc.dram_tensor("wf1", [128, 16, 128], F32, kind="ExternalInput"))
    wf2 = ap(nc.dram_tensor("wf2", [128, 512], F32, kind="ExternalInput"))
    wbm = ap(nc.dram_tensor("wbm", [128, 16, 4], F32, kind="ExternalInput"))
    wg1 = ap(nc.dram_tensor("wg1", [128, 16, 128], F32, kind="ExternalInput"))
    wg2 = ap(nc.dram_tensor("wg2", [128, 512], F32, kind="ExternalInput"))
    bg2 = ap(nc.dram_tensor("bg2", [128, 4], F32, kind="ExternalInput"))
    normw = ap(nc.dram_tensor("normw", [128, 1], F32, kind="ExternalInput"))
    wo = ap(nc.dram_tensor("wo", [128, 16, HID], F32, kind="ExternalInput"))
    wgt = ap(nc.dram_tensor("wgt", [NIB, 128, 16, 128], I8, kind="ExternalInput"))
    wup = ap(nc.dram_tensor("wup", [NIB, 128, 16, 128], I8, kind="ExternalInput"))
    wdn = ap(nc.dram_tensor("wdn", [4, 128, 64, 512], I8, kind="ExternalInput"))
    sgt = ap(nc.dram_tensor("sgt", [128, NIB], F32, kind="ExternalInput"))
    sup = ap(nc.dram_tensor("sup", [128, NIB], F32, kind="ExternalInput"))
    sdn = ap(nc.dram_tensor("sdn", [1, HID], F32, kind="ExternalInput"))
    qoff = ap(nc.dram_tensor("qoff", [1, 1], mybir.dt.uint32, kind="ExternalInput"))
    out_q = ap(nc.dram_tensor("out_q", [S, HID], F32, kind="ExternalOutput"))

    AGR = HID + 256
    agin = ap(nc.dram_tensor("agin", [AGR, S], BF16, kind="Internal"))
    agout = ap(nc.dram_tensor("agout", [4 * AGR, S], BF16, kind="Internal"))
    fuse_scr = ap(nc.dram_tensor("fuse_scr", [INTER, S], BF16, kind="Internal"))
    pr_scr = [ap(nc.dram_tensor(f"pr{i}_scr", [HPC, 128, T], BF16, kind="Internal")) for i in range(4)]
    zg_scr = ap(nc.dram_tensor("zg_scr", [T, 512], F32, kind="Internal"))
    sg_scr = ap(nc.dram_tensor("sg_scr", [HPC, 128, T], BF16, kind="Internal"))
    agin2 = ap(nc.dram_tensor("agin2", [HID, S], BF16, kind="Internal"))
    agout2 = ap(nc.dram_tensor("agout2", [4 * HID, S], BF16, kind="Internal"))
    osel = ap(nc.dram_tensor("osel", [HID, S], BF16, kind="Internal"))
    ffn_scr = ap(nc.dram_tensor("ffn_scr", [S, HID], F32, kind="Internal"))
    if taps:
        tp_xnT = ap(nc.dram_tensor("tp_xnT", [4 * AGR, S], BF16, kind="ExternalOutput"))
        tp_fuse = ap(nc.dram_tensor("tp_fuse", [INTER, S], BF16, kind="ExternalOutput"))
        tp_pr = [ap(nc.dram_tensor(f"tp_pr{i}", [HPC, 128, T], BF16, kind="ExternalOutput")) for i in range(4)]
        tp_zg = ap(nc.dram_tensor("tp_zg", [T, 512], F32, kind="ExternalOutput"))
        tp_sg = ap(nc.dram_tensor("tp_sg", [HPC, 128, T], BF16, kind="ExternalOutput"))
        tp_o = ap(nc.dram_tensor("tp_o", [HID, S], BF16, kind="ExternalOutput"))
        tp_ffn = ap(nc.dram_tensor("tp_ffn", [S, HID], F32, kind="ExternalOutput"))

    rg = [[0, 1, 2, 3], [4, 5, 6, 7]]

    from contextlib import ExitStack
    ctx = ExitStack()
    cur_stack = [ctx]
    sb = lambda name, shape, dt: cur_stack[0].enter_context(nc.sbuf_tensor(name, shape, dt))
    ps = lambda name, shape, dt=F32: cur_stack[0].enter_context(nc.psum_tensor(name, shape, dt))

    s_identb = sb("s_identb", [128, 128], BF16)
    s_identf = sb("s_identf", [128, 128], F32)
    s_mcx = sb("s_mcx", [128, 132], F32)
    s_ones1f = sb("s_ones1f", [1, 128], F32)
    s_ones1b = sb("s_ones1b", [1, 128], BF16)
    s_onescf = sb("s_onescf", [128, 1], F32)
    s_onescb = sb("s_onescb", [128, 1], BF16)
    s_sel4 = sb("s_sel4", [4, 512], F32)
    s_cw = sb("s_cw", [128, 64], F32)
    s_sgt = sb("s_sgt", [128, NIB], F32)
    s_sup = sb("s_sup", [128, NIB], F32)
    s_sdnr = sb("s_sdnr", [1, HID], F32)
    s_normw = sb("s_normw", [128, 1], F32)
    s_bg2h = sb("s_bg2h", [128, 4], F32)
    CVALS = {"ln_eps": 1e-5, "l2_eps": 1e-6, "rms_eps": 1e-5, "one": 1.0,
             "neg_shift": -15.0, "clamp_l": 80.0, "neg_clamp_l": -80.0}
    s_const = sb("s_const", [128, len(CVALS)], F32)
    CIDX = {k: i for i, k in enumerate(CVALS)}
    ccol = lambda k: s_const[:, CIDX[k]:CIDX[k] + 1]

    tE = Sem(nc, "tE")
    vE = Sem(nc, "vE")
    sE = Sem(nc, "sE")
    gE = Sem(nc, "gE")
    cS = Sem(nc, "cS")
    rI = Ring(nc, "rI", 1)
    rWf = Ring(nc, "rWf", 1)
    rX = Ring(nc, "rX", 2)
    rAg = Ring(nc, "rAg", 1)
    rA2w = Ring(nc, "rA2w", 2)
    rFu = Ring(nc, "rFu", 2)
    rB1w = Ring(nc, "rB1w", 1)
    rBx = Ring(nc, "rBx", 2)
    rBst = Ring(nc, "rBst", 2)
    rOut = Ring(nc, "rOut", 1)

    def vop(vector, fn, *a, **k):
        vector.wait_ge(vE.h, vE.n)
        return vE.inc(fn(*a, **k))

    def sop(scalar, fn, *a, **k):
        scalar.wait_ge(sE.h, sE.n)
        return sE.inc(fn(*a, **k))

    def gop(gpsimd, fn, *a, **k):
        gpsimd.wait_ge(gE.h, gE.n)
        return gE.inc(fn(*a, **k))

    blk = ctx.enter_context(nc.Block())

    def full_barrier():
        for mk in ("tensor", "vector", "scalar", "gpsimd", "sync"):
            @getattr(blk, mk)
            def _(eng):
                for s in ALL_SEMS:
                    if s.n:
                        eng.wait_ge(s.h, s.n)

    def open_phase():
        st = ExitStack()
        cur_stack[0] = st
        return st

    scope_stack = []
    def scope(name):
        if scope_stack:
            nc.leave_named_scope(scope_stack[-1][0], scope_stack[-1][1], False)
            scope_stack.pop()
        sid, _ = nc.enter_named_scope(name, False)
        scope_stack.append((name, sid))

    scope("pA")
    @blk.sync
    def _(sync):
        for d, s_ in [(cw.rearrange("c p h t -> c (p h t)"), s_cw), (sgt, s_sgt), (sup, s_sup),
                      (sdn, s_sdnr), (normw, s_normw), (bg2, s_bg2h)]:
            rI.sem(0).inc(sync.dma_start(out=s_[:], in_=d), 16)

    @blk.gpsimd
    def _(gpsimd):
        gop(gpsimd, gpsimd.memset, s_identb[:], 1.0)
        gop(gpsimd, gpsimd.affine_select, out=s_identb[:], in_=s_identb[:], pattern=[[-1, 128]],
            compare_op=ALU.is_ge, fill=0.0, base=0, channel_multiplier=1)
        gop(gpsimd, gpsimd.affine_select, out=s_identb[:], in_=s_identb[:], pattern=[[1, 128]],
            compare_op=ALU.is_ge, fill=0.0, base=0, channel_multiplier=-1)
        gop(gpsimd, gpsimd.memset, s_identf[:], 1.0)
        gop(gpsimd, gpsimd.affine_select, out=s_identf[:], in_=s_identf[:], pattern=[[-1, 128]],
            compare_op=ALU.is_ge, fill=0.0, base=0, channel_multiplier=1)
        gop(gpsimd, gpsimd.affine_select, out=s_identf[:], in_=s_identf[:], pattern=[[1, 128]],
            compare_op=ALU.is_ge, fill=0.0, base=0, channel_multiplier=-1)
        gop(gpsimd, gpsimd.memset, s_mcx[:, 0:128], 1.0)
        gop(gpsimd, gpsimd.affine_select, out=s_mcx[:, 0:128], in_=s_mcx[:, 0:128], pattern=[[1, 128]],
            compare_op=ALU.is_ge, fill=0.0, base=0, channel_multiplier=-1)
        gop(gpsimd, gpsimd.memset, s_mcx[0:MID, 128:129], 1.0)
        gop(gpsimd, gpsimd.memset, s_mcx[MID:128, 128:129], 0.0)
        gop(gpsimd, gpsimd.memset, s_mcx[0:128, 129:130], 0.0)
        gop(gpsimd, gpsimd.memset, s_mcx[64:128, 129:130], 0.0)
        gop(gpsimd, gpsimd.memset, s_ones1f[:], 1.0)
        gop(gpsimd, gpsimd.memset, s_ones1b[:], 1.0)
        gop(gpsimd, gpsimd.memset, s_onescf[:], 1.0)
        gop(gpsimd, gpsimd.memset, s_onescb[:], 1.0)
        for k, v in CVALS.items():
            gop(gpsimd, gpsimd.memset, ccol(k), v)
        gop(gpsimd, gpsimd.memset, s_sel4[:], 1.0)
        sel3 = s_sel4.rearrange("k (h j) -> k h j", j=128)
        gop(gpsimd, gpsimd.affine_select, out=sel3, in_=sel3, pattern=[[1, 4], [0, 128]],
            compare_op=ALU.is_ge, fill=0.0, base=0, channel_multiplier=-1)
        gop(gpsimd, gpsimd.affine_select, out=sel3, in_=sel3, pattern=[[-1, 4], [0, 128]],
            compare_op=ALU.is_ge, fill=0.0, base=0, channel_multiplier=1)

    @blk.vector
    def _(vector):
        rI.sem(0).wait(vector)
        gE.wait(vector)
        vop(vector, vector.tensor_scalar, out=s_mcx[:, 0:128], in0=s_mcx[:, 0:128],
            scalar1=s_mcx[:, 128:129], scalar2=None, op0=ALU.subtract)
        vop(vector, vector.tensor_scalar_mul, s_bg2h[:], s_bg2h[:], 0.5)
    init_v_mark = vE.n

    stA = open_phase()
    s_x = [sb(f"s_x{i}", [128, HID], F32) for i in range(2)]
    s_xnt_all = sb("s_xnt_all", [128, 16, S], BF16)
    s_f1t = sb("s_f1t", [128, S], BF16)
    s_g1t = sb("s_g1t", [128, S], BF16)
    s_xn = sb("s_xn", [128, HID], BF16)
    s_mcol = sb("s_mcol", [128, 1], F32)
    s_vcol = sb("s_vcol", [128, 1], F32)
    s_rstd = sb("s_rstd", [128, 1], F32)
    s_wf1 = sb("s_wf1", [128, 16, 128], BF16)
    s_wg1 = sb("s_wg1", [128, 16, 128], BF16)
    p_tr = ps("p_tr", [128, 4, 128], BF16)
    p_f1 = ps("p_f1", [128, 128])
    p_g1 = ps("p_g1", [128, 128])

    @blk.gpsimd
    def _(gpsimd):
        rWf.sem(0).inc(gpsimd.dma_start(out=s_wf1[:, :, :], in_=wf1), 16)
        rWf.sem(0).inc(gpsimd.dma_start(out=s_wg1[:, :, :], in_=wg1), 16)

    apply_marks = []
    tr_marks = []
    for it in range(NT1):
        buf = s_x[it % 2]
        xnb = s_xn
        @blk.sync
        def _(sync):
            if it >= 2:
                sync.wait_ge(vE.h, apply_marks[it - 2])
            rX.sem(it).inc(sync.dma_start(out=buf[:], in_=x_q[it * 128:(it + 1) * 128, :]), 16)

        @blk.scalar
        def _(scalar):
            rX.sem(it).wait(scalar)
            if it >= 1:
                scalar.wait_ge(tE.h, tr_marks[it - 1])
            sop(scalar, scalar.activation, out=s_xn[:], in_=buf[:], func=AF.Square,
                accum_out=s_vcol[:])

        @blk.vector
        def _(vector):
            rX.sem(it).wait(vector)
            vop(vector, vector.reduce_sum, s_mcol[:], buf[:], axis=AX.X)
            vop(vector, vector.tensor_scalar_mul, s_mcol[:], s_mcol[:], 1.0 / HID)
            sE.wait(vector)
            vop(vector, vector.tensor_scalar_mul, s_vcol[:], s_vcol[:], 1.0 / HID)
            vop(vector, vector.scalar_tensor_tensor, out=s_vcol[:], in0=s_mcol[:],
                scalar=s_mcol[:, 0:1], in1=s_vcol[:], op0=ALU.mult, op1=ALU.subtract)

        @blk.scalar
        def _(scalar):
            vE.wait(scalar)
            sop(scalar, scalar.activation, out=s_rstd[:], in_=s_vcol[:], func=AF.Ln,
                scale=-1.0, bias=ccol("ln_eps"))
            sop(scalar, scalar.activation, out=s_rstd[:], in_=s_rstd[:], func=AF.Exp, scale=-0.5)

        @blk.vector
        def _(vector):
            sE.wait(vector)
            if it >= 1:
                vector.wait_ge(tE.h, tr_marks[it - 1])
            vop(vector, vector.tensor_scalar, out=xnb[:], in0=buf[:], scalar1=s_mcol[:, 0:1],
                scalar2=s_rstd[:, 0:1], op0=ALU.subtract, op1=ALU.mult)
        apply_marks.append(vE.n)

        for grp in range(4):
            @blk.tensor
            def _(tensor, grp=grp):
                tensor.wait_ge(vE.h, apply_marks[-1])
                if tE.n > 0:
                    tensor.wait_ge(sE.h, sE.n)
                for j in range(4):
                    dc = grp * 4 + j
                    tE.inc(tensor.transpose(p_tr[:, j, :], xnb[:, dc * 128:(dc + 1) * 128], s_identb[:]))
            tE.n += 0
            @blk.scalar
            def _(scalar, grp=grp):
                tE.wait(scalar)
                sop(scalar, scalar.copy,
                    out=s_xnt_all[:, grp * 4:(grp + 1) * 4, it * 128:(it + 1) * 128], in_=p_tr[:])
        tr_marks.append(tE.n)

        @blk.tensor
        def _(tensor):
            tensor.wait_ge(sE.h, sE.n)
            rWf.sem(0).wait(tensor)
            for dc in range(16):
                xt = s_xnt_all[:, dc, it * 128:(it + 1) * 128]
                mm = tensor.matmul(p_f1[:], s_wf1[:, dc, :], xt, start=(dc == 0), stop=(dc == 15))
            tE.inc(mm)
            for dc in range(16):
                xt = s_xnt_all[:, dc, it * 128:(it + 1) * 128]
                mm = tensor.matmul(p_g1[:], s_wg1[:, dc, :], xt, start=(dc == 0), stop=(dc == 15))
            tE.inc(mm)
        @blk.scalar
        def _(scalar):
            tE.wait(scalar)
            sop(scalar, scalar.copy, out=s_f1t[:, it * 128:(it + 1) * 128], in_=p_f1[:])
            sop(scalar, scalar.copy, out=s_g1t[:, it * 128:(it + 1) * 128], in_=p_g1[:])

    @blk.sync
    def _(sync):
        sync.wait_ge(sE.h, sE.n)
        for dc in range(16):
            rAg.sem(0).inc(sync.dma_start(out=agin[dc * 128:(dc + 1) * 128, :], in_=s_xnt_all[:, dc, :]), 16)
        rAg.sem(0).inc(sync.dma_start(out=agin[HID:HID + 128, :], in_=s_f1t[:]), 16)
        rAg.sem(0).inc(sync.dma_start(out=agin[HID + 128:HID + 256, :], in_=s_g1t[:]), 16)

    @blk.gpsimd
    def _(gpsimd):
        rAg.sem(0).wait(gpsimd)
        gpsimd.collective_compute("AllGather", ALU.bypass, replica_groups=rg,
                                  ins=[agin.opt()], outs=[agout.opt()]).then_inc(cS.h, 1)
    cS.n += 1

    scope("pA2")
    s_wgi = [sb(f"s_wgi{i}", [128, 16, 128], BF16) for i in range(2)]
    s_wui = [sb(f"s_wui{i}", [128, 16, 128], BF16) for i in range(2)]
    s_gtb = [sb(f"s_gtb{i}", [128, 512], BF16) for i in range(2)]
    s_upb = [sb(f"s_upb{i}", [128, 512], BF16) for i in range(2)]
    s_fu = [sb(f"s_fu{i}", [128, 512], BF16) for i in range(2)]
    p_gt = [ps(f"p_gt{i}", [128, 512]) for i in range(2)]
    p_up = [ps(f"p_up{i}", [128, 512]) for i in range(2)]

    fuse_dma_marks = []
    a2_mm_marks = []
    a2_evac_marks = []
    a2_tt_marks = []
    for ib in range(NIB):
        wgb, wub = s_wgi[ib % 2], s_wui[ib % 2]
        @blk.gpsimd
        def _(gpsimd):
            if ib >= 2:
                gpsimd.wait_ge(tE.h, a2_mm_marks[ib - 2])
            rA2w.sem(ib).inc(gpsimd.dma_start(out=wgb[:, :, :], in_=wgt[ib]), 16)
            rA2w.sem(ib).inc(gpsimd.dma_start(out=wub[:, :, :], in_=wup[ib]), 16)
        fub = s_fu[ib % 2]
        for t5 in range(NT5Q):
            u = (ib * NT5Q + t5)
            pg, pu = p_gt[u % 2], p_up[u % 2]
            @blk.tensor
            def _(tensor, t5=t5, pg=pg, pu=pu):
                rA2w.sem(ib).wait(tensor)
                if u >= 2:
                    tensor.wait_ge(sE.h, a2_evac_marks[u - 2])
                for dc in range(16):
                    xt = s_xnt_all[:, dc, t5 * 512:(t5 + 1) * 512]
                    mm = tensor.matmul(pg[:], wgb[:, dc, :], xt, start=(dc == 0), stop=(dc == 15))
                tE.inc(mm)
                for dc in range(16):
                    xt = s_xnt_all[:, dc, t5 * 512:(t5 + 1) * 512]
                    mm = tensor.matmul(pu[:], wub[:, dc, :], xt, start=(dc == 0), stop=(dc == 15))
                tE.inc(mm)
            @blk.scalar
            def _(scalar, t5=t5, pg=pg, pu=pu):
                tE.wait(scalar)
                if u >= 2:
                    scalar.wait_ge(vE.h, a2_tt_marks[u - 2])
                sop(scalar, scalar.activation, out=s_gtb[u % 2][:], in_=pg[:], func=AF.Silu,
                    scale=s_sgt[:, ib:ib + 1])
                sop(scalar, scalar.activation, out=s_upb[u % 2][:], in_=pu[:], func=AF.Copy,
                    scale=s_sup[:, ib:ib + 1])
                a2_evac_marks.append(sE.n)
            fub = s_fu[t5 % 2]
            @blk.vector
            def _(vector, t5=t5, fub=fub):
                sE.wait(vector)
                if ib > 0 or t5 >= 2:
                    rFu.sem(t5).wait(vector)
                vop(vector, vector.tensor_tensor, out=fub[:],
                    in0=s_gtb[u % 2][:], in1=s_upb[u % 2][:], op=ALU.mult)
                a2_tt_marks.append(vE.n)
            @blk.sync
            def _(sync, t5=t5, fub=fub):
                sync.wait_ge(vE.h, vE.n)
                rFu.sem(t5).inc(sync.dma_start(
                    out=fuse_scr[ib * 128:(ib + 1) * 128, t5 * 512:(t5 + 1) * 512], in_=fub[:]), 16)
        a2_mm_marks.append(tE.n)

    full_barrier()
    stA.close()

    scope("pB1")
    stB1 = open_phase()
    s_wp = [sb(f"s_wp{p}", [128, 16, 512], BF16) for p in range(4)]
    s_wf2 = sb("s_wf2", [128, 512], BF16)
    s_wg2 = sb("s_wg2", [128, 512], BF16)
    s_wbs = sb("s_wbs", [128, 16, 4], BF16)
    s_xt = [sb(f"s_xt{i}", [128, 16, 512], BF16) for i in range(2)]
    s_ft = [sb(f"s_ft{i}", [128, 512], BF16) for i in range(2)]
    s_gt1 = [sb(f"s_gt1{i}", [128, 512], BF16) for i in range(2)]
    s_xh = [sb(f"s_xh{p}", [128, 4, 515], BF16) for p in range(4)]
    s_yc = [sb(f"s_yc{i}", [128, 4, 512], BF16) for i in range(2)]
    s_out4 = [[sb(f"s_out4{p}_{i}", [128, 4, 512], BF16) for i in range(2)] for p in range(4)]
    s_wsq = sb("s_wsq", [128, 4, 512], BF16)
    s_beta = [sb(f"s_beta{i}", [4, 512], F32) for i in range(2)]
    s_rsr = sb("s_rsr", [1, 2048], BF16)
    s_rsb = sb("s_rsb", [128, 512], F32)
    s_wbt = sb("s_wbt", [128, 4, 512], BF16)
    s_zgt = sb("s_zgt", [128, 4, 512], F32)
    s_sgt2 = sb("s_sgt2", [128, 4, 512], BF16)
    p_pr = [ps(f"p_pr{i}", [128, 512]) for i in range(4)]
    p_msq = ps("p_msq", [1, 512])
    p_bc1 = ps("p_bc1", [128, 512])
    p_bc2 = ps("p_bc2", [128, 512])
    p_zg = ps("p_zg", [128, 512])

    @blk.gpsimd
    def _(gpsimd):
        for p in range(4):
            rB1w.sem(0).inc(gpsimd.dma_start(out=s_wp[p][:, :, :], in_=wproj[p]), 16)
        rB1w.sem(0).inc(gpsimd.dma_start(out=s_wf2[:], in_=wf2), 16)
        rB1w.sem(0).inc(gpsimd.dma_start(out=s_wg2[:], in_=wg2), 16)
        rB1w.sem(0).inc(gpsimd.dma_start(out=s_wbs[:, :, :], in_=wbm), 16)

    @blk.sync
    def _(sync):
        cS.wait(sync)

    prev_evac_marks = {}
    b1_evac_marks = []
    b1_silu_marks = []
    beta_tanh_marks = {}

    def b1_projs(t5):
        rk = (t5 * 512) // S
        toff = (t5 * 512) % S
        xt = s_xt[t5 % 2]
        ft = s_ft[t5 % 2]
        gt1 = s_gt1[t5 % 2]
        @blk.sync
        def _(sync):
            if t5 >= 2:
                sync.wait_ge(tE.h, prev_evac_marks.get(("mmdone", t5 - 2), 0))
            base = rk * AGR
            for dcg in range(4):
                rBx.sem(t5).inc(sync.dma_start(out=xt[:, dcg * 4:(dcg + 1) * 4, :],
                               in_=agout[base + dcg * 512:base + (dcg + 1) * 512,
                                         toff:toff + 512].rearrange("(dc p) t -> p dc t", p=128)), 16)
            rBx.sem(t5).inc(sync.dma_start(out=ft[:], in_=agout[base + HID:base + HID + 128, toff:toff + 512]), 16)
            rBx.sem(t5).inc(sync.dma_start(out=gt1[:], in_=agout[base + HID + 128:base + HID + 256, toff:toff + 512]), 16)

        for p in range(4):
            @blk.tensor
            def _(tensor, p=p):
                rBx.sem(t5).wait(tensor)
                if p == 0:
                    rB1w.sem(0).wait(tensor)
                if b1_evac_marks:
                    tensor.wait_ge(vE.h, b1_evac_marks[-1])
                for cb in range(4):
                    for dc in range(16):
                        mm = tensor.matmul(p_pr[cb][:], s_wp[p][:, dc, cb * 128:(cb + 1) * 128],
                                      xt[:, dc, :], start=(dc == 0), stop=(dc == 15))
                    tE.inc(mm)
                if p == 3:
                    tensor.wait_ge(vE.h, vE.n)
                    for dc in range(16):
                        mm = tensor.matmul(p_bc2[0:4, :], s_wbs[:, dc, :], xt[:, dc, :],
                                      start=(dc == 0), stop=(dc == 15))
                    tE.inc(mm)
            mm_mark = tE.n

            xh = s_xh[p]
            @blk.gpsimd
            def _(gpsimd, p=p):
                if t5 == 0:
                    gop(gpsimd, gpsimd.memset, xh[:, :, 0:3], 0.0)
            @blk.vector
            def _(vector, p=p):
                if t5 == 0:
                    gE.wait(vector)
                vector.wait_ge(tE.h, mm_mark)
                for cb in range(4):
                    vop(vector, vector.tensor_copy, out=xh[:, cb, 3:515], in_=p_pr[cb][:])
                b1_evac_marks.append(vE.n)
                if len(b1_silu_marks) >= 2:
                    vector.wait_ge(sE.h, b1_silu_marks[-2])
                for h in range(4):
                    cwc = lambda tap: s_cw[:, p * 16 + h * 4 + tap:p * 16 + h * 4 + tap + 1]
                    yc = s_yc[p % 2]
                    vop(vector, vector.tensor_scalar, out=yc[:, h, :], in0=xh[:, h, 3:515],
                        scalar1=cwc(3), scalar2=None, op0=ALU.mult)
                    for j in (1, 2, 3):
                        vop(vector, vector.scalar_tensor_tensor, out=yc[:, h, :],
                            in0=xh[:, h, 3 - j:515 - j], scalar=cwc(3 - j),
                            in1=yc[:, h, :], op0=ALU.mult, op1=ALU.add)
                vop(vector, vector.tensor_copy, out=xh[:, :, 0:3], in_=xh[:, :, 512:515])
            @blk.scalar
            def _(scalar, p=p):
                vE.wait(scalar)
                if t5 >= 2:
                    rBst.sem(t5).wait(scalar)
                sop(scalar, scalar.activation, out=s_out4[p][t5 % 2][:, :, :],
                    in_=s_yc[p % 2][:, :, :], func=AF.Silu)
                b1_silu_marks.append(sE.n)
                if p == 3:
                    sop(scalar, scalar.activation, out=s_beta[t5 % 2][:], in_=p_bc2[0:4, :],
                        func=AF.Tanh, scale=0.5)
                    beta_tanh_marks[t5] = sE.n
        @blk.vector
        def _(vector):
            sE.wait(vector)
            vop(vector, vector.tensor_scalar, out=s_beta[t5 % 2][:], in0=s_beta[t5 % 2][:],
                scalar1=0.5, scalar2=0.5, op0=ALU.mult, op1=ALU.add)
        prev_evac_marks[("beta_v", t5)] = vE.n

    def b1_tail(t5):
        par = t5 % 2
        ft = s_ft[par]
        gt1 = s_gt1[par]
        out4w = s_out4[3][par]
        @blk.scalar
        def _(scalar):
            if t5 > 0:
                rBst.sem(t5 - 1).wait(scalar)
            sop(scalar, scalar.square, s_wsq[:], out4w[:])
        for h in range(4):
            @blk.tensor
            def _(tensor, h=h):
                tensor.wait_ge(sE.h, sE.n)
                tE.inc(tensor.matmul(p_msq[:], s_onescb[:], s_wsq[:, h, :], start=True, stop=True))
            @blk.scalar
            def _(scalar, h=h):
                tE.wait(scalar)
                sop(scalar, scalar.activation, out=s_rsr[:, h * 512:(h + 1) * 512], in_=p_msq[:],
                    func=AF.Ln, bias=ccol("l2_eps")[0:1, :])
                sop(scalar, scalar.activation, out=s_rsr[:, h * 512:(h + 1) * 512],
                    in_=s_rsr[:, h * 512:(h + 1) * 512], func=AF.Exp, scale=-0.5)
        for h in range(4):
            @blk.tensor
            def _(tensor, h=h):
                tensor.wait_ge(sE.h, sE.n)
                tensor.wait_ge(vE.h, vE.n)
                tensor.wait_ge(vE.h, prev_evac_marks[("beta_v", t5)])
                if h == 0 and (t5 + 1) in beta_tanh_marks:
                    tensor.wait_ge(sE.h, beta_tanh_marks[t5 + 1])
                tE.inc(tensor.matmul(p_bc1[:], s_ones1b[:], s_rsr[:, h * 512:(h + 1) * 512],
                              start=True, stop=True))
                tE.inc(tensor.matmul(p_bc2[:], s_sel4[:, h * 128:(h + 1) * 128], s_beta[par][:],
                              start=True, stop=True))
            @blk.vector
            def _(vector, h=h):
                tE.wait(vector)
                vop(vector, vector.tensor_tensor, out=s_wbt[:, h, :], in0=out4w[:, h, :],
                    in1=p_bc1[:], op=ALU.mult)
                vop(vector, vector.tensor_tensor, out=s_wbt[:, h, :], in0=s_wbt[:, h, :],
                    in1=p_bc2[:], op=ALU.mult)
        for tt in range(4):
            @blk.tensor
            def _(tensor, tt=tt):
                tensor.wait_ge(sE.h, sE.n)
                tE.inc(tensor.matmul(p_zg[:], ft[:, tt * 128:(tt + 1) * 128], s_wf2[:],
                              start=True, stop=True))
            @blk.scalar
            def _(scalar, tt=tt):
                tE.wait(scalar)
                sop(scalar, scalar.copy, out=s_zgt[:, tt, :], in_=p_zg[:])
        for cb in range(4):
            @blk.tensor
            def _(tensor, cb=cb):
                tensor.wait_ge(sE.h, sE.n)
                tensor.wait_ge(vE.h, vE.n)
                tE.inc(tensor.matmul(p_bc1[:], s_wg2[:, cb * 128:(cb + 1) * 128], gt1[:],
                              start=True, stop=True))
            @blk.scalar
            def _(scalar, cb=cb):
                tE.wait(scalar)
                sop(scalar, scalar.activation, out=s_sgt2[:, cb, :], in_=p_bc1[:], func=AF.Tanh,
                    scale=0.5, bias=s_bg2h[:, cb:cb + 1])
        @blk.vector
        def _(vector):
            sE.wait(vector)
            vop(vector, vector.tensor_scalar, out=s_sgt2[:, :, :], in0=s_sgt2[:, :, :], scalar1=0.5,
                scalar2=0.5, op0=ALU.mult, op1=ALU.add)
        prev_evac_marks[("mmdone", t5)] = tE.n
        @blk.sync
        def _(sync):
            sync.wait_ge(vE.h, vE.n)
            sync.wait_ge(sE.h, sE.n)
            span = slice(t5 * 512, (t5 + 1) * 512)
            for p in range(3):
                for h in range(4):
                    rBst.sem(t5).inc(sync.dma_start(out=pr_scr[p][h, :, span], in_=s_out4[p][par][:, h, :]), 16)
            for h in range(4):
                rBst.sem(t5).inc(sync.dma_start(out=pr_scr[3][h, :, span], in_=s_wbt[:, h, :]), 16)
                rBst.sem(t5).inc(sync.dma_start(out=sg_scr[h, :, span], in_=s_sgt2[:, h, :]), 16)
            rBst.sem(t5).inc(sync.dma_start(out=zg_scr[span, :].rearrange("(tt p) m -> p tt m", p=128),
                           in_=s_zgt[:]), 16)

    for t5 in range(NT5B):
        b1_projs(t5)
        if t5 >= 1:
            b1_tail(t5 - 1)
    b1_tail(NT5B - 1)

    if taps:
        full_barrier()
        @blk.sync
        def _(sync):
            rOut.sem(0).inc(sync.dma_start(out=tp_xnT, in_=agout), 16)
            rOut.sem(0).inc(sync.dma_start(out=tp_fuse, in_=fuse_scr), 16)
            for p in range(4):
                rOut.sem(0).inc(sync.dma_start(out=tp_pr[p], in_=pr_scr[p]), 16)
            rOut.sem(0).inc(sync.dma_start(out=tp_zg, in_=zg_scr), 16)
            rOut.sem(0).inc(sync.dma_start(out=tp_sg, in_=sg_scr), 16)

    full_barrier()
    stB1.close()
    cur_stack[0] = ctx

    scope("pB2")
    stB2 = open_phase()
    rCh = Ring(nc, "rCh", 3)
    rCt = Ring(nc, "rCt", 3)
    rCz = Ring(nc, "rCz", 3)
    rO2 = Ring(nc, "rO2", 2)
    s_ones4 = sb("s_ones4", [4, 128], F32)
    s_qf = [sb(f"s_qf{i}", [128, 4, 128], BF16) for i in range(3)]
    s_kf = [sb(f"s_kf{i}", [128, 4, 128], BF16) for i in range(3)]
    s_wf = [sb(f"s_wbf{i}", [128, 4, 128], BF16) for i in range(3)]
    s_kt = [sb(f"s_kt{i}", [128, 4, 128], BF16) for i in range(3)]
    s_vt = [sb(f"s_vt{i}", [128, 4, 128], BF16) for i in range(3)]
    s_wt = [sb(f"s_wt{i}", [128, 4, 128], BF16) for i in range(3)]
    s_zg2 = [sb(f"s_zg2{i}", [128, 512], F32) for i in range(3)]
    s_sg2 = [sb(f"s_sg2{i}", [128, 4, 128], BF16) for i in range(3)]
    s_St = sb("s_St", [128, 4, 256], F32)
    s_Stb = sb("s_Stb", [128, 4, 256], BF16)
    s_SkT = sb("s_SkT", [128, 4, 128], BF16)
    s_sp = [sb(f"s_sp{i}", [128, 512], F32) for i in range(2)]
    s_Ltok = [sb(f"s_Ltok{i}", [128, 512], F32) for i in range(2)]
    s_Lf = [sb(f"s_Lf{i}", [128, 4, 128], F32) for i in range(2)]
    s_Ep = [sb(f"s_Ep{i}", [128, 512], F32) for i in range(2)]
    s_Emt = [sb(f"s_Emt{i}", [128, 512], F32) for i in range(2)]
    s_Emf = [sb(f"s_Emf{i}", [128, 4, 128], F32) for i in range(2)]
    s_Ptok = [sb(f"s_Ptok{i}", [128, 512], BF16) for i in range(2)]
    s_Pf = [sb(f"s_Pf{i}", [128, 4, 128], BF16) for i in range(2)]
    s_Am = [sb(f"s_Am{i}", [128, 4, 128], BF16) for i in range(2)]
    s_Bm = sb("s_Bm", [128, 4, 128], BF16)
    s_t1 = sb("s_t1", [128, 512], F32)
    s_scl = sb("s_scl", [128, 512], F32)
    s_expo = sb("s_expo", [128, 512], F32)
    s_dn = sb("s_dn", [128, 4, 1], F32)
    s_rdn = sb("s_rdn", [128, 4, 1], F32)
    s_t2 = sb("s_t2", [128, 512], F32)
    s_eopf = sb("s_eopf", [128, 512], F32)
    s_eop = sb("s_eop", [128, 4, 128], BF16)
    s_ehat = sb("s_ehat", [128, 4, 128], BF16)
    s_rc = sb("s_rc", [128, 4, 1], F32)
    s_rexp = sb("s_rexp", [128, 4, 1], F32)
    s_tail = sb("s_tail", [128, 4, 1], F32)
    s_osq = sb("s_osq", [128, 512], BF16)
    s_r2 = sb("s_r2", [1, 512], F32)
    s_rsb2 = sb("s_rsb2", [128, 512], F32)
    s_t3 = sb("s_t3", [128, 512], F32)
    s_ofin = [sb(f"s_ofin{i}", [128, 4, 128], BF16) for i in range(2)]
    p_misc = ps("p_misc", [128, 512])
    p_lf = ps("p_lf", [128, 4, 128])
    p_ab = [ps(f"p_ab{i}", [128, 4, 128]) for i in range(2)]
    p_so = ps("p_so", [128, 512])
    p_add = ps("p_add", [128, 512])
    p_et = ps("p_et", [128, 4, 128], BF16)

    rDw = Ring(nc, "rDw", 1)
    rDf = Ring(nc, "rDf", 2)
    rDst = Ring(nc, "rDst", 2)
    s_sdnb = sb("s_sdnb", [128, HID], F32)
    s_wdnp = sb("s_wdnp", [128, 64, 512], BF16)
    s_fuT = [sb(f"s_fuT{i}", [128, 64, 128], BF16) for i in range(2)]
    s_do = [sb(f"s_do{i}", [128, 512], F32) for i in range(2)]
    p_dn = ps("p_dn", [128, 512])

    @blk.gpsimd
    def _(gpsimd):
        gop(gpsimd, gpsimd.memset, s_ones4[:], 1.0)
        gop(gpsimd, gpsimd.memset, s_St[:], 0.0)
        gop(gpsimd, gpsimd.memset, s_Stb[:], 0.0)
        gop(gpsimd, gpsimd.memset, s_tail[:], 0.0)

    for dp in range(4):
        @blk.tensor
        def _(tensor, dp=dp):
            if dp == 0:
                rI.sem(0).wait(tensor)
            else:
                tensor.wait_ge(sE.h, sE.n)
            tE.inc(tensor.matmul(p_dn[:], s_ones1f[:], s_sdnr[:, dp * 512:(dp + 1) * 512],
                                 start=True, stop=True))
        @blk.scalar
        def _(scalar, dp=dp):
            tE.wait(scalar)
            sop(scalar, scalar.copy, out=s_sdnb[:, dp * 512:(dp + 1) * 512], in_=p_dn[:])
    sdnb_mark = sE.n

    dn_marks = {}
    dn_evac_marks = {}

    def dn_wload(dp):
        @blk.gpsimd
        def _(gpsimd):
            if dp > 0:
                gpsimd.wait_ge(tE.h, dn_marks[(dp - 1, NT1 - 1)])
            rDw.sem(0).inc(gpsimd.dma_start(out=s_wdnp[:, :, :], in_=wdn[dp]), 16)

    def dn_loads(u):
        if u >= 4 * NT1:
            return
        dp, it = u // NT1, u % NT1
        @blk.scalar
        def _(scalar):
            if u >= 2:
                scalar.wait_ge(tE.h, dn_marks.get((dp, it - 2) if it >= 2 else (dp - 1, NT1 - 2 + it), 0))
            rDf.sem(u).inc(scalar.dma_start(
                out=s_fuT[u % 2][:, :, :],
                in_=fuse_scr[:, it * 128:(it + 1) * 128].rearrange("(gi p) t -> p gi t", p=128)), 16)

    def dn_begin(u):
        if u >= 4 * NT1:
            return None
        return {"u": u, "dp": u // NT1, "it": u % NT1, "fslot": s_fuT[u % 2], "ic": 0}

    def dn_fill(st, n):
        if st is None or st["ic"] >= 64:
            return
        u, ic0 = st["u"], st["ic"]
        n = min(n, 64 - ic0)
        @blk.tensor
        def _(tensor):
            if ic0 == 0:
                rDw.sem(0).wait(tensor)
                rDf.sem(u).wait(tensor)
                if u == 0:
                    tensor.wait_ge(sE.h, sdnb_mark)
                if u >= 1:
                    tensor.wait_ge(vE.h, dn_evac_marks[u - 1])
            for ic in range(ic0, ic0 + n):
                mm = tensor.matmul(p_dn[:], st["fslot"][:, ic, :], s_wdnp[:, ic, :],
                                   start=(ic == 0), stop=(ic == 63))
            tE.inc(mm)
        st["ic"] = ic0 + n

    def dn_finish(st):
        if st is None:
            return
        dn_fill(st, 64 - st["ic"])
        u, dp, it = st["u"], st["dp"], st["it"]
        dn_marks[(dp, it)] = tE.n
        if it == NT1 - 1 and dp < 3:
            dn_wload(dp + 1)
        @blk.vector
        def _(vector):
            vector.wait_ge(tE.h, dn_marks[(dp, it)])
            vector.wait_ge(sE.h, sdnb_mark)
            if u >= 2:
                rDst.sem(u).wait(vector)
            vop(vector, vector.tensor_tensor, out=s_do[u % 2][:], in0=p_dn[:],
                in1=s_sdnb[:, dp * 512:(dp + 1) * 512], op=ALU.mult)
        dn_evac_marks[u] = vE.n
        @blk.gpsimd
        def _(gpsimd):
            gpsimd.wait_ge(vE.h, dn_evac_marks[u])
            rDst.sem(u).inc(gpsimd.dma_start(
                out=ffn_scr[it * 128:(it + 1) * 128, dp * 512:(dp + 1) * 512], in_=s_do[u % 2][:]), 16)

    dn_wload(0)
    dn_loads(0)
    dn_loads(1)
    chunk_done = {}
    clamp_marks = {}
    rc_marks = {}
    epi_marks = {}
    t1_marks = {}
    sp_marks = {}
    lcmm_marks = {}

    def b2_prefix(ch):
        par = ch % 2
        sl4 = ch % 3
        span = slice(ch * 128, (ch + 1) * 128)
        qf, kf, wf = s_qf[sl4], s_kf[sl4], s_wf[sl4]
        kt, vt, wt = s_kt[sl4], s_vt[sl4], s_wt[sl4]
        zg2, sg2 = s_zg2[sl4], s_sg2[sl4]

        @blk.sync
        def _(sync):
            if ch >= 3:
                d = chunk_done[ch - 3]
                sync.wait_ge(tE.h, d["t"]); sync.wait_ge(vE.h, d["v"]); sync.wait_ge(sE.h, d["s"])
            rCz.sem(ch).inc(sync.dma_start(out=zg2[:], in_=zg_scr[span, :]), 16)
            rCh.sem(ch).inc(sync.dma_start(out=qf[:, :, :], in_=pr_scr[0][:, :, span].rearrange("h c t -> c h t")), 16)
            rCh.sem(ch).inc(sync.dma_start(out=kf[:, :, :], in_=pr_scr[1][:, :, span].rearrange("h c t -> c h t")), 16)
            rCh.sem(ch).inc(sync.dma_start(out=wf[:, :, :], in_=pr_scr[3][:, :, span].rearrange("h c t -> c h t")), 16)
            rCh.sem(ch).inc(sync.dma_start(out=sg2[:, :, :], in_=sg_scr[:, :, span].rearrange("h c t -> c h t")), 16)
            rCt.sem(ch).inc(sync.dma_start(out=kt[:, :, :].rearrange("t h c -> t (h c)"),
                                           in_=pr_scr[1][:, :, span].rearrange("h c t -> (h c) t"),
                                           transpose=True), 16)
            rCt.sem(ch).inc(sync.dma_start(out=vt[:, :, :].rearrange("t h c -> t (h c)"),
                                           in_=pr_scr[2][:, :, span].rearrange("h c t -> (h c) t"),
                                           transpose=True), 16)
            rCt.sem(ch).inc(sync.dma_start(out=wt[:, :, :].rearrange("t h c -> t (h c)"),
                                           in_=pr_scr[3][:, :, span].rearrange("h c t -> (h c) t"),
                                           transpose=True), 16)

        @blk.scalar
        def _(scalar):
            rCz.sem(ch).wait(scalar)
            sop(scalar, scalar.activation, out=s_sp[par][:], in_=zg2[:], func=AF.Exp, scale=-1.0)
            sop(scalar, scalar.activation, out=s_sp[par][:], in_=s_sp[par][:], func=AF.Ln, bias=ccol("one"))
        sp_marks[ch] = sE.n

        @blk.tensor
        def _(tensor):
            tensor.wait_ge(sE.h, sp_marks[ch])
            if ch >= 1:
                tensor.wait_ge(vE.h, clamp_marks[ch - 1])
            if ch == 0:
                gE.wait(tensor)
            tE.inc(tensor.matmul(p_misc[:], s_mcx[:, 0:128], s_sp[par][:], start=True, stop=True))
            for h in range(4):
                tE.inc(tensor.matmul(p_lf[:, h, :], s_sp[par][:, h * 128:(h + 1) * 128],
                                     s_mcx[:, 0:128], start=True, stop=True))
        lmm_mark = tE.n

        @blk.vector
        def _(vector):
            vector.wait_ge(tE.h, lmm_mark)
            vop(vector, vector.tensor_scalar, out=s_Ltok[par][:], in0=p_misc[:], scalar1=80.0,
                scalar2=-80.0, op0=ALU.min, op1=ALU.max)
            vop(vector, vector.tensor_scalar, out=s_Lf[par][:, :, :], in0=p_lf[:, :, :], scalar1=80.0,
                scalar2=-80.0, op0=ALU.min, op1=ALU.max)
        clamp_marks[ch] = vE.n

        @blk.tensor
        def _(tensor):
            rCh.sem(ch).wait(tensor)
            if ch >= 2:
                tensor.wait_ge(vE.h, epi_marks[ch - 2])
            for h in range(4):
                tE.inc(tensor.matmul(p_ab[par][:, h, :], kf[:, h, :], qf[:, h, :], start=True, stop=True))
        amm_mark = tE.n
        @blk.vector
        def _(vector):
            vector.wait_ge(tE.h, amm_mark)
            vop(vector, vector.tensor_copy, out=s_Am[par][:, :, :], in_=p_ab[par][:, :, :])
        amcp_mark = vE.n
        @blk.gpsimd
        def _(gpsimd):
            gpsimd.wait_ge(vE.h, amcp_mark)
            gop(gpsimd, gpsimd.affine_select, out=s_Am[par][:, :, :], in_=s_Am[par][:, :, :],
                pattern=[[0, 4], [1, 128]], compare_op=ALU.is_ge, fill=0.0, base=0, channel_multiplier=-1)
        return {}

    def b2_suffix(ch, pm):
        par = ch % 2
        sl4 = ch % 3
        span = slice(ch * 128, (ch + 1) * 128)
        qf = s_qf[sl4]
        kt, vt = s_kt[sl4], s_vt[sl4]
        sg2 = s_sg2[sl4]
        ofin = s_ofin[par]

        s_base = sE.n
        rexp_mark = s_base + 1
        eexp_mark = s_base + 4
        expo_mark = s_base + 5
        eopf_mark = s_base + 6
        osq_mark = s_base + 7
        r2_mark = s_base + 9

        dnst = dn_begin(ch)
        dn_fill(dnst, 6)
        @blk.tensor
        def _(tensor):
            if ch >= 1:
                tensor.wait_ge(vE.h, t1_marks[ch - 1])
            for h in range(4):
                tE.inc(tensor.matmul(p_so[:, h:h + 1], s_sp[par][:, h * 128:(h + 1) * 128],
                                     s_mcx[:, 128:129], start=True, stop=True))
        lcmm_marks[ch] = tE.n
        @blk.vector
        def _(vector):
            vector.wait_ge(tE.h, lcmm_marks[ch])
            vop(vector, vector.tensor_tensor, out=s_rc[:, :, :],
                in0=bass.AP(p_so, 0, [[512, 128], [1, 4], [1, 1]]),
                in1=s_tail[:, :, :], op=ALU.add)
        rc_marks[ch] = vE.n
        @blk.vector
        def _(vector):
            vector.wait_ge(sE.h, rexp_mark)
            if ch > 0:
                for h in range(4):
                    vop(vector, vector.tensor_scalar, out=s_St[:, h, :], in0=s_St[:, h, :],
                        scalar1=s_rexp[:, h, 0:1].opt(), scalar2=None, op0=ALU.mult)
            vop(vector, vector.tensor_copy, out=s_Stb[:, :, :], in_=s_St[:, :, :])
        resc_mark = vE.n
        dn_fill(dnst, 6)
        @blk.tensor
        def _(tensor):
            tensor.wait_ge(vE.h, resc_mark)
            if ch >= 1:
                tensor.wait_ge(vE.h, ehat_marks[ch - 1])
            for h in range(4):
                tE.inc(tensor.transpose(p_et[:, h, :], s_Stb[:, h, 0:128], s_identb[:]))
        sktr_mark = tE.n
        @blk.vector
        def _(vector):
            vector.wait_ge(tE.h, sktr_mark)
            vop(vector, vector.tensor_copy, out=s_SkT[:, :, :], in_=p_et[:, :, :])
        skt_mark = vE.n
        wt, wf = s_wt[sl4], s_wf[sl4]
        @blk.vector
        def _(vector):
            rCt.sem(ch).wait(vector)
            vector.wait_ge(sE.h, eexp_mark)
            vop(vector, vector.tensor_tensor, out=s_Ptok[par][:],
                in0=wt[:, :, :].rearrange("s h m -> s (h m)"), in1=s_Emt[par][:], op=ALU.mult)
            vop(vector, vector.tensor_tensor, out=s_Pf[par][:, :, :], in0=wf[:, :, :],
                in1=s_Emf[par][:, :, :], op=ALU.mult)
        p_mark = vE.n

        dn_fill(dnst, 8)
        @blk.tensor
        def _(tensor):
            tensor.wait_ge(vE.h, skt_mark)
            tensor.wait_ge(gE.h, gE.n)
            tensor.wait_ge(vE.h, p_mark)
            for h in range(4):
                hs = slice(h * 128, (h + 1) * 128)
                tensor.matmul(p_so[:, hs], qf[:, h, :], s_SkT[:, h, :], start=True, stop=False)
                tE.inc(tensor.matmul(p_so[:, hs], s_Am[par][:, h, :], s_Ptok[par][:, hs], start=False, stop=True))
        smm_mark = tE.n

        @blk.vector
        def _(vector):
            vector.wait_ge(tE.h, smm_mark)
            vop(vector, vector.tensor_tensor, out=s_t1[:], in0=p_so[:], in1=s_Ep[par][:], op=ALU.mult)
            vop(vector, vector.tensor_scalar, out=s_scl[:], in0=s_t1[:], scalar1=SCALE,
                scalar2=70.0, op0=ALU.mult, op1=ALU.min)
        t1_marks[ch] = vE.n
        @blk.vector
        def _(vector):
            vector.wait_ge(sE.h, expo_mark)
            vop(vector, vector.tensor_reduce, out=s_dn[:, :, :],
                in_=s_expo[:].rearrange("t (h m) -> t h m", m=128), axis=AX.X, op=ALU.add)
            vop(vector, vector.reciprocal, out=s_rdn[:, :, :], in_=s_dn[:, :, :])
            vop(vector, vector.tensor_tensor, out=s_t2[:], in0=s_scl[:], in1=s_Ltok[par][:], op=ALU.subtract)
            vop(vector, vector.tensor_scalar, out=s_t2[:], in0=s_t2[:], scalar1=85.0, scalar2=None,
                op0=ALU.min)
        t2_mark = vE.n
        @blk.vector
        def _(vector):
            vector.wait_ge(sE.h, eopf_mark)
            for h in range(4):
                hs = slice(h * 128, (h + 1) * 128)
                vop(vector, vector.tensor_scalar, out=s_eop[:, h, :], in0=s_eopf[:, hs],
                    scalar1=s_rdn[:, h, 0:1].opt(), scalar2=None, op0=ALU.mult)
        eop_mark = vE.n
        dn_fill(dnst, 8)
        @blk.tensor
        def _(tensor):
            tensor.wait_ge(vE.h, eop_mark)
            tensor.wait_ge(vE.h, skt_mark)
            for h in range(4):
                tE.inc(tensor.transpose(p_et[:, h, :], s_eop[:, h, :], s_identb[:]))
        etr_mark = tE.n
        @blk.vector
        def _(vector):
            vector.wait_ge(tE.h, etr_mark)
            vop(vector, vector.tensor_copy, out=s_ehat[:, :, :], in_=p_et[:, :, :])
        ehat_marks[ch] = vE.n

        dn_fill(dnst, 6)
        @blk.tensor
        def _(tensor):
            tensor.wait_ge(vE.h, ehat_marks[ch])
            tensor.wait_ge(vE.h, p_mark)
            for h in range(4):
                tE.inc(tensor.matmul(p_ab[par][:, h, :], s_Pf[par][:, h, :], s_ehat[:, h, :],
                                     start=True, stop=True))
        bmm_mark = tE.n
        @blk.vector
        def _(vector):
            vector.wait_ge(tE.h, bmm_mark)
            vop(vector, vector.tensor_copy, out=s_Bm[:, :, :], in_=p_ab[par][:, :, :])
        bmcp_mark = vE.n
        @blk.gpsimd
        def _(gpsimd):
            gpsimd.wait_ge(vE.h, bmcp_mark)
            gop(gpsimd, gpsimd.affine_select, out=s_Bm[:, :, :], in_=s_Bm[:, :, :],
                pattern=[[0, 4], [1, 128]], compare_op=ALU.is_ge, fill=0.0, base=0, channel_multiplier=-1)
        dn_fill(dnst, 8)
        @blk.tensor
        def _(tensor):
            tensor.wait_ge(gE.h, gE.n)
            for h in range(4):
                tensor.matmul(p_ab[par][:, h, :], s_Stb[:, h, 128:256], s_ehat[:, h, :],
                              start=True, stop=False)
                tE.inc(tensor.matmul(p_ab[par][:, h, :], vt[:, h, :], s_Bm[:, h, :],
                                     start=False, stop=True))
        omm_mark = tE.n
        dn_fill(dnst, 8)
        @blk.tensor
        def _(tensor):
            tensor.wait_ge(sE.h, osq_mark)
            if ch >= 1:
                tensor.wait_ge(vE.h, svadd_marks[ch - 1])
            tE.inc(tensor.matmul(p_add[0:1, 0:512], s_onescb[:], s_osq[:], start=True, stop=True))
        msq_mark = tE.n
        dn_fill(dnst, 6)
        @blk.tensor
        def _(tensor):
            tensor.wait_ge(sE.h, r2_mark)
            tE.inc(tensor.matmul(p_add[:], s_ones1f[:], s_r2[:], start=True, stop=True))
        rsb_mark = tE.n
        @blk.vector
        def _(vector):
            vector.wait_ge(tE.h, rsb_mark)
            if ch >= 2:
                rO2.sem(ch).wait(vector)
            vop(vector, vector.tensor_copy, out=s_rsb2[:], in_=p_add[:])
            vop(vector, vector.tensor_tensor, out=s_t3[:],
                in0=p_ab[par][:, :, :].rearrange("e h t -> e (h t)"), in1=s_rsb2[:], op=ALU.mult)
            vop(vector, vector.tensor_tensor, out=ofin[:, :, :].rearrange("e h t -> e (h t)"),
                in0=s_t3[:], in1=sg2[:, :, :].rearrange("e h t -> e (h t)"), op=ALU.mult)
        epi_marks[ch] = vE.n

        dn_fill(dnst, 4)
        @blk.tensor
        def _(tensor):
            tensor.wait_ge(vE.h, epi_marks[ch])
            for h in range(4):
                hs = slice(h * 128, (h + 1) * 128)
                tE.inc(tensor.matmul(p_add[:, hs], s_Ptok[par][:, hs], kt[:, h, :],
                                     start=True, stop=True))
        kadd_mark = tE.n
        @blk.vector
        def _(vector):
            vector.wait_ge(tE.h, kadd_mark)
            vop(vector, vector.tensor_tensor, out=s_St[:, :, 0:128], in0=s_St[:, :, 0:128],
                in1=p_add[:].rearrange("m (h d) -> m h d", d=128), op=ALU.add)
        skadd_mark = vE.n
        dn_fill(dnst, 4)
        @blk.tensor
        def _(tensor):
            tensor.wait_ge(vE.h, skadd_mark)
            for h in range(4):
                hs = slice(h * 128, (h + 1) * 128)
                tE.inc(tensor.matmul(p_add[:, hs], s_Ptok[par][:, hs], vt[:, h, :],
                                     start=True, stop=True))
        vadd_mark = tE.n
        @blk.vector
        def _(vector):
            vector.wait_ge(tE.h, vadd_mark)
            vop(vector, vector.tensor_tensor, out=s_St[:, :, 128:256], in0=s_St[:, :, 128:256],
                in1=p_add[:].rearrange("m (h e) -> m h e", e=128), op=ALU.add)
            vop(vector, vector.tensor_copy, out=s_tail[:, :, :], in_=s_Lf[par][:, :, 127:128])
        svadd_marks[ch] = vE.n

        @blk.scalar
        def _(scalar):
            scalar.wait_ge(vE.h, rc_marks[ch])
            sop(scalar, scalar.activation, out=s_rexp[:, :, :], in_=s_rc[:, :, :], func=AF.Exp, scale=-1.0)
            scalar.wait_ge(vE.h, clamp_marks[ch])
            sop(scalar, scalar.activation, out=s_Ep[par][:], in_=s_Ltok[par][:], func=AF.Exp, scale=-1.0)
            sop(scalar, scalar.activation, out=s_Emt[par][:], in_=s_Ltok[par][:], func=AF.Exp)
            sop(scalar, scalar.activation, out=s_Emf[par][:, :, :], in_=s_Lf[par][:, :, :], func=AF.Exp)
            scalar.wait_ge(vE.h, t1_marks[ch])
            sop(scalar, scalar.activation, out=s_expo[:], in_=s_scl[:], func=AF.Exp, bias=ccol("neg_shift"))
            scalar.wait_ge(vE.h, t2_mark)
            sop(scalar, scalar.activation, out=s_eopf[:], in_=s_t2[:], func=AF.Exp, bias=ccol("neg_shift"))
            scalar.wait_ge(tE.h, omm_mark)
            sop(scalar, scalar.square, s_osq[:], p_ab[par][:, :, :].rearrange("e h t -> e (h t)"))
            scalar.wait_ge(tE.h, msq_mark)
            sop(scalar, scalar.activation, out=s_r2[:], in_=p_add[0:1, 0:512], func=AF.Ln,
                scale=1.0 / 128.0, bias=ccol("rms_eps")[0:1, :])
            sop(scalar, scalar.activation, out=s_r2[:], in_=s_r2[:], func=AF.Exp, scale=-0.5)
        assert sE.n == s_base + 9, (sE.n, s_base)

        jq = (ch * 128) // S
        cspan = slice((ch * 128) % S, (ch * 128) % S + 128)
        @blk.gpsimd
        def _(gpsimd):
            gpsimd.wait_ge(vE.h, epi_marks[ch])
            rO2.sem(ch).inc(gpsimd.dma_start(
                out=agin2[jq * 512:(jq + 1) * 512, cspan].rearrange("(h c) t -> c h t", c=128),
                in_=ofin[:, :, :]), 16)
        dn_finish(dnst)
        dn_loads(ch + 2)
        chunk_done[ch] = {"t": tE.n, "v": vE.n, "s": sE.n}

    ehat_marks = {}
    svadd_marks = {}
    pm_store = {}
    for ch in range(NCH):
        pm_store[ch] = b2_prefix(ch)
        if ch >= 1:
            b2_suffix(ch - 1, pm_store[ch - 1])
    b2_suffix(NCH - 1, pm_store[NCH - 1])

    @blk.gpsimd
    def _(gpsimd):
        for ch in range(max(0, NCH - 2), NCH):
            rO2.sem(ch).wait(gpsimd)
        gpsimd.collective_compute("AllGather", ALU.bypass, replica_groups=rg,
                                  ins=[agin2.opt()], outs=[agout2.opt()]).then_inc(cS.h, 1)
    cS.n += 1

    full_barrier()
    stB2.close()

    scope("pC2")
    stC2 = open_phase()
    rCw = Ring(nc, "rCw", 1)
    rCq = Ring(nc, "rCq", 1)
    rCo = Ring(nc, "rCo", 2)
    rCx = Ring(nc, "rCx", 2)
    rCf = Ring(nc, "rCf", 2)
    rOq = Ring(nc, "rOq", 2)
    s_qoff = sb("s_qoff", [1, 1], mybir.dt.uint32)
    s_wos = sb("s_wos", [128, 16, 2048], BF16)
    s_ot = [sb(f"s_ot{i}", [128, 16, 128], BF16) for i in range(2)]
    s_xc = [sb(f"s_xc{i}", [128, HID], F32) for i in range(2)]
    s_fc = [sb(f"s_fc{i}", [128, HID], F32) for i in range(2)]
    s_oq = [sb(f"s_oq{i}", [128, HID], F32) for i in range(2)]
    p_c2 = [ps(f"p_c2{i}", [128, 512]) for i in range(2)]

    @blk.gpsimd
    def _(gpsimd):
        rCw.sem(0).inc(gpsimd.dma_start(out=s_wos[:, :, :], in_=wo), 16)
    @blk.sync
    def _(sync):
        rCq.sem(0).inc(sync.dma_start(out=s_qoff[:], in_=qoff), 16)
    rSel = Ring(nc, "rSel", 1)
    @blk.gpsimd
    def _(gpsimd):
        from concourse.expressions import RuntimeValue
        rCq.sem(0).wait(gpsimd)
        cS.wait(gpsimd)
        rq = gpsimd.alloc_register("qreg")
        gpsimd.reg_load(rq, s_qoff[0:1, 0:1])
        gpsimd.reg_mul(rq, rq, 512 * S)
        for g in range(4):
            dyn = bass.AP(agout2.tensor, RuntimeValue(rq), [[S, 512], [1, S]])
            rSel.sem(0).inc(gpsimd.dma_start(out=osel[g * 512:(g + 1) * 512, :], in_=dyn), 16)
            if g < 3:
                gpsimd.reg_add(rq, rq, HID * S)
    @blk.vector
    def _(vector):
        rCw.sem(0).wait(vector)
        rI.sem(0).wait(vector)
        vop(vector, vector.tensor_scalar, out=s_wos[:, :, :], in0=s_wos[:, :, :],
            scalar1=s_normw[:, 0:1], scalar2=None, op0=ALU.mult)
    wos_mark = vE.n

    c2_marks = {}
    for it in range(NT1):
        ot, xc, fc, oq = s_ot[it % 2], s_xc[it % 2], s_fc[it % 2], s_oq[it % 2]
        @blk.sync
        def _(sync):
            if it >= 2:
                sync.wait_ge(tE.h, c2_marks[it - 2])
            rSel.sem(0).wait(sync)
            for g in range(4):
                rCo.sem(it).inc(sync.dma_start(
                    out=ot[:, g * 4:(g + 1) * 4, :],
                    in_=osel[g * 512:(g + 1) * 512,
                             it * 128:(it + 1) * 128].rearrange("(dc p) t -> p dc t", p=128)), 16)
            rCx.sem(it).inc(sync.dma_start(out=xc[:], in_=x_q[it * 128:(it + 1) * 128, :]), 16)
            rCf.sem(it).inc(sync.dma_start(out=fc[:], in_=ffn_scr[it * 128:(it + 1) * 128, :]), 16)
        for dp in range(4):
            @blk.tensor
            def _(tensor, dp=dp):
                rCo.sem(it).wait(tensor)
                if it == 0 and dp == 0:
                    tensor.wait_ge(vE.h, wos_mark)
                if it > 0 or dp > 1:
                    tensor.wait_ge(vE.h, vE.n)
                for dc in range(16):
                    mm = tensor.matmul(p_c2[dp % 2][:], ot[:, dc, :],
                                       s_wos[:, dc, dp * 512:(dp + 1) * 512],
                                       start=(dc == 0), stop=(dc == 15))
                tE.inc(mm)
            @blk.vector
            def _(vector, dp=dp):
                tE.wait(vector)
                rCx.sem(it).wait(vector)
                rCf.sem(it).wait(vector)
                if it >= 2 and dp == 0:
                    rOq.sem(it).wait(vector)
                dsl = slice(dp * 512, (dp + 1) * 512)
                vop(vector, vector.tensor_tensor, out=oq[:, dsl], in0=p_c2[dp % 2][:],
                    in1=fc[:, dsl], op=ALU.add)
                vop(vector, vector.tensor_tensor, out=oq[:, dsl], in0=oq[:, dsl],
                    in1=xc[:, dsl], op=ALU.add)
        c2_marks[it] = tE.n
        @blk.sync
        def _(sync):
            sync.wait_ge(vE.h, vE.n)
            rOq.sem(it).inc(sync.dma_start(out=out_q[it * 128:(it + 1) * 128, :], in_=oq[:]), 16)

    if taps:
        full_barrier()
        @blk.sync
        def _(sync):
            rOut.sem(0).inc(sync.dma_start(out=tp_o, in_=agin2), 16)
            rOut.sem(0).inc(sync.dma_start(out=tp_ffn, in_=ffn_scr), 16)

    full_barrier()
    @blk.sync
    def _(sync):
        for it in range(NT1):
            rOq.sem(it).wait(sync)
        rOut.sem(0).wait(sync)
    stC2.close()

    if scope_stack:
        nc.leave_named_scope(scope_stack[-1][0], scope_stack[-1][1], False)
        scope_stack.pop()
    ctx.close()
    return nc


def shard_inputs(inputs, T=16384):
    S = T // 4
    f32 = np.float32
    x = np.ascontiguousarray(np.asarray(inputs["L0_attn_hidden_states"], f32).reshape(2, -1, HID)[:, :T])
    NIB = INTER // 128
    cq, ck, cv = (np.asarray(inputs[f"L0_attn_c{n}"], f32) for n in "qkv")

    def rt16(w):
        C = w.shape[1]
        return np.ascontiguousarray(w.reshape(16, 128, C).transpose(1, 0, 2))

    NIBv = INTER // 128
    wgt_r = np.ascontiguousarray(np.asarray(inputs["L0_ffn_w_gate"])
                                 .reshape(16, 128, NIBv, 128).transpose(2, 1, 0, 3))
    wup_r = np.ascontiguousarray(np.asarray(inputs["L0_ffn_w_up"])
                                 .reshape(16, 128, NIBv, 128).transpose(2, 1, 0, 3))
    wdn_r = np.ascontiguousarray(np.asarray(inputs["L0_ffn_w_down"])
                                 .reshape(64, 128, 4, 512).transpose(2, 1, 0, 3))
    maps = []
    for c in range(8):
        b, hg = c // 4, c % 4
        sl = slice(hg * 512, (hg + 1) * 512)
        cwa = np.stack([cq[sl], ck[sl], cv[sl], cv[sl]])
        cwa = cwa.reshape(4, 4, 128, 4).transpose(2, 0, 1, 3)
        m = {
            "x_q": np.ascontiguousarray(x[b, hg * S:(hg + 1) * S]),
            "wq": rt16(np.asarray(inputs["L0_attn_Wq"], f32)[:, sl]),
            "wk": rt16(np.asarray(inputs["L0_attn_Wk"], f32)[:, sl]),
            "wv": rt16(np.asarray(inputs["L0_attn_Wv"], f32)[:, sl]),
            "ww": rt16(np.asarray(inputs["L0_attn_Ww"], f32)[:, sl]),
            "cw": np.ascontiguousarray(cwa, f32),
            "wf1": rt16(np.asarray(inputs["L0_attn_Wf1"], f32)),
            "wf2": np.ascontiguousarray(np.asarray(inputs["L0_attn_Wf2"], f32)[:, sl]),
            "wbm": rt16(np.asarray(inputs["L0_attn_Wb"], f32)[:, hg * 4:(hg + 1) * 4]),
            "wg1": rt16(np.asarray(inputs["L0_attn_Wg1"], f32)),
            "wg2": np.ascontiguousarray(np.asarray(inputs["L0_attn_Wg2"], f32)[:, sl]),
            "bg2": np.ascontiguousarray(np.asarray(inputs["L0_attn_bg2"], f32)[sl].reshape(4, 128).T),
            "normw": np.ascontiguousarray(np.asarray(inputs["L0_attn_norm_w"], f32).reshape(128, 1)),
            "wo": rt16(np.asarray(inputs["L0_attn_Wo"], f32)),
            "wgt": wgt_r, "wup": wup_r, "wdn": wdn_r,
            "sgt": np.ascontiguousarray(np.asarray(inputs["L0_ffn_s_gate"], f32).reshape(NIB, 128).T),
            "sup": np.ascontiguousarray(np.asarray(inputs["L0_ffn_s_up"], f32).reshape(NIB, 128).T),
            "sdn": np.ascontiguousarray(np.asarray(inputs["L0_ffn_s_down"], f32).reshape(1, HID)),
            "qoff": np.array([[hg]], np.uint32),
        }
        maps.append(m)
    return maps


def kernel(**inputs):
    T = 16384
    nc = build_program(T=T, taps=False)
    in_maps = shard_inputs(inputs, T)
    res = run_bass_kernel_spmd(nc, in_maps, core_ids=list(range(8)))
    S = T // 4
    out = np.zeros((2, T, HID), np.float32)
    for c in range(8):
        b, hg = c // 4, c % 4
        out[b, hg * S:(hg + 1) * S] = res.results[c]["out_q"]
    return out


# version 10 (final)
# speedup vs baseline: 1.3410x; 1.0208x over previous
import numpy as np
import ml_dtypes
import concourse.bass as bass
import concourse.mybir as mybir
from concourse.bass_utils import run_bass_kernel_spmd

F32 = mybir.dt.float32
BF16 = mybir.dt.bfloat16
I8 = mybir.dt.int8
AF = mybir.ActivationFunctionType
ALU = mybir.AluOpType
AX = mybir.AxisListType

HID = 2048
HPC = 4
INTER = 8192
CCH = 128
MID = 64
SCALE = 128.0 ** -0.5


ALL_SEMS = []


class Sem:
    def __init__(self, nc, name):
        self.h = nc.alloc_semaphore(name=name)
        self.n = 0
        ALL_SEMS.append(self)

    def inc(self, inst, k=1):
        inst.then_inc(self.h, k)
        self.n += k
        return inst

    def wait(self, eng, val=None):
        eng.wait_ge(self.h, self.n if val is None else val)


class Ring:
    def __init__(self, nc, name, slots):
        self.s = [Sem(nc, f"{name}{i}") for i in range(slots)]
        self.k = slots

    def sem(self, i):
        return self.s[i % self.k]


def build_program(T=16384, taps=False):
    ALL_SEMS.clear()
    S = T // 4
    NT1 = S // 128
    NT5Q = S // 512
    NT5B = T // 512
    NCH = T // CCH
    NIB = INTER // 128

    nc = bass.Bass("TRN2", target_bir_lowering=False)
    ap = lambda t: t.ap()
    x_q = ap(nc.dram_tensor("x_q", [S, HID], F32, kind="ExternalInput"))
    wproj = [ap(nc.dram_tensor(n, [128, 16, 512], F32, kind="ExternalInput")) for n in ("wq", "wk", "wv", "ww")]
    cw = ap(nc.dram_tensor("cw", [128, 4, 4, 4], F32, kind="ExternalInput"))
    wf1 = ap(nc.dram_tensor("wf1", [128, 16, 128], F32, kind="ExternalInput"))
    wf2 = ap(nc.dram_tensor("wf2", [128, 512], F32, kind="ExternalInput"))
    wbm = ap(nc.dram_tensor("wbm", [128, 16, 4], F32, kind="ExternalInput"))
    wg1 = ap(nc.dram_tensor("wg1", [128, 16, 128], F32, kind="ExternalInput"))
    wg2 = ap(nc.dram_tensor("wg2", [128, 512], F32, kind="ExternalInput"))
    bg2 = ap(nc.dram_tensor("bg2", [128, 4], F32, kind="ExternalInput"))
    normw = ap(nc.dram_tensor("normw", [128, 1], F32, kind="ExternalInput"))
    wo = ap(nc.dram_tensor("wo", [128, 16, HID], F32, kind="ExternalInput"))
    wgt = ap(nc.dram_tensor("wgt", [NIB, 128, 16, 128], I8, kind="ExternalInput"))
    wup = ap(nc.dram_tensor("wup", [NIB, 128, 16, 128], I8, kind="ExternalInput"))
    wdn = ap(nc.dram_tensor("wdn", [4, 128, 64, 512], I8, kind="ExternalInput"))
    sgt = ap(nc.dram_tensor("sgt", [128, NIB], F32, kind="ExternalInput"))
    sup = ap(nc.dram_tensor("sup", [128, NIB], F32, kind="ExternalInput"))
    sdn = ap(nc.dram_tensor("sdn", [1, HID], F32, kind="ExternalInput"))
    qoff = ap(nc.dram_tensor("qoff", [1, 1], mybir.dt.uint32, kind="ExternalInput"))
    out_q = ap(nc.dram_tensor("out_q", [S, HID], F32, kind="ExternalOutput"))

    AGR = HID + 256
    agin = ap(nc.dram_tensor("agin", [AGR, S], BF16, kind="Internal"))
    agout = ap(nc.dram_tensor("agout", [4 * AGR, S], BF16, kind="Internal"))
    fuse_scr = ap(nc.dram_tensor("fuse_scr", [INTER, S], BF16, kind="Internal"))
    pr_scr = [ap(nc.dram_tensor(f"pr{i}_scr", [HPC, 128, T], BF16, kind="Internal")) for i in range(4)]
    zg_scr = ap(nc.dram_tensor("zg_scr", [T, 512], F32, kind="Internal"))
    sg_scr = ap(nc.dram_tensor("sg_scr", [HPC, 128, T], BF16, kind="Internal"))
    agin2 = ap(nc.dram_tensor("agin2", [HID, S], BF16, kind="Internal"))
    agout2 = ap(nc.dram_tensor("agout2", [4 * HID, S], BF16, kind="Internal"))
    osel = ap(nc.dram_tensor("osel", [HID, S], BF16, kind="Internal"))
    ffn_scr = ap(nc.dram_tensor("ffn_scr", [S, HID], F32, kind="Internal"))
    if taps:
        tp_xnT = ap(nc.dram_tensor("tp_xnT", [4 * AGR, S], BF16, kind="ExternalOutput"))
        tp_fuse = ap(nc.dram_tensor("tp_fuse", [INTER, S], BF16, kind="ExternalOutput"))
        tp_pr = [ap(nc.dram_tensor(f"tp_pr{i}", [HPC, 128, T], BF16, kind="ExternalOutput")) for i in range(4)]
        tp_zg = ap(nc.dram_tensor("tp_zg", [T, 512], F32, kind="ExternalOutput"))
        tp_sg = ap(nc.dram_tensor("tp_sg", [HPC, 128, T], BF16, kind="ExternalOutput"))
        tp_o = ap(nc.dram_tensor("tp_o", [HID, S], BF16, kind="ExternalOutput"))
        tp_ffn = ap(nc.dram_tensor("tp_ffn", [S, HID], F32, kind="ExternalOutput"))

    rg = [[0, 1, 2, 3], [4, 5, 6, 7]]

    from contextlib import ExitStack
    ctx = ExitStack()
    cur_stack = [ctx]
    sb = lambda name, shape, dt: cur_stack[0].enter_context(nc.sbuf_tensor(name, shape, dt))
    ps = lambda name, shape, dt=F32: cur_stack[0].enter_context(nc.psum_tensor(name, shape, dt))

    s_identb = sb("s_identb", [128, 128], BF16)
    s_identf = sb("s_identf", [128, 128], F32)
    s_mcx = sb("s_mcx", [128, 132], F32)
    s_ones1f = sb("s_ones1f", [1, 128], F32)
    s_ones1b = sb("s_ones1b", [1, 128], BF16)
    s_onescf = sb("s_onescf", [128, 1], F32)
    s_onescb = sb("s_onescb", [128, 1], BF16)
    s_sel4 = sb("s_sel4", [4, 512], F32)
    s_cw = sb("s_cw", [128, 64], F32)
    s_sgt = sb("s_sgt", [128, NIB], F32)
    s_sup = sb("s_sup", [128, NIB], F32)
    s_sdnr = sb("s_sdnr", [1, HID], F32)
    s_normw = sb("s_normw", [128, 1], F32)
    s_bg2h = sb("s_bg2h", [128, 4], F32)
    CVALS = {"ln_eps": 1e-5, "l2_eps": 1e-6, "rms_eps": 1e-5, "one": 1.0,
             "neg_shift": -15.0, "clamp_l": 80.0, "neg_clamp_l": -80.0}
    s_const = sb("s_const", [128, len(CVALS)], F32)
    CIDX = {k: i for i, k in enumerate(CVALS)}
    ccol = lambda k: s_const[:, CIDX[k]:CIDX[k] + 1]

    tE = Sem(nc, "tE")
    vE = Sem(nc, "vE")
    sE = Sem(nc, "sE")
    gE = Sem(nc, "gE")
    cS = Sem(nc, "cS")
    rI = Ring(nc, "rI", 1)
    rWf = Ring(nc, "rWf", 1)
    rX = Ring(nc, "rX", 2)
    rAg = Ring(nc, "rAg", 1)
    rA2w = Ring(nc, "rA2w", 2)
    rFu = Ring(nc, "rFu", 2)
    rB1w = Ring(nc, "rB1w", 1)
    rBx = Ring(nc, "rBx", 2)
    rBst = Ring(nc, "rBst", 2)
    rOut = Ring(nc, "rOut", 1)

    def vop(vector, fn, *a, **k):
        vector.wait_ge(vE.h, vE.n)
        return vE.inc(fn(*a, **k))

    def sop(scalar, fn, *a, **k):
        scalar.wait_ge(sE.h, sE.n)
        return sE.inc(fn(*a, **k))

    def gop(gpsimd, fn, *a, **k):
        gpsimd.wait_ge(gE.h, gE.n)
        return gE.inc(fn(*a, **k))

    blk = ctx.enter_context(nc.Block())

    def full_barrier():
        for mk in ("tensor", "vector", "scalar", "gpsimd", "sync"):
            @getattr(blk, mk)
            def _(eng):
                for s in ALL_SEMS:
                    if s.n:
                        eng.wait_ge(s.h, s.n)

    def open_phase():
        st = ExitStack()
        cur_stack[0] = st
        return st

    scope_stack = []
    def scope(name):
        if scope_stack:
            nc.leave_named_scope(scope_stack[-1][0], scope_stack[-1][1], False)
            scope_stack.pop()
        sid, _ = nc.enter_named_scope(name, False)
        scope_stack.append((name, sid))

    scope("pA")
    @blk.sync
    def _(sync):
        for d, s_ in [(cw.rearrange("c p h t -> c (p h t)"), s_cw), (sgt, s_sgt), (sup, s_sup),
                      (sdn, s_sdnr), (normw, s_normw), (bg2, s_bg2h)]:
            rI.sem(0).inc(sync.dma_start(out=s_[:], in_=d), 16)

    @blk.gpsimd
    def _(gpsimd):
        gop(gpsimd, gpsimd.memset, s_identb[:], 1.0)
        gop(gpsimd, gpsimd.affine_select, out=s_identb[:], in_=s_identb[:], pattern=[[-1, 128]],
            compare_op=ALU.is_ge, fill=0.0, base=0, channel_multiplier=1)
        gop(gpsimd, gpsimd.affine_select, out=s_identb[:], in_=s_identb[:], pattern=[[1, 128]],
            compare_op=ALU.is_ge, fill=0.0, base=0, channel_multiplier=-1)
        gop(gpsimd, gpsimd.memset, s_identf[:], 1.0)
        gop(gpsimd, gpsimd.affine_select, out=s_identf[:], in_=s_identf[:], pattern=[[-1, 128]],
            compare_op=ALU.is_ge, fill=0.0, base=0, channel_multiplier=1)
        gop(gpsimd, gpsimd.affine_select, out=s_identf[:], in_=s_identf[:], pattern=[[1, 128]],
            compare_op=ALU.is_ge, fill=0.0, base=0, channel_multiplier=-1)
        gop(gpsimd, gpsimd.memset, s_mcx[:, 0:128], 1.0)
        gop(gpsimd, gpsimd.affine_select, out=s_mcx[:, 0:128], in_=s_mcx[:, 0:128], pattern=[[1, 128]],
            compare_op=ALU.is_ge, fill=0.0, base=0, channel_multiplier=-1)
        gop(gpsimd, gpsimd.memset, s_mcx[0:MID, 128:129], 1.0)
        gop(gpsimd, gpsimd.memset, s_mcx[MID:128, 128:129], 0.0)
        gop(gpsimd, gpsimd.memset, s_mcx[0:128, 129:130], 0.0)
        gop(gpsimd, gpsimd.memset, s_mcx[64:128, 129:130], 0.0)
        gop(gpsimd, gpsimd.memset, s_ones1f[:], 1.0)
        gop(gpsimd, gpsimd.memset, s_ones1b[:], 1.0)
        gop(gpsimd, gpsimd.memset, s_onescf[:], 1.0)
        gop(gpsimd, gpsimd.memset, s_onescb[:], 1.0)
        for k, v in CVALS.items():
            gop(gpsimd, gpsimd.memset, ccol(k), v)
        gop(gpsimd, gpsimd.memset, s_sel4[:], 1.0)
        sel3 = s_sel4.rearrange("k (h j) -> k h j", j=128)
        gop(gpsimd, gpsimd.affine_select, out=sel3, in_=sel3, pattern=[[1, 4], [0, 128]],
            compare_op=ALU.is_ge, fill=0.0, base=0, channel_multiplier=-1)
        gop(gpsimd, gpsimd.affine_select, out=sel3, in_=sel3, pattern=[[-1, 4], [0, 128]],
            compare_op=ALU.is_ge, fill=0.0, base=0, channel_multiplier=1)

    @blk.vector
    def _(vector):
        rI.sem(0).wait(vector)
        gE.wait(vector)
        vop(vector, vector.tensor_scalar, out=s_mcx[:, 0:128], in0=s_mcx[:, 0:128],
            scalar1=s_mcx[:, 128:129], scalar2=None, op0=ALU.subtract)
        vop(vector, vector.tensor_scalar_mul, s_bg2h[:], s_bg2h[:], 0.5)
    init_v_mark = vE.n

    stA = open_phase()
    s_x = [sb(f"s_x{i}", [128, HID], F32) for i in range(2)]
    s_xnt_all = sb("s_xnt_all", [128, 16, S], BF16)
    s_f1t = sb("s_f1t", [128, S], BF16)
    s_g1t = sb("s_g1t", [128, S], BF16)
    s_xn = sb("s_xn", [128, HID], BF16)
    s_mcol = sb("s_mcol", [128, 1], F32)
    s_vcol = sb("s_vcol", [128, 1], F32)
    s_rstd = sb("s_rstd", [128, 1], F32)
    s_wf1 = sb("s_wf1", [128, 16, 128], BF16)
    s_wg1 = sb("s_wg1", [128, 16, 128], BF16)
    p_tr = ps("p_tr", [128, 4, 128], BF16)
    p_f1 = ps("p_f1", [128, 128])
    p_g1 = ps("p_g1", [128, 128])

    @blk.gpsimd
    def _(gpsimd):
        rWf.sem(0).inc(gpsimd.dma_start(out=s_wf1[:, :, :], in_=wf1), 16)
        rWf.sem(0).inc(gpsimd.dma_start(out=s_wg1[:, :, :], in_=wg1), 16)

    apply_marks = []
    tr_marks = []
    for it in range(NT1):
        buf = s_x[it % 2]
        xnb = s_xn
        @blk.sync
        def _(sync):
            if it >= 2:
                sync.wait_ge(vE.h, apply_marks[it - 2])
            rX.sem(it).inc(sync.dma_start(out=buf[:], in_=x_q[it * 128:(it + 1) * 128, :]), 16)

        @blk.scalar
        def _(scalar):
            rX.sem(it).wait(scalar)
            if it >= 1:
                scalar.wait_ge(tE.h, tr_marks[it - 1])
            sop(scalar, scalar.activation, out=s_xn[:], in_=buf[:], func=AF.Square,
                accum_out=s_vcol[:])

        @blk.vector
        def _(vector):
            rX.sem(it).wait(vector)
            vop(vector, vector.reduce_sum, s_mcol[:], buf[:], axis=AX.X)
            vop(vector, vector.tensor_scalar_mul, s_mcol[:], s_mcol[:], 1.0 / HID)
            sE.wait(vector)
            vop(vector, vector.tensor_scalar_mul, s_vcol[:], s_vcol[:], 1.0 / HID)
            vop(vector, vector.scalar_tensor_tensor, out=s_vcol[:], in0=s_mcol[:],
                scalar=s_mcol[:, 0:1], in1=s_vcol[:], op0=ALU.mult, op1=ALU.subtract)

        @blk.scalar
        def _(scalar):
            vE.wait(scalar)
            sop(scalar, scalar.activation, out=s_rstd[:], in_=s_vcol[:], func=AF.Ln,
                scale=-1.0, bias=ccol("ln_eps"))
            sop(scalar, scalar.activation, out=s_rstd[:], in_=s_rstd[:], func=AF.Exp, scale=-0.5)

        @blk.vector
        def _(vector):
            sE.wait(vector)
            if it >= 1:
                vector.wait_ge(tE.h, tr_marks[it - 1])
            vop(vector, vector.tensor_scalar, out=xnb[:], in0=buf[:], scalar1=s_mcol[:, 0:1],
                scalar2=s_rstd[:, 0:1], op0=ALU.subtract, op1=ALU.mult)
        apply_marks.append(vE.n)

        for grp in range(4):
            @blk.tensor
            def _(tensor, grp=grp):
                tensor.wait_ge(vE.h, apply_marks[-1])
                if tE.n > 0:
                    tensor.wait_ge(sE.h, sE.n)
                for j in range(4):
                    dc = grp * 4 + j
                    tE.inc(tensor.transpose(p_tr[:, j, :], xnb[:, dc * 128:(dc + 1) * 128], s_identb[:]))
            tE.n += 0
            @blk.scalar
            def _(scalar, grp=grp):
                tE.wait(scalar)
                sop(scalar, scalar.copy,
                    out=s_xnt_all[:, grp * 4:(grp + 1) * 4, it * 128:(it + 1) * 128], in_=p_tr[:])
        tr_marks.append(tE.n)

        @blk.tensor
        def _(tensor):
            tensor.wait_ge(sE.h, sE.n)
            rWf.sem(0).wait(tensor)
            for dc in range(16):
                xt = s_xnt_all[:, dc, it * 128:(it + 1) * 128]
                mm = tensor.matmul(p_f1[:], s_wf1[:, dc, :], xt, start=(dc == 0), stop=(dc == 15))
            tE.inc(mm)
            for dc in range(16):
                xt = s_xnt_all[:, dc, it * 128:(it + 1) * 128]
                mm = tensor.matmul(p_g1[:], s_wg1[:, dc, :], xt, start=(dc == 0), stop=(dc == 15))
            tE.inc(mm)
        @blk.scalar
        def _(scalar):
            tE.wait(scalar)
            sop(scalar, scalar.copy, out=s_f1t[:, it * 128:(it + 1) * 128], in_=p_f1[:])
            sop(scalar, scalar.copy, out=s_g1t[:, it * 128:(it + 1) * 128], in_=p_g1[:])

    @blk.sync
    def _(sync):
        sync.wait_ge(sE.h, sE.n)
        for dc in range(16):
            rAg.sem(0).inc(sync.dma_start(out=agin[dc * 128:(dc + 1) * 128, :], in_=s_xnt_all[:, dc, :]), 16)
        rAg.sem(0).inc(sync.dma_start(out=agin[HID:HID + 128, :], in_=s_f1t[:]), 16)
        rAg.sem(0).inc(sync.dma_start(out=agin[HID + 128:HID + 256, :], in_=s_g1t[:]), 16)

    @blk.gpsimd
    def _(gpsimd):
        rAg.sem(0).wait(gpsimd)
        gpsimd.collective_compute("AllGather", ALU.bypass, replica_groups=rg,
                                  ins=[agin.opt()], outs=[agout.opt()]).then_inc(cS.h, 1)
    cS.n += 1

    scope("pA2")
    s_wgi = [sb(f"s_wgi{i}", [128, 16, 128], BF16) for i in range(2)]
    s_wui = [sb(f"s_wui{i}", [128, 16, 128], BF16) for i in range(2)]
    s_gtb = [sb(f"s_gtb{i}", [128, 512], BF16) for i in range(2)]
    s_upb = [sb(f"s_upb{i}", [128, 512], BF16) for i in range(2)]
    s_fu = [sb(f"s_fu{i}", [128, 512], BF16) for i in range(2)]
    p_gt = [ps(f"p_gt{i}", [128, 512]) for i in range(2)]
    p_up = [ps(f"p_up{i}", [128, 512]) for i in range(2)]

    fuse_dma_marks = []
    a2_mm_marks = []
    a2_evac_marks = []
    a2_tt_marks = []
    for ib in range(NIB):
        wgb, wub = s_wgi[ib % 2], s_wui[ib % 2]
        @blk.gpsimd
        def _(gpsimd):
            if ib >= 2:
                gpsimd.wait_ge(tE.h, a2_mm_marks[ib - 2])
            rA2w.sem(ib).inc(gpsimd.dma_start(out=wgb[:, :, :], in_=wgt[ib]), 16)
            rA2w.sem(ib).inc(gpsimd.dma_start(out=wub[:, :, :], in_=wup[ib]), 16)
        fub = s_fu[ib % 2]
        for t5 in range(NT5Q):
            u = (ib * NT5Q + t5)
            pg, pu = p_gt[u % 2], p_up[u % 2]
            @blk.tensor
            def _(tensor, t5=t5, pg=pg, pu=pu):
                rA2w.sem(ib).wait(tensor)
                if u >= 2:
                    tensor.wait_ge(sE.h, a2_evac_marks[u - 2])
                for dc in range(16):
                    xt = s_xnt_all[:, dc, t5 * 512:(t5 + 1) * 512]
                    mm = tensor.matmul(pg[:], wgb[:, dc, :], xt, start=(dc == 0), stop=(dc == 15))
                tE.inc(mm)
                for dc in range(16):
                    xt = s_xnt_all[:, dc, t5 * 512:(t5 + 1) * 512]
                    mm = tensor.matmul(pu[:], wub[:, dc, :], xt, start=(dc == 0), stop=(dc == 15))
                tE.inc(mm)
            @blk.scalar
            def _(scalar, t5=t5, pg=pg, pu=pu):
                tE.wait(scalar)
                if u >= 2:
                    scalar.wait_ge(vE.h, a2_tt_marks[u - 2])
                sop(scalar, scalar.activation, out=s_gtb[u % 2][:], in_=pg[:], func=AF.Silu,
                    scale=s_sgt[:, ib:ib + 1])
                sop(scalar, scalar.activation, out=s_upb[u % 2][:], in_=pu[:], func=AF.Copy,
                    scale=s_sup[:, ib:ib + 1])
                a2_evac_marks.append(sE.n)
            fub = s_fu[t5 % 2]
            @blk.vector
            def _(vector, t5=t5, fub=fub):
                sE.wait(vector)
                if ib > 0 or t5 >= 2:
                    rFu.sem(t5).wait(vector)
                vop(vector, vector.tensor_tensor, out=fub[:],
                    in0=s_gtb[u % 2][:], in1=s_upb[u % 2][:], op=ALU.mult)
                a2_tt_marks.append(vE.n)
            @blk.sync
            def _(sync, t5=t5, fub=fub):
                sync.wait_ge(vE.h, vE.n)
                rFu.sem(t5).inc(sync.dma_start(
                    out=fuse_scr[ib * 128:(ib + 1) * 128, t5 * 512:(t5 + 1) * 512], in_=fub[:]), 16)
        a2_mm_marks.append(tE.n)

    full_barrier()
    stA.close()

    scope("pB1")
    stB1 = open_phase()
    s_wp = [sb(f"s_wp{p}", [128, 16, 512], BF16) for p in range(4)]
    s_wf2 = sb("s_wf2", [128, 512], BF16)
    s_wg2 = sb("s_wg2", [128, 512], BF16)
    s_wbs = sb("s_wbs", [128, 16, 4], BF16)
    s_xt = [sb(f"s_xt{i}", [128, 16, 512], BF16) for i in range(2)]
    s_ft = [sb(f"s_ft{i}", [128, 512], BF16) for i in range(2)]
    s_gt1 = [sb(f"s_gt1{i}", [128, 512], BF16) for i in range(2)]
    s_xh = [sb(f"s_xh{p}", [128, 4, 515], BF16) for p in range(4)]
    s_yc = [sb(f"s_yc{i}", [128, 4, 512], BF16) for i in range(2)]
    s_out4 = [[sb(f"s_out4{p}_{i}", [128, 4, 512], BF16) for i in range(2)] for p in range(4)]
    s_wsq = sb("s_wsq", [128, 4, 512], BF16)
    s_beta = [sb(f"s_beta{i}", [4, 512], F32) for i in range(2)]
    s_rsr = sb("s_rsr", [1, 2048], BF16)
    s_rsb = sb("s_rsb", [128, 512], F32)
    s_wbt = sb("s_wbt", [128, 4, 512], BF16)
    s_zgt = sb("s_zgt", [128, 4, 512], F32)
    s_sgt2 = sb("s_sgt2", [128, 4, 512], BF16)
    p_pr = [ps(f"p_pr{i}", [128, 512]) for i in range(4)]
    p_msq = ps("p_msq", [1, 512])
    p_bc1 = ps("p_bc1", [128, 512])
    p_bc2 = ps("p_bc2", [128, 512])
    p_zg = ps("p_zg", [128, 512])

    @blk.gpsimd
    def _(gpsimd):
        for p in range(4):
            rB1w.sem(0).inc(gpsimd.dma_start(out=s_wp[p][:, :, :], in_=wproj[p]), 16)
        rB1w.sem(0).inc(gpsimd.dma_start(out=s_wf2[:], in_=wf2), 16)
        rB1w.sem(0).inc(gpsimd.dma_start(out=s_wg2[:], in_=wg2), 16)
        rB1w.sem(0).inc(gpsimd.dma_start(out=s_wbs[:, :, :], in_=wbm), 16)

    @blk.sync
    def _(sync):
        cS.wait(sync)

    prev_evac_marks = {}
    b1_evac_marks = []
    b1_silu_marks = []
    beta_tanh_marks = {}

    def b1_projs(t5):
        rk = (t5 * 512) // S
        toff = (t5 * 512) % S
        xt = s_xt[t5 % 2]
        ft = s_ft[t5 % 2]
        gt1 = s_gt1[t5 % 2]
        @blk.sync
        def _(sync):
            if t5 >= 2:
                sync.wait_ge(tE.h, prev_evac_marks.get(("mmdone", t5 - 2), 0))
            base = rk * AGR
            for dcg in range(4):
                rBx.sem(t5).inc(sync.dma_start(out=xt[:, dcg * 4:(dcg + 1) * 4, :],
                               in_=agout[base + dcg * 512:base + (dcg + 1) * 512,
                                         toff:toff + 512].rearrange("(dc p) t -> p dc t", p=128)), 16)
            rBx.sem(t5).inc(sync.dma_start(out=ft[:], in_=agout[base + HID:base + HID + 128, toff:toff + 512]), 16)
            rBx.sem(t5).inc(sync.dma_start(out=gt1[:], in_=agout[base + HID + 128:base + HID + 256, toff:toff + 512]), 16)

        for p in range(4):
            @blk.tensor
            def _(tensor, p=p):
                rBx.sem(t5).wait(tensor)
                if p == 0:
                    rB1w.sem(0).wait(tensor)
                if b1_evac_marks:
                    tensor.wait_ge(vE.h, b1_evac_marks[-1])
                for cb in range(4):
                    for dc in range(16):
                        mm = tensor.matmul(p_pr[cb][:], s_wp[p][:, dc, cb * 128:(cb + 1) * 128],
                                      xt[:, dc, :], start=(dc == 0), stop=(dc == 15))
                    tE.inc(mm)
                if p == 3:
                    tensor.wait_ge(vE.h, vE.n)
                    for dc in range(16):
                        mm = tensor.matmul(p_bc2[0:4, :], s_wbs[:, dc, :], xt[:, dc, :],
                                      start=(dc == 0), stop=(dc == 15))
                    tE.inc(mm)
            mm_mark = tE.n

            xh = s_xh[p]
            @blk.gpsimd
            def _(gpsimd, p=p):
                if t5 == 0:
                    gop(gpsimd, gpsimd.memset, xh[:, :, 0:3], 0.0)
            @blk.vector
            def _(vector, p=p):
                if t5 == 0:
                    gE.wait(vector)
                vector.wait_ge(tE.h, mm_mark)
                for cb in range(4):
                    vop(vector, vector.tensor_copy, out=xh[:, cb, 3:515], in_=p_pr[cb][:])
                b1_evac_marks.append(vE.n)
                if len(b1_silu_marks) >= 2:
                    vector.wait_ge(sE.h, b1_silu_marks[-2])
                for h in range(4):
                    cwc = lambda tap: s_cw[:, p * 16 + h * 4 + tap:p * 16 + h * 4 + tap + 1]
                    yc = s_yc[p % 2]
                    vop(vector, vector.tensor_scalar, out=yc[:, h, :], in0=xh[:, h, 3:515],
                        scalar1=cwc(3), scalar2=None, op0=ALU.mult)
                    for j in (1, 2, 3):
                        vop(vector, vector.scalar_tensor_tensor, out=yc[:, h, :],
                            in0=xh[:, h, 3 - j:515 - j], scalar=cwc(3 - j),
                            in1=yc[:, h, :], op0=ALU.mult, op1=ALU.add)
                vop(vector, vector.tensor_copy, out=xh[:, :, 0:3], in_=xh[:, :, 512:515])
            @blk.scalar
            def _(scalar, p=p):
                vE.wait(scalar)
                if t5 >= 2:
                    rBst.sem(t5).wait(scalar)
                sop(scalar, scalar.activation, out=s_out4[p][t5 % 2][:, :, :],
                    in_=s_yc[p % 2][:, :, :], func=AF.Silu)
                b1_silu_marks.append(sE.n)
                if p == 3:
                    sop(scalar, scalar.activation, out=s_beta[t5 % 2][:], in_=p_bc2[0:4, :],
                        func=AF.Tanh, scale=0.5)
                    beta_tanh_marks[t5] = sE.n
        @blk.vector
        def _(vector):
            sE.wait(vector)
            vop(vector, vector.tensor_scalar, out=s_beta[t5 % 2][:], in0=s_beta[t5 % 2][:],
                scalar1=0.5, scalar2=0.5, op0=ALU.mult, op1=ALU.add)
        prev_evac_marks[("beta_v", t5)] = vE.n

    def b1_tail(t5):
        par = t5 % 2
        ft = s_ft[par]
        gt1 = s_gt1[par]
        out4w = s_out4[3][par]
        @blk.scalar
        def _(scalar):
            if t5 > 0:
                rBst.sem(t5 - 1).wait(scalar)
            sop(scalar, scalar.square, s_wsq[:], out4w[:])
        for h in range(4):
            @blk.tensor
            def _(tensor, h=h):
                tensor.wait_ge(sE.h, sE.n)
                tE.inc(tensor.matmul(p_msq[:], s_onescb[:], s_wsq[:, h, :], start=True, stop=True))
            @blk.scalar
            def _(scalar, h=h):
                tE.wait(scalar)
                sop(scalar, scalar.activation, out=s_rsr[:, h * 512:(h + 1) * 512], in_=p_msq[:],
                    func=AF.Ln, bias=ccol("l2_eps")[0:1, :])
                sop(scalar, scalar.activation, out=s_rsr[:, h * 512:(h + 1) * 512],
                    in_=s_rsr[:, h * 512:(h + 1) * 512], func=AF.Exp, scale=-0.5)
        for h in range(4):
            @blk.tensor
            def _(tensor, h=h):
                tensor.wait_ge(sE.h, sE.n)
                tensor.wait_ge(vE.h, vE.n)
                tensor.wait_ge(vE.h, prev_evac_marks[("beta_v", t5)])
                if h == 0 and (t5 + 1) in beta_tanh_marks:
                    tensor.wait_ge(sE.h, beta_tanh_marks[t5 + 1])
                tE.inc(tensor.matmul(p_bc1[:], s_ones1b[:], s_rsr[:, h * 512:(h + 1) * 512],
                              start=True, stop=True))
                tE.inc(tensor.matmul(p_bc2[:], s_sel4[:, h * 128:(h + 1) * 128], s_beta[par][:],
                              start=True, stop=True))
            @blk.vector
            def _(vector, h=h):
                tE.wait(vector)
                vop(vector, vector.tensor_tensor, out=s_wbt[:, h, :], in0=out4w[:, h, :],
                    in1=p_bc1[:], op=ALU.mult)
                vop(vector, vector.tensor_tensor, out=s_wbt[:, h, :], in0=s_wbt[:, h, :],
                    in1=p_bc2[:], op=ALU.mult)
        for tt in range(4):
            @blk.tensor
            def _(tensor, tt=tt):
                tensor.wait_ge(sE.h, sE.n)
                tE.inc(tensor.matmul(p_zg[:], ft[:, tt * 128:(tt + 1) * 128], s_wf2[:],
                              start=True, stop=True))
            @blk.scalar
            def _(scalar, tt=tt):
                tE.wait(scalar)
                sop(scalar, scalar.copy, out=s_zgt[:, tt, :], in_=p_zg[:])
        for cb in range(4):
            @blk.tensor
            def _(tensor, cb=cb):
                tensor.wait_ge(sE.h, sE.n)
                tensor.wait_ge(vE.h, vE.n)
                tE.inc(tensor.matmul(p_bc1[:], s_wg2[:, cb * 128:(cb + 1) * 128], gt1[:],
                              start=True, stop=True))
            @blk.scalar
            def _(scalar, cb=cb):
                tE.wait(scalar)
                sop(scalar, scalar.activation, out=s_sgt2[:, cb, :], in_=p_bc1[:], func=AF.Tanh,
                    scale=0.5, bias=s_bg2h[:, cb:cb + 1])
        @blk.vector
        def _(vector):
            sE.wait(vector)
            vop(vector, vector.tensor_scalar, out=s_sgt2[:, :, :], in0=s_sgt2[:, :, :], scalar1=0.5,
                scalar2=0.5, op0=ALU.mult, op1=ALU.add)
        prev_evac_marks[("mmdone", t5)] = tE.n
        @blk.sync
        def _(sync):
            sync.wait_ge(vE.h, vE.n)
            sync.wait_ge(sE.h, sE.n)
            span = slice(t5 * 512, (t5 + 1) * 512)
            for p in range(3):
                rBst.sem(t5).inc(sync.dma_start(
                    out=pr_scr[p][:, :, span].rearrange("h c t -> c h t"), in_=s_out4[p][par][:, :, :]), 16)
            rBst.sem(t5).inc(sync.dma_start(
                out=pr_scr[3][:, :, span].rearrange("h c t -> c h t"), in_=s_wbt[:, :, :]), 16)
            rBst.sem(t5).inc(sync.dma_start(
                out=sg_scr[:, :, span].rearrange("h c t -> c h t"), in_=s_sgt2[:, :, :]), 16)
            rBst.sem(t5).inc(sync.dma_start(out=zg_scr[span, :].rearrange("(tt p) m -> p tt m", p=128),
                           in_=s_zgt[:]), 16)

    for t5 in range(NT5B):
        b1_projs(t5)
        if t5 >= 1:
            b1_tail(t5 - 1)
    b1_tail(NT5B - 1)

    if taps:
        full_barrier()
        @blk.sync
        def _(sync):
            rOut.sem(0).inc(sync.dma_start(out=tp_xnT, in_=agout), 16)
            rOut.sem(0).inc(sync.dma_start(out=tp_fuse, in_=fuse_scr), 16)
            for p in range(4):
                rOut.sem(0).inc(sync.dma_start(out=tp_pr[p], in_=pr_scr[p]), 16)
            rOut.sem(0).inc(sync.dma_start(out=tp_zg, in_=zg_scr), 16)
            rOut.sem(0).inc(sync.dma_start(out=tp_sg, in_=sg_scr), 16)

    full_barrier()
    stB1.close()
    cur_stack[0] = ctx

    scope("pB2")
    stB2 = open_phase()
    rCh = Ring(nc, "rCh", 3)
    rCt = Ring(nc, "rCt", 3)
    rCz = Ring(nc, "rCz", 3)
    rO2 = Ring(nc, "rO2", 2)
    s_ones4 = sb("s_ones4", [4, 128], F32)
    s_qf = [sb(f"s_qf{i}", [128, 4, 128], BF16) for i in range(3)]
    s_kf = [sb(f"s_kf{i}", [128, 4, 128], BF16) for i in range(3)]
    s_wf = [sb(f"s_wbf{i}", [128, 4, 128], BF16) for i in range(3)]
    s_kt = [sb(f"s_kt{i}", [128, 4, 128], BF16) for i in range(3)]
    s_vt = [sb(f"s_vt{i}", [128, 4, 128], BF16) for i in range(3)]
    s_wt = [sb(f"s_wt{i}", [128, 4, 128], BF16) for i in range(3)]
    s_zg2 = [sb(f"s_zg2{i}", [128, 512], F32) for i in range(3)]
    s_sg2 = [sb(f"s_sg2{i}", [128, 4, 128], BF16) for i in range(3)]
    s_St = sb("s_St", [128, 4, 256], F32)
    s_Stb = sb("s_Stb", [128, 4, 256], BF16)
    s_SkT = sb("s_SkT", [128, 4, 128], BF16)
    s_sp = [sb(f"s_sp{i}", [128, 512], F32) for i in range(2)]
    s_Ltok = [sb(f"s_Ltok{i}", [128, 512], F32) for i in range(2)]
    s_Lf = [sb(f"s_Lf{i}", [128, 4, 128], F32) for i in range(2)]
    s_Ep = [sb(f"s_Ep{i}", [128, 512], F32) for i in range(2)]
    s_Emt = [sb(f"s_Emt{i}", [128, 512], F32) for i in range(2)]
    s_Emf = [sb(f"s_Emf{i}", [128, 4, 128], F32) for i in range(2)]
    s_Ptok = [sb(f"s_Ptok{i}", [128, 512], BF16) for i in range(2)]
    s_Pf = [sb(f"s_Pf{i}", [128, 4, 128], BF16) for i in range(2)]
    s_Am = [sb(f"s_Am{i}", [128, 4, 128], BF16) for i in range(2)]
    s_Bm = sb("s_Bm", [128, 4, 128], BF16)
    s_t1 = sb("s_t1", [128, 512], F32)
    s_scl = sb("s_scl", [128, 512], F32)
    s_expo = sb("s_expo", [128, 512], F32)
    s_dn = sb("s_dn", [128, 4, 1], F32)
    s_rdn = sb("s_rdn", [128, 4, 1], F32)
    s_t2 = sb("s_t2", [128, 512], F32)
    s_eopf = sb("s_eopf", [128, 512], F32)
    s_eop = sb("s_eop", [128, 4, 128], BF16)
    s_ehat = sb("s_ehat", [128, 4, 128], BF16)
    s_rc = sb("s_rc", [128, 4, 1], F32)
    s_rexp = sb("s_rexp", [128, 4, 1], F32)
    s_tail = sb("s_tail", [128, 4, 1], F32)
    s_osq = sb("s_osq", [128, 512], BF16)
    s_r2 = sb("s_r2", [1, 512], F32)
    s_rsb2 = sb("s_rsb2", [128, 512], F32)
    s_t3 = sb("s_t3", [128, 512], F32)
    s_ofin = [sb(f"s_ofin{i}", [128, 4, 128], BF16) for i in range(2)]
    p_misc = ps("p_misc", [128, 512])
    p_lf = ps("p_lf", [128, 4, 128])
    p_ab = [ps(f"p_ab{i}", [128, 4, 128]) for i in range(2)]
    p_so = ps("p_so", [128, 512])
    p_add = ps("p_add", [128, 512])
    p_et = ps("p_et", [128, 4, 128], BF16)

    rDw = Ring(nc, "rDw", 1)
    rDf = Ring(nc, "rDf", 2)
    rDst = Ring(nc, "rDst", 2)
    s_sdnb = sb("s_sdnb", [128, HID], F32)
    s_wdnp = sb("s_wdnp", [128, 64, 512], BF16)
    s_fuT = [sb(f"s_fuT{i}", [128, 64, 128], BF16) for i in range(2)]
    s_do = [sb(f"s_do{i}", [128, 512], F32) for i in range(2)]
    p_dn = ps("p_dn", [128, 512])

    @blk.gpsimd
    def _(gpsimd):
        gop(gpsimd, gpsimd.memset, s_ones4[:], 1.0)
        gop(gpsimd, gpsimd.memset, s_St[:], 0.0)
        gop(gpsimd, gpsimd.memset, s_Stb[:], 0.0)
        gop(gpsimd, gpsimd.memset, s_tail[:], 0.0)

    for dp in range(4):
        @blk.tensor
        def _(tensor, dp=dp):
            if dp == 0:
                rI.sem(0).wait(tensor)
            else:
                tensor.wait_ge(sE.h, sE.n)
            tE.inc(tensor.matmul(p_dn[:], s_ones1f[:], s_sdnr[:, dp * 512:(dp + 1) * 512],
                                 start=True, stop=True))
        @blk.scalar
        def _(scalar, dp=dp):
            tE.wait(scalar)
            sop(scalar, scalar.copy, out=s_sdnb[:, dp * 512:(dp + 1) * 512], in_=p_dn[:])
    sdnb_mark = sE.n

    dn_marks = {}
    dn_evac_marks = {}

    def dn_wload(dp):
        @blk.gpsimd
        def _(gpsimd):
            if dp > 0:
                gpsimd.wait_ge(tE.h, dn_marks[(dp - 1, NT1 - 1)])
            rDw.sem(0).inc(gpsimd.dma_start(out=s_wdnp[:, :, :], in_=wdn[dp]), 16)

    def dn_loads(u):
        if u >= 4 * NT1:
            return
        dp, it = u // NT1, u % NT1
        @blk.scalar
        def _(scalar):
            if u >= 2:
                scalar.wait_ge(tE.h, dn_marks.get((dp, it - 2) if it >= 2 else (dp - 1, NT1 - 2 + it), 0))
            rDf.sem(u).inc(scalar.dma_start(
                out=s_fuT[u % 2][:, :, :],
                in_=fuse_scr[:, it * 128:(it + 1) * 128].rearrange("(gi p) t -> p gi t", p=128)), 16)

    def dn_begin(u):
        if u >= 4 * NT1:
            return None
        return {"u": u, "dp": u // NT1, "it": u % NT1, "fslot": s_fuT[u % 2], "ic": 0}

    def dn_fill(st, n):
        if st is None or st["ic"] >= 64:
            return
        u, ic0 = st["u"], st["ic"]
        n = min(n, 64 - ic0)
        @blk.tensor
        def _(tensor):
            if ic0 == 0:
                rDw.sem(0).wait(tensor)
                rDf.sem(u).wait(tensor)
                if u == 0:
                    tensor.wait_ge(sE.h, sdnb_mark)
                if u >= 1:
                    tensor.wait_ge(vE.h, dn_evac_marks[u - 1])
            for ic in range(ic0, ic0 + n):
                mm = tensor.matmul(p_dn[:], st["fslot"][:, ic, :], s_wdnp[:, ic, :],
                                   start=(ic == 0), stop=(ic == 63))
            tE.inc(mm)
        st["ic"] = ic0 + n

    def dn_finish(st):
        if st is None:
            return
        dn_fill(st, 64 - st["ic"])
        u, dp, it = st["u"], st["dp"], st["it"]
        dn_marks[(dp, it)] = tE.n
        if it == NT1 - 1 and dp < 3:
            dn_wload(dp + 1)
        @blk.vector
        def _(vector):
            vector.wait_ge(tE.h, dn_marks[(dp, it)])
            vector.wait_ge(sE.h, sdnb_mark)
            if u >= 2:
                rDst.sem(u).wait(vector)
            vop(vector, vector.tensor_tensor, out=s_do[u % 2][:], in0=p_dn[:],
                in1=s_sdnb[:, dp * 512:(dp + 1) * 512], op=ALU.mult)
        dn_evac_marks[u] = vE.n
        @blk.gpsimd
        def _(gpsimd):
            gpsimd.wait_ge(vE.h, dn_evac_marks[u])
            rDst.sem(u).inc(gpsimd.dma_start(
                out=ffn_scr[it * 128:(it + 1) * 128, dp * 512:(dp + 1) * 512], in_=s_do[u % 2][:]), 16)

    dn_wload(0)
    dn_loads(0)
    dn_loads(1)
    chunk_done = {}
    clamp_marks = {}
    rc_marks = {}
    epi_marks = {}
    t1_marks = {}
    sp_marks = {}
    lcmm_marks = {}

    def b2_prefix(ch):
        par = ch % 2
        sl4 = ch % 3
        span = slice(ch * 128, (ch + 1) * 128)
        qf, kf, wf = s_qf[sl4], s_kf[sl4], s_wf[sl4]
        kt, vt, wt = s_kt[sl4], s_vt[sl4], s_wt[sl4]
        zg2, sg2 = s_zg2[sl4], s_sg2[sl4]

        @blk.sync
        def _(sync):
            if ch >= 3:
                d = chunk_done[ch - 3]
                sync.wait_ge(tE.h, d["t"]); sync.wait_ge(vE.h, d["v"]); sync.wait_ge(sE.h, d["s"])
            rCz.sem(ch).inc(sync.dma_start(out=zg2[:], in_=zg_scr[span, :]), 16)
            rCh.sem(ch).inc(sync.dma_start(out=qf[:, :, :], in_=pr_scr[0][:, :, span].rearrange("h c t -> c h t")), 16)
            rCh.sem(ch).inc(sync.dma_start(out=kf[:, :, :], in_=pr_scr[1][:, :, span].rearrange("h c t -> c h t")), 16)
            rCh.sem(ch).inc(sync.dma_start(out=wf[:, :, :], in_=pr_scr[3][:, :, span].rearrange("h c t -> c h t")), 16)
            rCh.sem(ch).inc(sync.dma_start(out=sg2[:, :, :], in_=sg_scr[:, :, span].rearrange("h c t -> c h t")), 16)
            rCt.sem(ch).inc(sync.dma_start(out=kt[:, :, :].rearrange("t h c -> t (h c)"),
                                           in_=pr_scr[1][:, :, span].rearrange("h c t -> (h c) t"),
                                           transpose=True), 16)
            rCt.sem(ch).inc(sync.dma_start(out=vt[:, :, :].rearrange("t h c -> t (h c)"),
                                           in_=pr_scr[2][:, :, span].rearrange("h c t -> (h c) t"),
                                           transpose=True), 16)
            rCt.sem(ch).inc(sync.dma_start(out=wt[:, :, :].rearrange("t h c -> t (h c)"),
                                           in_=pr_scr[3][:, :, span].rearrange("h c t -> (h c) t"),
                                           transpose=True), 16)

        @blk.scalar
        def _(scalar):
            rCz.sem(ch).wait(scalar)
            sop(scalar, scalar.activation, out=s_sp[par][:], in_=zg2[:], func=AF.Exp, scale=-1.0)
            sop(scalar, scalar.activation, out=s_sp[par][:], in_=s_sp[par][:], func=AF.Ln, bias=ccol("one"))
        sp_marks[ch] = sE.n

        @blk.tensor
        def _(tensor):
            tensor.wait_ge(sE.h, sp_marks[ch])
            if ch >= 1:
                tensor.wait_ge(vE.h, clamp_marks[ch - 1])
            if ch == 0:
                gE.wait(tensor)
            tE.inc(tensor.matmul(p_misc[:], s_mcx[:, 0:128], s_sp[par][:], start=True, stop=True))
            for h in range(4):
                tE.inc(tensor.matmul(p_lf[:, h, :], s_sp[par][:, h * 128:(h + 1) * 128],
                                     s_mcx[:, 0:128], start=True, stop=True))
        lmm_mark = tE.n

        @blk.vector
        def _(vector):
            vector.wait_ge(tE.h, lmm_mark)
            vop(vector, vector.tensor_scalar, out=s_Ltok[par][:], in0=p_misc[:], scalar1=80.0,
                scalar2=-80.0, op0=ALU.min, op1=ALU.max)
            vop(vector, vector.tensor_scalar, out=s_Lf[par][:, :, :], in0=p_lf[:, :, :], scalar1=80.0,
                scalar2=-80.0, op0=ALU.min, op1=ALU.max)
        clamp_marks[ch] = vE.n

        @blk.tensor
        def _(tensor):
            rCh.sem(ch).wait(tensor)
            if ch >= 2:
                tensor.wait_ge(vE.h, epi_marks[ch - 2])
            for h in range(4):
                tE.inc(tensor.matmul(p_ab[par][:, h, :], kf[:, h, :], qf[:, h, :], start=True, stop=True))
        amm_mark = tE.n
        @blk.vector
        def _(vector):
            vector.wait_ge(tE.h, amm_mark)
            vop(vector, vector.tensor_copy, out=s_Am[par][:, :, :], in_=p_ab[par][:, :, :])
        amcp_mark = vE.n
        @blk.gpsimd
        def _(gpsimd):
            gpsimd.wait_ge(vE.h, amcp_mark)
            gop(gpsimd, gpsimd.affine_select, out=s_Am[par][:, :, :], in_=s_Am[par][:, :, :],
                pattern=[[0, 4], [1, 128]], compare_op=ALU.is_ge, fill=0.0, base=0, channel_multiplier=-1)
        return {}

    def b2_suffix(ch, pm):
        par = ch % 2
        sl4 = ch % 3
        span = slice(ch * 128, (ch + 1) * 128)
        qf = s_qf[sl4]
        kt, vt = s_kt[sl4], s_vt[sl4]
        sg2 = s_sg2[sl4]
        ofin = s_ofin[par]

        s_base = sE.n
        rexp_mark = s_base + 1
        eexp_mark = s_base + 4
        expo_mark = s_base + 5
        eopf_mark = s_base + 6
        osq_mark = s_base + 7
        r2_mark = s_base + 9

        dnst = dn_begin(ch)
        dn_fill(dnst, 6)
        @blk.tensor
        def _(tensor):
            if ch >= 1:
                tensor.wait_ge(vE.h, t1_marks[ch - 1])
            for h in range(4):
                tE.inc(tensor.matmul(p_so[:, h:h + 1], s_sp[par][:, h * 128:(h + 1) * 128],
                                     s_mcx[:, 128:129], start=True, stop=True))
        lcmm_marks[ch] = tE.n
        @blk.vector
        def _(vector):
            vector.wait_ge(tE.h, lcmm_marks[ch])
            vop(vector, vector.tensor_tensor, out=s_rc[:, :, :],
                in0=bass.AP(p_so, 0, [[512, 128], [1, 4], [1, 1]]),
                in1=s_tail[:, :, :], op=ALU.add)
        rc_marks[ch] = vE.n
        @blk.vector
        def _(vector):
            vector.wait_ge(sE.h, rexp_mark)
            if ch > 0:
                for h in range(4):
                    vop(vector, vector.tensor_scalar, out=s_St[:, h, :], in0=s_St[:, h, :],
                        scalar1=s_rexp[:, h, 0:1].opt(), scalar2=None, op0=ALU.mult)
            vop(vector, vector.tensor_copy, out=s_Stb[:, :, :], in_=s_St[:, :, :])
        resc_mark = vE.n
        dn_fill(dnst, 6)
        @blk.tensor
        def _(tensor):
            tensor.wait_ge(vE.h, resc_mark)
            if ch >= 1:
                tensor.wait_ge(vE.h, ehat_marks[ch - 1])
            for h in range(4):
                tE.inc(tensor.transpose(p_et[:, h, :], s_Stb[:, h, 0:128], s_identb[:]))
        sktr_mark = tE.n
        @blk.vector
        def _(vector):
            vector.wait_ge(tE.h, sktr_mark)
            vop(vector, vector.tensor_copy, out=s_SkT[:, :, :], in_=p_et[:, :, :])
        skt_mark = vE.n
        wt, wf = s_wt[sl4], s_wf[sl4]
        @blk.vector
        def _(vector):
            rCt.sem(ch).wait(vector)
            vector.wait_ge(sE.h, eexp_mark)
            vop(vector, vector.tensor_tensor, out=s_Ptok[par][:],
                in0=wt[:, :, :].rearrange("s h m -> s (h m)"), in1=s_Emt[par][:], op=ALU.mult)
            vop(vector, vector.tensor_tensor, out=s_Pf[par][:, :, :], in0=wf[:, :, :],
                in1=s_Emf[par][:, :, :], op=ALU.mult)
        p_mark = vE.n

        dn_fill(dnst, 8)
        @blk.tensor
        def _(tensor):
            tensor.wait_ge(vE.h, skt_mark)
            tensor.wait_ge(gE.h, gE.n)
            tensor.wait_ge(vE.h, p_mark)
            for h in range(4):
                hs = slice(h * 128, (h + 1) * 128)
                tensor.matmul(p_so[:, hs], qf[:, h, :], s_SkT[:, h, :], start=True, stop=False)
                tE.inc(tensor.matmul(p_so[:, hs], s_Am[par][:, h, :], s_Ptok[par][:, hs], start=False, stop=True))
        smm_mark = tE.n

        @blk.vector
        def _(vector):
            vector.wait_ge(tE.h, smm_mark)
            vop(vector, vector.tensor_tensor, out=s_t1[:], in0=p_so[:], in1=s_Ep[par][:], op=ALU.mult)
            vop(vector, vector.tensor_scalar, out=s_scl[:], in0=s_t1[:], scalar1=SCALE,
                scalar2=70.0, op0=ALU.mult, op1=ALU.min)
        t1_marks[ch] = vE.n
        @blk.vector
        def _(vector):
            vector.wait_ge(sE.h, expo_mark)
            vop(vector, vector.tensor_reduce, out=s_dn[:, :, :],
                in_=s_expo[:].rearrange("t (h m) -> t h m", m=128), axis=AX.X, op=ALU.add)
            vop(vector, vector.reciprocal, out=s_rdn[:, :, :], in_=s_dn[:, :, :])
            vop(vector, vector.tensor_tensor, out=s_t2[:], in0=s_scl[:], in1=s_Ltok[par][:], op=ALU.subtract)
            vop(vector, vector.tensor_scalar, out=s_t2[:], in0=s_t2[:], scalar1=85.0, scalar2=None,
                op0=ALU.min)
        t2_mark = vE.n
        @blk.vector
        def _(vector):
            vector.wait_ge(sE.h, eopf_mark)
            for h in range(4):
                hs = slice(h * 128, (h + 1) * 128)
                vop(vector, vector.tensor_scalar, out=s_eop[:, h, :], in0=s_eopf[:, hs],
                    scalar1=s_rdn[:, h, 0:1].opt(), scalar2=None, op0=ALU.mult)
        eop_mark = vE.n
        dn_fill(dnst, 8)
        @blk.tensor
        def _(tensor):
            tensor.wait_ge(vE.h, eop_mark)
            tensor.wait_ge(vE.h, skt_mark)
            for h in range(4):
                tE.inc(tensor.transpose(p_et[:, h, :], s_eop[:, h, :], s_identb[:]))
        etr_mark = tE.n
        @blk.vector
        def _(vector):
            vector.wait_ge(tE.h, etr_mark)
            vop(vector, vector.tensor_copy, out=s_ehat[:, :, :], in_=p_et[:, :, :])
        ehat_marks[ch] = vE.n

        dn_fill(dnst, 6)
        @blk.tensor
        def _(tensor):
            tensor.wait_ge(vE.h, ehat_marks[ch])
            tensor.wait_ge(vE.h, p_mark)
            for h in range(4):
                tE.inc(tensor.matmul(p_ab[par][:, h, :], s_Pf[par][:, h, :], s_ehat[:, h, :],
                                     start=True, stop=True))
        bmm_mark = tE.n
        @blk.vector
        def _(vector):
            vector.wait_ge(tE.h, bmm_mark)
            vop(vector, vector.tensor_copy, out=s_Bm[:, :, :], in_=p_ab[par][:, :, :])
        bmcp_mark = vE.n
        @blk.gpsimd
        def _(gpsimd):
            gpsimd.wait_ge(vE.h, bmcp_mark)
            gop(gpsimd, gpsimd.affine_select, out=s_Bm[:, :, :], in_=s_Bm[:, :, :],
                pattern=[[0, 4], [1, 128]], compare_op=ALU.is_ge, fill=0.0, base=0, channel_multiplier=-1)
        dn_fill(dnst, 8)
        @blk.tensor
        def _(tensor):
            tensor.wait_ge(gE.h, gE.n)
            for h in range(4):
                tensor.matmul(p_ab[par][:, h, :], s_Stb[:, h, 128:256], s_ehat[:, h, :],
                              start=True, stop=False)
                tE.inc(tensor.matmul(p_ab[par][:, h, :], vt[:, h, :], s_Bm[:, h, :],
                                     start=False, stop=True))
        omm_mark = tE.n
        dn_fill(dnst, 8)
        @blk.tensor
        def _(tensor):
            tensor.wait_ge(sE.h, osq_mark)
            if ch >= 1:
                tensor.wait_ge(vE.h, svadd_marks[ch - 1])
            tE.inc(tensor.matmul(p_add[0:1, 0:512], s_onescb[:], s_osq[:], start=True, stop=True))
        msq_mark = tE.n
        dn_fill(dnst, 6)
        @blk.tensor
        def _(tensor):
            tensor.wait_ge(sE.h, r2_mark)
            tE.inc(tensor.matmul(p_add[:], s_ones1f[:], s_r2[:], start=True, stop=True))
        rsb_mark = tE.n
        @blk.vector
        def _(vector):
            vector.wait_ge(tE.h, rsb_mark)
            if ch >= 2:
                rO2.sem(ch).wait(vector)
            vop(vector, vector.tensor_copy, out=s_rsb2[:], in_=p_add[:])
            vop(vector, vector.tensor_tensor, out=s_t3[:],
                in0=p_ab[par][:, :, :].rearrange("e h t -> e (h t)"), in1=s_rsb2[:], op=ALU.mult)
            vop(vector, vector.tensor_tensor, out=ofin[:, :, :].rearrange("e h t -> e (h t)"),
                in0=s_t3[:], in1=sg2[:, :, :].rearrange("e h t -> e (h t)"), op=ALU.mult)
        epi_marks[ch] = vE.n

        dn_fill(dnst, 4)
        @blk.tensor
        def _(tensor):
            tensor.wait_ge(vE.h, epi_marks[ch])
            for h in range(4):
                hs = slice(h * 128, (h + 1) * 128)
                tE.inc(tensor.matmul(p_add[:, hs], s_Ptok[par][:, hs], kt[:, h, :],
                                     start=True, stop=True))
        kadd_mark = tE.n
        @blk.vector
        def _(vector):
            vector.wait_ge(tE.h, kadd_mark)
            vop(vector, vector.tensor_tensor, out=s_St[:, :, 0:128], in0=s_St[:, :, 0:128],
                in1=p_add[:].rearrange("m (h d) -> m h d", d=128), op=ALU.add)
        skadd_mark = vE.n
        dn_fill(dnst, 4)
        @blk.tensor
        def _(tensor):
            tensor.wait_ge(vE.h, skadd_mark)
            for h in range(4):
                hs = slice(h * 128, (h + 1) * 128)
                tE.inc(tensor.matmul(p_add[:, hs], s_Ptok[par][:, hs], vt[:, h, :],
                                     start=True, stop=True))
        vadd_mark = tE.n
        @blk.vector
        def _(vector):
            vector.wait_ge(tE.h, vadd_mark)
            vop(vector, vector.tensor_tensor, out=s_St[:, :, 128:256], in0=s_St[:, :, 128:256],
                in1=p_add[:].rearrange("m (h e) -> m h e", e=128), op=ALU.add)
            vop(vector, vector.tensor_copy, out=s_tail[:, :, :], in_=s_Lf[par][:, :, 127:128])
        svadd_marks[ch] = vE.n

        @blk.scalar
        def _(scalar):
            scalar.wait_ge(vE.h, rc_marks[ch])
            sop(scalar, scalar.activation, out=s_rexp[:, :, :], in_=s_rc[:, :, :], func=AF.Exp, scale=-1.0)
            scalar.wait_ge(vE.h, clamp_marks[ch])
            sop(scalar, scalar.activation, out=s_Ep[par][:], in_=s_Ltok[par][:], func=AF.Exp, scale=-1.0)
            sop(scalar, scalar.activation, out=s_Emt[par][:], in_=s_Ltok[par][:], func=AF.Exp)
            sop(scalar, scalar.activation, out=s_Emf[par][:, :, :], in_=s_Lf[par][:, :, :], func=AF.Exp)
            scalar.wait_ge(vE.h, t1_marks[ch])
            sop(scalar, scalar.activation, out=s_expo[:], in_=s_scl[:], func=AF.Exp, bias=ccol("neg_shift"))
            scalar.wait_ge(vE.h, t2_mark)
            sop(scalar, scalar.activation, out=s_eopf[:], in_=s_t2[:], func=AF.Exp, bias=ccol("neg_shift"))
            scalar.wait_ge(tE.h, omm_mark)
            sop(scalar, scalar.square, s_osq[:], p_ab[par][:, :, :].rearrange("e h t -> e (h t)"))
            scalar.wait_ge(tE.h, msq_mark)
            sop(scalar, scalar.activation, out=s_r2[:], in_=p_add[0:1, 0:512], func=AF.Ln,
                scale=1.0 / 128.0, bias=ccol("rms_eps")[0:1, :])
            sop(scalar, scalar.activation, out=s_r2[:], in_=s_r2[:], func=AF.Exp, scale=-0.5)
        assert sE.n == s_base + 9, (sE.n, s_base)

        jq = (ch * 128) // S
        cspan = slice((ch * 128) % S, (ch * 128) % S + 128)
        @blk.gpsimd
        def _(gpsimd):
            gpsimd.wait_ge(vE.h, epi_marks[ch])
            rO2.sem(ch).inc(gpsimd.dma_start(
                out=agin2[jq * 512:(jq + 1) * 512, cspan].rearrange("(h c) t -> c h t", c=128),
                in_=ofin[:, :, :]), 16)
        dn_finish(dnst)
        dn_loads(ch + 2)
        chunk_done[ch] = {"t": tE.n, "v": vE.n, "s": sE.n}

    ehat_marks = {}
    svadd_marks = {}
    pm_store = {}
    for ch in range(NCH):
        pm_store[ch] = b2_prefix(ch)
        if ch >= 1:
            b2_suffix(ch - 1, pm_store[ch - 1])
    b2_suffix(NCH - 1, pm_store[NCH - 1])

    @blk.gpsimd
    def _(gpsimd):
        for ch in range(max(0, NCH - 2), NCH):
            rO2.sem(ch).wait(gpsimd)
        gpsimd.collective_compute("AllGather", ALU.bypass, replica_groups=rg,
                                  ins=[agin2.opt()], outs=[agout2.opt()]).then_inc(cS.h, 1)
    cS.n += 1

    full_barrier()
    stB2.close()

    scope("pC2")
    stC2 = open_phase()
    rCw = Ring(nc, "rCw", 1)
    rCq = Ring(nc, "rCq", 1)
    rCo = Ring(nc, "rCo", 2)
    rCx = Ring(nc, "rCx", 2)
    rCf = Ring(nc, "rCf", 2)
    rOq = Ring(nc, "rOq", 2)
    s_qoff = sb("s_qoff", [1, 1], mybir.dt.uint32)
    s_wos = sb("s_wos", [128, 16, 2048], BF16)
    s_ot = [sb(f"s_ot{i}", [128, 16, 128], BF16) for i in range(2)]
    s_xc = [sb(f"s_xc{i}", [128, HID], F32) for i in range(2)]
    s_fc = [sb(f"s_fc{i}", [128, HID], F32) for i in range(2)]
    s_oq = [sb(f"s_oq{i}", [128, HID], F32) for i in range(2)]
    p_c2 = [ps(f"p_c2{i}", [128, 512]) for i in range(2)]

    @blk.gpsimd
    def _(gpsimd):
        rCw.sem(0).inc(gpsimd.dma_start(out=s_wos[:, :, :], in_=wo), 16)
    @blk.sync
    def _(sync):
        rCq.sem(0).inc(sync.dma_start(out=s_qoff[:], in_=qoff), 16)
    rSel = Ring(nc, "rSel", 1)
    @blk.gpsimd
    def _(gpsimd):
        from concourse.expressions import RuntimeValue
        rCq.sem(0).wait(gpsimd)
        cS.wait(gpsimd)
        rq = gpsimd.alloc_register("qreg")
        gpsimd.reg_load(rq, s_qoff[0:1, 0:1])
        gpsimd.reg_mul(rq, rq, 512 * S)
        for g in range(4):
            dyn = bass.AP(agout2.tensor, RuntimeValue(rq), [[S, 512], [1, S]])
            rSel.sem(0).inc(gpsimd.dma_start(out=osel[g * 512:(g + 1) * 512, :], in_=dyn), 16)
            if g < 3:
                gpsimd.reg_add(rq, rq, HID * S)
    @blk.vector
    def _(vector):
        rCw.sem(0).wait(vector)
        rI.sem(0).wait(vector)
        vop(vector, vector.tensor_scalar, out=s_wos[:, :, :], in0=s_wos[:, :, :],
            scalar1=s_normw[:, 0:1], scalar2=None, op0=ALU.mult)
    wos_mark = vE.n

    c2_marks = {}
    for it in range(NT1):
        ot, xc, fc, oq = s_ot[it % 2], s_xc[it % 2], s_fc[it % 2], s_oq[it % 2]
        @blk.sync
        def _(sync):
            if it >= 2:
                sync.wait_ge(tE.h, c2_marks[it - 2])
            rSel.sem(0).wait(sync)
            for g in range(4):
                rCo.sem(it).inc(sync.dma_start(
                    out=ot[:, g * 4:(g + 1) * 4, :],
                    in_=osel[g * 512:(g + 1) * 512,
                             it * 128:(it + 1) * 128].rearrange("(dc p) t -> p dc t", p=128)), 16)
            rCx.sem(it).inc(sync.dma_start(out=xc[:], in_=x_q[it * 128:(it + 1) * 128, :]), 16)
            rCf.sem(it).inc(sync.dma_start(out=fc[:], in_=ffn_scr[it * 128:(it + 1) * 128, :]), 16)
        for dp in range(4):
            @blk.tensor
            def _(tensor, dp=dp):
                rCo.sem(it).wait(tensor)
                if it == 0 and dp == 0:
                    tensor.wait_ge(vE.h, wos_mark)
                if it > 0 or dp > 1:
                    tensor.wait_ge(vE.h, vE.n)
                for dc in range(16):
                    mm = tensor.matmul(p_c2[dp % 2][:], ot[:, dc, :],
                                       s_wos[:, dc, dp * 512:(dp + 1) * 512],
                                       start=(dc == 0), stop=(dc == 15))
                tE.inc(mm)
            @blk.vector
            def _(vector, dp=dp):
                tE.wait(vector)
                rCx.sem(it).wait(vector)
                rCf.sem(it).wait(vector)
                if it >= 2 and dp == 0:
                    rOq.sem(it).wait(vector)
                dsl = slice(dp * 512, (dp + 1) * 512)
                vop(vector, vector.tensor_tensor, out=oq[:, dsl], in0=p_c2[dp % 2][:],
                    in1=fc[:, dsl], op=ALU.add)
                vop(vector, vector.tensor_tensor, out=oq[:, dsl], in0=oq[:, dsl],
                    in1=xc[:, dsl], op=ALU.add)
        c2_marks[it] = tE.n
        @blk.sync
        def _(sync):
            sync.wait_ge(vE.h, vE.n)
            rOq.sem(it).inc(sync.dma_start(out=out_q[it * 128:(it + 1) * 128, :], in_=oq[:]), 16)

    if taps:
        full_barrier()
        @blk.sync
        def _(sync):
            rOut.sem(0).inc(sync.dma_start(out=tp_o, in_=agin2), 16)
            rOut.sem(0).inc(sync.dma_start(out=tp_ffn, in_=ffn_scr), 16)

    full_barrier()
    @blk.sync
    def _(sync):
        for it in range(NT1):
            rOq.sem(it).wait(sync)
        rOut.sem(0).wait(sync)
    stC2.close()

    if scope_stack:
        nc.leave_named_scope(scope_stack[-1][0], scope_stack[-1][1], False)
        scope_stack.pop()
    ctx.close()
    return nc


def shard_inputs(inputs, T=16384):
    S = T // 4
    f32 = np.float32
    x = np.ascontiguousarray(np.asarray(inputs["L0_attn_hidden_states"], f32).reshape(2, -1, HID)[:, :T])
    NIB = INTER // 128
    cq, ck, cv = (np.asarray(inputs[f"L0_attn_c{n}"], f32) for n in "qkv")

    def rt16(w):
        C = w.shape[1]
        return np.ascontiguousarray(w.reshape(16, 128, C).transpose(1, 0, 2))

    NIBv = INTER // 128
    wgt_r = np.ascontiguousarray(np.asarray(inputs["L0_ffn_w_gate"])
                                 .reshape(16, 128, NIBv, 128).transpose(2, 1, 0, 3))
    wup_r = np.ascontiguousarray(np.asarray(inputs["L0_ffn_w_up"])
                                 .reshape(16, 128, NIBv, 128).transpose(2, 1, 0, 3))
    wdn_r = np.ascontiguousarray(np.asarray(inputs["L0_ffn_w_down"])
                                 .reshape(64, 128, 4, 512).transpose(2, 1, 0, 3))
    maps = []
    for c in range(8):
        b, hg = c // 4, c % 4
        sl = slice(hg * 512, (hg + 1) * 512)
        cwa = np.stack([cq[sl], ck[sl], cv[sl], cv[sl]])
        cwa = cwa.reshape(4, 4, 128, 4).transpose(2, 0, 1, 3)
        m = {
            "x_q": np.ascontiguousarray(x[b, hg * S:(hg + 1) * S]),
            "wq": rt16(np.asarray(inputs["L0_attn_Wq"], f32)[:, sl]),
            "wk": rt16(np.asarray(inputs["L0_attn_Wk"], f32)[:, sl]),
            "wv": rt16(np.asarray(inputs["L0_attn_Wv"], f32)[:, sl]),
            "ww": rt16(np.asarray(inputs["L0_attn_Ww"], f32)[:, sl]),
            "cw": np.ascontiguousarray(cwa, f32),
            "wf1": rt16(np.asarray(inputs["L0_attn_Wf1"], f32)),
            "wf2": np.ascontiguousarray(np.asarray(inputs["L0_attn_Wf2"], f32)[:, sl]),
            "wbm": rt16(np.asarray(inputs["L0_attn_Wb"], f32)[:, hg * 4:(hg + 1) * 4]),
            "wg1": rt16(np.asarray(inputs["L0_attn_Wg1"], f32)),
            "wg2": np.ascontiguousarray(np.asarray(inputs["L0_attn_Wg2"], f32)[:, sl]),
            "bg2": np.ascontiguousarray(np.asarray(inputs["L0_attn_bg2"], f32)[sl].reshape(4, 128).T),
            "normw": np.ascontiguousarray(np.asarray(inputs["L0_attn_norm_w"], f32).reshape(128, 1)),
            "wo": rt16(np.asarray(inputs["L0_attn_Wo"], f32)),
            "wgt": wgt_r, "wup": wup_r, "wdn": wdn_r,
            "sgt": np.ascontiguousarray(np.asarray(inputs["L0_ffn_s_gate"], f32).reshape(NIB, 128).T),
            "sup": np.ascontiguousarray(np.asarray(inputs["L0_ffn_s_up"], f32).reshape(NIB, 128).T),
            "sdn": np.ascontiguousarray(np.asarray(inputs["L0_ffn_s_down"], f32).reshape(1, HID)),
            "qoff": np.array([[hg]], np.uint32),
        }
        maps.append(m)
    return maps


def kernel(**inputs):
    T = 16384
    nc = build_program(T=T, taps=False)
    in_maps = shard_inputs(inputs, T)
    res = run_bass_kernel_spmd(nc, in_maps, core_ids=list(range(8)))
    S = T // 4
    out = np.zeros((2, T, HID), np.float32)
    for c in range(8):
        b, hg = c // 4, c % 4
        out[b, hg * S:(hg + 1) * S] = res.results[c]["out_q"]
    return out
